# version 01
# baseline (speedup 1.0000x reference)
import math
import numpy as np
import ml_dtypes
import concourse.bass as bass
import concourse.mybir as mybir
from concourse.bass import broadcast_tensor_aps
from concourse.bass_utils import run_bass_kernel_spmd

F32 = mybir.dt.float32
BF16 = mybir.dt.bfloat16
AF = mybir.ActivationFunctionType
ALU = mybir.AluOpType
BF = ml_dtypes.bfloat16

BATCH, SEQ, D = 4, 8192, 1024
DEBUG_L0A = False
DEBUG_L0F = False
NCORE = 8
TC = SEQ // 2


class Sched:

    ENGINES = ("sync", "vector", "scalar", "tensor", "gpsimd")

    def __init__(self, nc):
        self.nc = nc
        self.ops = {e: [] for e in self.ENGINES}
        self.counts = {e: 0 for e in self.ENGINES}
        self.sems = {e: nc.alloc_semaphore(f"sem_{e}") for e in self.ENGINES}

    def _sem(self, name):
        if name not in self.sems:
            self.sems[name] = self.nc.alloc_semaphore(f"sem_{name}")
            self.counts[name] = 0
        return self.sems[name]

    def emit(self, eng, fn, waits=(), inc=1, ch=None):
        ws = tuple(w for w in waits if w is not None)
        sem = ch if ch is not None else eng
        self._sem(sem)
        self.ops[eng].append((fn, ws, inc, sem))
        self.counts[sem] += inc
        return (sem, self.counts[sem])

    def dma(self, eng, out, in_, waits=(), ch=None, slow=False):
        assert ch is not None, "DMA needs a dedicated channel"
        return self.emit(eng, lambda e, o=out, i=in_, s=slow: e.dma_start(
            out=o, in_=i, allow_slow_non_contiguous=s), waits, inc=16, ch=ch)

    def finalize(self):
        nc = self.nc
        with nc.Block() as blk:
            for ename in self.ENGINES:
                if not self.ops[ename]:
                    continue
                def mk(ename):
                    def body(eng):
                        hiwater = {}
                        for fn, waits, inc, sem in self.ops[ename]:
                            for (we, wv) in waits:
                                if we == ename:
                                    continue
                                if hiwater.get(we, -1) >= wv:
                                    continue
                                eng.wait_ge(self.sems[we], wv)
                                hiwater[we] = wv
                            inst = fn(eng)
                            if inst is not None and inc:
                                inst.then_inc(self.sems[sem], inc)
                    return body
                getattr(blk, ename)(mk(ename))


def build_l0a_scan():
    nc = bass.Bass("TRN2", target_bir_lowering=False,
                   detect_race_conditions=False)
    HL = 8
    SC = 1024
    NSC = SEQ // SC
    TT = 512
    NTT = SC // TT

    xT = nc.dram_tensor("xT", [D, SEQ], F32, kind="ExternalInput").ap()
    wqkvT = nc.dram_tensor("wqkvT", [D, 1536], F32, kind="ExternalInput").ap()
    bqkv = nc.dram_tensor("bqkv", [128, 12], F32, kind="ExternalInput").ap()
    lam_t = nc.dram_tensor("lam_t", [128, 256], F32, kind="ExternalInput").ap()
    gam_t = nc.dram_tensor("gam_t", [128, 256], F32, kind="ExternalInput").ap()
    wsT = nc.dram_tensor("wsT", [64, 512], F32, kind="ExternalInput").ap()
    perm = nc.dram_tensor("perm", [128, 64 * 128], BF16, kind="ExternalInput").ap()
    e2 = nc.dram_tensor("e2", [128, 32 * 64], BF16, kind="ExternalInput").ap()
    ones1 = nc.dram_tensor("ones1", [1, 128], BF16, kind="ExternalInput").ap()
    onesc = nc.dram_tensor("onesc", [128, 1], BF16, kind="ExternalInput").ap()
    y16 = nc.dram_tensor("y16", [512, SEQ], BF16, kind="ExternalOutput").ap()
    if DEBUG_L0A:
        dbg_qd = nc.dram_tensor("dbg_qd", [128, 8 * 1024], BF16, kind="ExternalOutput").ap()
        dbg_kd = nc.dram_tensor("dbg_kd", [128, 8 * 1024], BF16, kind="ExternalOutput").ap()
        dbg_v = nc.dram_tensor("dbg_v", [128, 4 * 1024], BF16, kind="ExternalOutput").ap()
        dbg_carry = nc.dram_tensor("dbg_carry", [128, 256], F32, kind="ExternalOutput").ap()
        dbg_rr = nc.dram_tensor("dbg_rr", [128, 256], F32, kind="ExternalOutput").ap()
        dbg_g1 = nc.dram_tensor("dbg_g1", [128, 2 * 1024], BF16, kind="ExternalOutput").ap()
        dbg_g2 = nc.dram_tensor("dbg_g2", [128, 2 * 1024], F32, kind="ExternalOutput").ap()
        dbg_s = nc.dram_tensor("dbg_s", [128, 2 * 1024], F32, kind="ExternalOutput").ap()
        dbg_z = nc.dram_tensor("dbg_z", [128, 2 * 1024], BF16, kind="ExternalOutput").ap()

    w16 = nc.alloc_sbuf_tensor("w16", [128, 8 * 1536], BF16)
    bq_sb = nc.alloc_sbuf_tensor("bq_sb", [128, 12], F32)
    lam_sb = nc.alloc_sbuf_tensor("lam_sb", [128, 256], F32)
    rr_sb = nc.alloc_sbuf_tensor("rr_sb", [128, 256], F32)
    gam_sb = nc.alloc_sbuf_tensor("gam_sb", [128, 256], F32)
    ws16 = nc.alloc_sbuf_tensor("ws16", [64, 512], BF16)
    perm_sb = nc.alloc_sbuf_tensor("perm_sb", [128, 64 * 128], BF16)
    e2_sb = nc.alloc_sbuf_tensor("e2_sb", [128, 32 * 64], BF16)
    ones1_sb = nc.alloc_sbuf_tensor("ones1_sb", [1, 128], BF16)
    onesc_sb = nc.alloc_sbuf_tensor("onesc_sb", [128, 1], BF16)

    x16 = nc.alloc_sbuf_tensor("x16", [128, 8 * TT], BF16)
    xtmp = nc.alloc_sbuf_tensor("xtmp", [128, 8 * TT], BF16)
    strow = nc.alloc_sbuf_tensor("strow", [1, 5 * TT], F32)
    srow16 = nc.alloc_sbuf_tensor("srow16", [1, 2 * TT], BF16)

    qd = nc.alloc_sbuf_tensor("qd", [128, HL * SC], BF16)
    kd = nc.alloc_sbuf_tensor("kd", [128, HL * SC], BF16)
    v16 = nc.alloc_sbuf_tensor("v16", [128, 4 * SC], BF16)
    g1 = nc.alloc_sbuf_tensor("g1", [128, 2 * SC], BF16)
    g2 = nc.alloc_sbuf_tensor("g2", [128, 2 * SC], F32)
    s_t = nc.alloc_sbuf_tensor("s_t", [128, 2 * SC], F32)
    z_t = nc.alloc_sbuf_tensor("z_t", [128, 2 * SC], BF16)
    qdf = nc.alloc_sbuf_tensor("qdf", [128, 2 * SC], F32)
    carry = nc.alloc_sbuf_tensor("carry", [128, 256], F32)
    ystage = nc.alloc_sbuf_tensor("ystage", [64, 2 * SC], BF16)

    ps = nc.alloc_psum_tensor("ps", [128, 4096], F32)

    S = Sched(nc)

    for c in range(8):
        S.dma("gpsimd", w16[:, c * 1536:(c + 1) * 1536],
              wqkvT[c * 128:(c + 1) * 128, :], ch="ldg")
    S.dma("sync", bq_sb[:, :], bqkv, ch="lds")
    S.dma("sync", lam_sb[:, :], lam_t, ch="lds")
    S.dma("sync", gam_sb[:, :], gam_t, ch="lds")
    t_ldg = S.dma("gpsimd", ws16[:, :], wsT, ch="ldg")
    S.dma("sync", perm_sb[:, :], perm, ch="lds")
    S.dma("sync", e2_sb[:, :], e2, ch="lds")
    S.dma("sync", ones1_sb[:, :], ones1, ch="lds")
    t_lds = S.dma("sync", onesc_sb[:, :], onesc, ch="lds")
    t_rr = S.emit("scalar", lambda e: e.activation(rr_sb[:, :], lam_sb[:, :],
                                                   AF.Sigmoid, scale=-1.0),
                  waits=[t_lds])
    pre = [t_ldg, t_lds, t_rr]

    PS_QKV = ps[:, 1024:1536]
    PS_SUM = ps[0:1, 1536:2048]
    PS_SQ = ps[0:1, 3072:3584]
    PS_R = ps[:, 3584:4096]
    PS_M = ps[:, 1536:2048]
    PS_VD = ps[:, 0:SC]
    NSP = SC // 512
    PS_Y = [ps[0:64, 2048 + s * 512: 2048 + (s + 1) * 512] for s in range(NSP)]

    prev_gamma_last = None
    prev_tg2_last = None
    prev_tz_last = None
    prev_vd_last = None
    prev_sq_last = None
    prev_stats = None
    prev_xn = []
    prev_mm_last = None
    ye_prev = None
    prev_oc_evacs = []
    qdf_last_reader = [None, None]
    pending_tcar = None
    ucount = 0
    tg2 = tz = tsc = tg1 = tvd = None
    tg2_hist = {}
    tz_hist = {}
    tsc_hist = {}
    tg1_hist = {}
    emm_hist = {}
    tlast = None
    st_half = [None, None]
    st_tok = None

    for sc in range(NSC):
        sc_evacs = []
        for tt in range(NTT):
            t0 = sc * SC + tt * TT
            ld_waits = list(pre) if (sc == 0 and tt == 0) else \
                ([prev_sq_last, prev_stats, prev_mm_last] + prev_xn)
            t_x = None
            for c in range(8):
                t_x = S.dma("gpsimd", x16[:, c * TT:(c + 1) * TT],
                            xT[c * 128:(c + 1) * 128, t0:t0 + TT],
                            waits=ld_waits if c == 0 else [], ch="ldx")
            t_sq = []
            for c in range(8):
                t_sq.append(S.emit("scalar", lambda e, c=c: e.activation(
                    xtmp[:, c * TT:(c + 1) * TT], x16[:, c * TT:(c + 1) * TT],
                    AF.Square), waits=[t_x, prev_mm_last]))
            prev_sq_last = t_sq[-1]
            stat_waits = [t_x] + t_sq
            tm = None
            for c in range(8):
                tm = S.emit("tensor", lambda e, c=c: e.matmul(
                    PS_SUM, onesc_sb[:, :], x16[:, c * TT:(c + 1) * TT],
                    start=(c == 0), stop=(c == 7)),
                    waits=stat_waits if c == 0 else [])
            for c in range(8):
                tm = S.emit("tensor", lambda e, c=c: e.matmul(
                    PS_SQ, onesc_sb[:, :], xtmp[:, c * TT:(c + 1) * TT],
                    start=(c == 0), stop=(c == 7)))
            t_stats = tm
            prev_stats = t_stats
            RW = TT
            mu = strow[0:1, 0:RW]; ex2 = strow[0:1, RW:2 * RW]
            var = strow[0:1, 2 * RW:3 * RW]; rstd = strow[0:1, 3 * RW:4 * RW]
            mur = strow[0:1, 4 * RW:5 * RW]
            t1 = S.emit("vector", lambda e: e.tensor_scalar_mul(mu, PS_SUM, 1.0 / D),
                        waits=[t_stats])
            S.emit("vector", lambda e: e.tensor_scalar_mul(ex2, PS_SQ, 1.0 / D))
            S.emit("vector", lambda e: e.tensor_tensor(var, mu, mu, ALU.mult))
            S.emit("vector", lambda e: e.tensor_tensor(var, ex2, var, ALU.subtract))
            t4 = S.emit("vector", lambda e: e.tensor_scalar_add(var, var, 1e-5))
            t5 = S.emit("scalar", lambda e: e.activation(rstd, var, AF.Sqrt),
                        waits=[t4])
            t6 = S.emit("vector", lambda e: e.reciprocal(rstd, rstd), waits=[t5])
            t7 = S.emit("vector", lambda e: e.tensor_tensor(mur, mu, rstd, ALU.mult))
            t8 = S.emit("scalar", lambda e: e.copy(srow16[0:1, 0:RW], rstd), waits=[t6])
            t9 = S.emit("scalar", lambda e: e.copy(srow16[0:1, RW:2 * RW], mur),
                        waits=[t7])
            tR = S.emit("tensor", lambda e: e.matmul(PS_R, ones1_sb[:, :],
                                                     srow16[0:1, 0:RW],
                                                     start=True, stop=True),
                        waits=[t8])
            tM = S.emit("tensor", lambda e: e.matmul(PS_M, ones1_sb[:, :],
                                                     srow16[0:1, RW:2 * RW],
                                                     start=True, stop=True),
                        waits=[t9, t1])
            t_xn = []
            for c in range(8):
                S.emit("vector", lambda e, c=c: e.tensor_tensor(
                    xtmp[:, c * TT:(c + 1) * TT], x16[:, c * TT:(c + 1) * TT],
                    PS_R, ALU.mult), waits=[tR])
                tb = S.emit("vector", lambda e, c=c: e.tensor_tensor(
                    xtmp[:, c * TT:(c + 1) * TT], xtmp[:, c * TT:(c + 1) * TT],
                    PS_M, ALU.subtract), waits=[tM])
                t_xn.append(tb)
            prev_xn = t_xn
            col = tt * TT
            mm_last = None
            prev_evac = list(prev_oc_evacs)
            for oc in range(12):
                for kc in range(8):
                    w = (t_xn + prev_evac) if kc == 0 else []
                    mm_last = S.emit("tensor", lambda e, oc=oc, kc=kc: e.matmul(
                        PS_QKV,
                        w16[:, kc * 1536 + oc * 128: kc * 1536 + (oc + 1) * 128],
                        xtmp[:, kc * TT:(kc + 1) * TT],
                        start=(kc == 0), stop=(kc == 7)), waits=w)
                if oc < 4:
                    hA, hB = 2 * oc, 2 * oc + 1
                    for (h, src0) in ((hA, 0), (hB, 64)):
                        for dst0 in (0, 64):
                            tq = S.emit("vector", lambda e, h=h, src0=src0, dst0=dst0,
                                        oc=oc, col=col: e.tensor_scalar_add(
                                qd[dst0:dst0 + 64, h * SC + col: h * SC + col + TT],
                                PS_QKV[src0:src0 + 64, :],
                                bq_sb[src0:src0 + 64, oc:oc + 1]),
                                waits=[mm_last, prev_tz_last])
                            sc_evacs.append(tq)
                elif oc < 8:
                    hA, hB = 2 * (oc - 4), 2 * (oc - 4) + 1
                    for (h, src0) in ((hA, 0), (hB, 64)):
                        for dst0 in (0, 64):
                            tk = S.emit("vector", lambda e, h=h, src0=src0, dst0=dst0,
                                        oc=oc, col=col: e.tensor_scalar(
                                kd[dst0:dst0 + 64, h * SC + col: h * SC + col + TT],
                                PS_QKV[src0:src0 + 64, :],
                                bq_sb[src0:src0 + 64, oc:oc + 1], 0.125,
                                ALU.add, ALU.mult),
                                waits=[mm_last, prev_tg2_last])
                            sc_evacs.append(tk)
                else:
                    vc = oc - 8
                    tv = S.emit("scalar", lambda e, vc=vc, oc=oc, col=col: e.activation(
                        v16[:, vc * SC + col: vc * SC + col + TT], PS_QKV,
                        AF.Identity, bias=bq_sb[:, oc:oc + 1]),
                        waits=[mm_last, prev_vd_last])
                    sc_evacs.append(tv)
                prev_evac = sc_evacs[-1:] if oc >= 8 else sc_evacs[-1:]
            prev_oc_evacs = prev_evac
            prev_mm_last = mm_last

        scan_waits = list(sc_evacs)
        for hl in range(HL):
            p2 = hl % 2
            qoc = hl // 2
            hslot = hl * 32
            qp = hl % 2
            tqdf = S.emit("scalar", lambda e, hl=hl, qp=qp: e.activation(
                qdf[:, qp * SC:(qp + 1) * SC], qd[:, hl * SC:(hl + 1) * SC],
                AF.Copy), waits=[qdf_last_reader[qp]] +
                (list(scan_waits) if hl == 0 else []))
            for s in range(NSP):
                w = list(scan_waits) if (hl == 0 and s == 0) else []
                if s == 0 and ye_prev is not None:
                    w.append(ye_prev)
                S.emit("tensor", lambda e, s=s, hl=hl: e.matmul(
                    PS_Y[s], ws16[:, hl * 64:(hl + 1) * 64],
                    qd[0:64, hl * SC + s * TT: hl * SC + (s + 1) * TT],
                    start=True, stop=False), waits=w)
            for u in range(32):
                pslot = (p2 * 32 + u) * 128
                half = ucount % 2
                wv = [tg1]
                tvd = None
                for s4 in range(NSP):
                    tvd = S.emit("tensor", lambda e, s4=s4, pslot=pslot, qoc=qoc:
                                 e.matmul(
                        PS_VD[:, s4 * 512:(s4 + 1) * 512],
                        perm_sb[:, pslot:pslot + 128],
                        v16[:, qoc * SC + s4 * 512: qoc * SC + (s4 + 1) * 512],
                        start=True, stop=True), waits=wv if s4 == 0 else [])
                tg1 = S.emit("scalar", lambda e, hslot=hslot, u=u, half=half:
                             e.activation(
                    g1[:, half * SC:(half + 1) * SC], PS_VD, AF.Identity,
                    scale=gam_sb[:, hslot + u: hslot + u + 1]),
                    waits=[tvd, tg2_hist.get(ucount - 2)])
                tg1_hist[ucount] = tg1
                tg2 = S.emit("gpsimd", lambda e, hl=hl, half=half: e.tensor_tensor(
                    g2[:, half * SC:(half + 1) * SC], g1[:, half * SC:(half + 1) * SC],
                    kd[:, hl * SC:(hl + 1) * SC], ALU.mult),
                    waits=[tg1, tsc_hist.get(ucount - 2)])
                tg2_hist[ucount] = tg2
                rcol = rr_sb[:, hslot + u: hslot + u + 1]
                init = 0.0 if sc == 0 else carry[:, hslot + u: hslot + u + 1]
                def mk_scan(rcol=rcol, init=init, half=half):
                    def f(e):
                        g2h = g2[:, half * SC:(half + 1) * SC]
                        _, rrep = broadcast_tensor_aps(g2h, rcol)
                        return e.tensor_tensor_scan(
                            s_t[:, half * SC:(half + 1) * SC], rrep, g2h, init,
                            ALU.mult, ALU.add)
                    return f
                tsc = S.emit("vector", mk_scan(),
                             waits=[tg2, tz_hist.get(ucount - 2)])
                tsc_hist[ucount] = tsc
                tcar = None
                if pending_tcar is not None:
                    slot_p, half_p = pending_tcar
                    tcar = S.emit("vector", lambda e, slot_p=slot_p, half_p=half_p:
                                  e.tensor_copy(
                        carry[:, slot_p: slot_p + 1],
                        s_t[:, half_p * SC + SC - 1: half_p * SC + SC]))
                pending_tcar = (hslot + u, half)
                tz = S.emit("gpsimd", lambda e, qp=qp, half=half: e.tensor_tensor(
                    z_t[:, half * SC:(half + 1) * SC],
                    s_t[:, half * SC:(half + 1) * SC],
                    qdf[:, qp * SC:(qp + 1) * SC], ALU.mult),
                    waits=[tsc, tcar, tqdf, emm_hist.get(ucount - 2)])
                tz_hist[ucount] = tz
                for s4 in range(NSP):
                    tlast = S.emit("tensor", lambda e, s4=s4, u=u, half=half: e.matmul(
                        PS_Y[s4], e2_sb[:, u * 64:(u + 1) * 64],
                        z_t[:, half * SC + s4 * 512: half * SC + (s4 + 1) * 512],
                        start=False, stop=(u == 31)), waits=[tz] if s4 == 0 else [])
                emm_hist[ucount] = tlast
                ucount += 1
            qdf_last_reader[qp] = tz
            ybuf = (sc * HL + hl) % 2
            ye = []
            for s4 in range(NSP):
                ye.append(S.emit("scalar", lambda e, s4=s4, ybuf=ybuf: e.activation(
                    ystage[:, ybuf * SC + s4 * 512: ybuf * SC + (s4 + 1) * 512],
                    PS_Y[s4], AF.Copy), waits=[tlast, st_half[ybuf]]))
            ye_prev = ye[-1]
            row0 = hl * 64
            st_tok = S.dma("sync", y16[row0:row0 + 64, sc * SC:(sc + 1) * SC],
                           ystage[:, ybuf * SC:(ybuf + 1) * SC], waits=ye,
                           ch=f"sty{ybuf}")
            st_half[ybuf] = st_tok
        prev_gamma_last = tg1
        prev_tg2_last = tg2
        prev_tz_last = tz
        prev_vd_last = tvd

    if pending_tcar is not None:
        slot_p, half_p = pending_tcar
        S.emit("vector", lambda e: e.tensor_copy(qdf[:, 0:64], s_t[:, 0:64]))
        tcar_f = S.emit("vector", lambda e, slot_p=slot_p, half_p=half_p:
                        e.tensor_copy(carry[:, slot_p: slot_p + 1],
                                      s_t[:, half_p * SC + SC - 1: half_p * SC + SC]))
    fin = [st_half[0], st_half[1]]
    if DEBUG_L0A:
        S.dma("sync", dbg_qd, qd[:, :], waits=[tz, st_tok], ch="dbg")
        S.dma("sync", dbg_kd, kd[:, :], waits=[], ch="dbg")
        S.dma("sync", dbg_v, v16[:, :], waits=[], ch="dbg")
        S.dma("sync", dbg_carry, carry[:, :], waits=[tcar_f], ch="dbg")
        S.dma("sync", dbg_rr, rr_sb[:, :], waits=[], ch="dbg")
        S.dma("sync", dbg_g1, g1[:, :], waits=[], ch="dbg")
        S.dma("sync", dbg_g2, g2[:, :], waits=[], ch="dbg")
        S.dma("sync", dbg_s, s_t[:, :], waits=[], ch="dbg")
        fin.append(S.dma("sync", dbg_z, z_t[:, :], waits=[], ch="dbg"))
    S.emit("gpsimd", lambda e: e.engine_nop(), waits=fin, inc=0)
    S.finalize()
    return nc


def _l0a_scan_host(inputs):
    x = inputs["L0_attn_x"]
    Wq = inputs["L0_attn_Wqkv_w"]
    bq = inputs["L0_attn_Wqkv_b"]
    lam = inputs["L0_attn_Lambda_raw"]
    gam = inputs["L0_attn_Gamma"]
    ws = inputs["L0_attn_W_static"]

    WqT = np.ascontiguousarray(Wq.T)
    perm = np.zeros((128, 64 * 128), dtype=BF)
    for p2 in range(2):
        for u in range(32):
            m = np.zeros((128, 128), dtype=np.float32)
            for p in range(128):
                r = p2 * 64 + 2 * u + p // 64
                m[r, p] = 1.0
            perm[:, (p2 * 32 + u) * 128:(p2 * 32 + u + 1) * 128] = m.astype(BF)
    e2m = np.zeros((128, 32 * 64), dtype=BF)
    for u in range(32):
        m = np.zeros((128, 64), dtype=np.float32)
        for p in range(128):
            m[p, 2 * u + p // 64] = 1.0
        e2m[:, u * 64:(u + 1) * 64] = m.astype(BF)
    ones1 = np.ones((1, 128), dtype=BF)
    onesc = np.ones((128, 1), dtype=BF)

    in_maps = []
    for cid in range(NCORE):
        b = cid // 2
        g = cid % 2
        H0 = 8 * g
        cols = np.concatenate([np.arange(H0 * 64, (H0 + 8) * 64),
                               1024 + np.arange(H0 * 64, (H0 + 8) * 64),
                               2048 + np.arange(H0 * 64, (H0 + 8) * 64)])
        wq_slice = np.ascontiguousarray(WqT[:, cols])
        bq_slice = np.ascontiguousarray(bq[cols].reshape(12, 128).T)
        lam_s = lam[H0:H0 + 8].reshape(8, 32, 128)
        lam_tiled = np.ascontiguousarray(lam_s.transpose(2, 0, 1).reshape(128, 256))
        gam_s = gam[H0:H0 + 8].reshape(8, 32, 128)
        gam_tiled = np.ascontiguousarray(gam_s.transpose(2, 0, 1).reshape(128, 256))
        wsT_s = np.ascontiguousarray(
            ws[H0:H0 + 8].transpose(0, 2, 1).transpose(1, 0, 2).reshape(64, 512))
        in_maps.append({
            "xT": np.ascontiguousarray(x[b].T),
            "wqkvT": wq_slice,
            "bqkv": bq_slice,
            "lam_t": lam_tiled,
            "gam_t": gam_tiled,
            "wsT": wsT_s,
            "perm": perm, "e2": e2m, "ones1": ones1, "onesc": onesc,
        })
    return in_maps


def run_l0a_scan(inputs):
    nc = build_l0a_scan()
    in_maps = _l0a_scan_host(inputs)
    res = run_bass_kernel_spmd(nc, in_maps, core_ids=list(range(NCORE)))
    y = np.zeros((BATCH, SEQ, D), dtype=np.float32)
    for cid in range(NCORE):
        b = cid // 2
        g = cid % 2
        yc = np.asarray(res.results[cid]["y16"], dtype=np.float32)
        y[b, :, g * 512:(g + 1) * 512] = yc.T
    return y




def build_l0a_out():
    nc = bass.Bass("TRN2", target_bir_lowering=False,
                   detect_race_conditions=False)
    NT = TC // 512

    ytop = nc.dram_tensor("ytop", [512, TC], BF16, kind="ExternalInput").ap()
    ybot = nc.dram_tensor("ybot", [512, TC], BF16, kind="ExternalInput").ap()
    xT2 = nc.dram_tensor("xT2", [D, TC], F32, kind="ExternalInput").ap()
    owT = nc.dram_tensor("owT", [D, D], F32, kind="ExternalInput").ap()
    obc = nc.dram_tensor("obc", [128, 8], F32, kind="ExternalInput").ap()
    ones1 = nc.dram_tensor("ones1", [1, 128], BF16, kind="ExternalInput").ap()
    onesc = nc.dram_tensor("onesc", [128, 1], BF16, kind="ExternalInput").ap()
    x1o = nc.dram_tensor("x1o", [D, TC], F32, kind="ExternalOutput").ap()
    z0o = nc.dram_tensor("z0o", [D, TC], F32, kind="ExternalOutput").ap()

    w16 = nc.alloc_sbuf_tensor("w16", [128, 8 * 1024], BF16)
    ob_sb = nc.alloc_sbuf_tensor("ob_sb", [128, 8], F32)
    ones1_sb = nc.alloc_sbuf_tensor("ones1_sb", [1, 128], BF16)
    onesc_sb = nc.alloc_sbuf_tensor("onesc_sb", [128, 1], BF16)
    ysb = nc.alloc_sbuf_tensor("ysb", [128, 2 * 8 * 512], BF16)
    xsb = nc.alloc_sbuf_tensor("xsb", [128, 2 * 8 * 512], F32)
    u16 = nc.alloc_sbuf_tensor("u16", [128, 8 * 512], BF16)
    usq = nc.alloc_sbuf_tensor("usq", [128, 8 * 512], BF16)
    x1b = nc.alloc_sbuf_tensor("x1b", [128, 8 * 512], F32)
    x1c = nc.alloc_sbuf_tensor("x1c", [128, 8 * 512], BF16)
    z0b = nc.alloc_sbuf_tensor("z0b", [128, 8 * 512], F32)
    strow = nc.alloc_sbuf_tensor("strow", [1, 5 * 512], F32)
    srow16 = nc.alloc_sbuf_tensor("srow16", [1, 2 * 512], BF16)

    ps = nc.alloc_psum_tensor("ps", [128, 4096], F32)
    PS_MM = ps[:, 0:512]
    PS_SUM = ps[0:1, 512:1024]
    PS_SQ = ps[0:1, 1024:1536]
    PS_R = ps[:, 1536:2048]
    PS_M = ps[:, 2048:2560]

    S = Sched(nc)
    for c in range(8):
        S.dma("gpsimd", w16[:, c * 1024:(c + 1) * 1024],
              owT[c * 128:(c + 1) * 128, :], ch="ldg")
    t_ldg = S.dma("gpsimd", ob_sb[:, :], obc, ch="ldg")
    S.dma("sync", ones1_sb[:, :], ones1, ch="lds")
    t_lds = S.dma("sync", onesc_sb[:, :], onesc, ch="lds")
    pre = [t_ldg, t_lds]

    def ln_rows(S, t_stats, eps=1e-5):
        RW = 512
        mu = strow[0:1, 0:RW]; ex2 = strow[0:1, RW:2 * RW]
        var = strow[0:1, 2 * RW:3 * RW]; rstd = strow[0:1, 3 * RW:4 * RW]
        mur = strow[0:1, 4 * RW:5 * RW]
        t1 = S.emit("vector", lambda e: e.tensor_scalar_mul(mu, PS_SUM, 1.0 / D),
                    waits=[t_stats])
        S.emit("vector", lambda e: e.tensor_scalar_mul(ex2, PS_SQ, 1.0 / D))
        S.emit("vector", lambda e: e.tensor_tensor(var, mu, mu, ALU.mult))
        S.emit("vector", lambda e: e.tensor_tensor(var, ex2, var, ALU.subtract))
        t4 = S.emit("vector", lambda e: e.tensor_scalar_add(var, var, eps))
        t5 = S.emit("scalar", lambda e: e.activation(rstd, var, AF.Sqrt), waits=[t4])
        t6 = S.emit("vector", lambda e: e.reciprocal(rstd, rstd), waits=[t5])
        t7 = S.emit("vector", lambda e: e.tensor_tensor(mur, mu, rstd, ALU.mult))
        t8 = S.emit("scalar", lambda e: e.copy(srow16[0:1, 0:RW], rstd), waits=[t6])
        t9 = S.emit("scalar", lambda e: e.copy(srow16[0:1, RW:2 * RW], mur),
                    waits=[t7])
        return t1, t8, t9

    prev_cons = []
    prev_R_readers = None
    prev_mm_evac = None
    st_toks = []
    for tt in range(NT):
        t0 = tt * 512
        db = tt % 2
        yb = ysb[:, db * 4096:(db + 1) * 4096]
        xb = xsb[:, db * 4096:(db + 1) * 4096]
        ldw = (pre + prev_cons) if tt < 2 else prev_cons
        t_y = None
        for c in range(4):
            t_y = S.dma("sync", yb[:, c * 512:(c + 1) * 512],
                        ytop[c * 128:(c + 1) * 128, t0:t0 + 512],
                        waits=ldw if c == 0 else [], ch=f"ldy{db}")
        for c in range(4):
            t_y = S.dma("sync", yb[:, (4 + c) * 512:(5 + c) * 512],
                        ybot[c * 128:(c + 1) * 128, t0:t0 + 512], ch=f"ldy{db}")
        t_x = None
        for c in range(8):
            t_x = S.dma("sync", xb[:, c * 512:(c + 1) * 512],
                        xT2[c * 128:(c + 1) * 128, t0:t0 + 512], ch=f"ldx{db}")
        cons = []
        ev_prev = prev_mm_evac
        for fc in range(8):
            mm = None
            for kc in range(8):
                w = ([t_y, ev_prev] + ([prev_R_readers] if fc == 0 else []))
                mm = S.emit("tensor", lambda e, fc=fc, kc=kc, yb=yb: e.matmul(
                    PS_MM, w16[:, kc * 1024 + fc * 128: kc * 1024 + (fc + 1) * 128],
                    yb[:, kc * 512:(kc + 1) * 512],
                    start=(kc == 0), stop=(kc == 7)), waits=w if kc == 0 else [])
            tu = S.emit("scalar", lambda e, fc=fc: e.activation(
                u16[:, fc * 512:(fc + 1) * 512], PS_MM, AF.Identity,
                bias=ob_sb[:, fc:fc + 1]), waits=[mm])
            ts_ = S.emit("scalar", lambda e, fc=fc: e.activation(
                usq[:, fc * 512:(fc + 1) * 512], u16[:, fc * 512:(fc + 1) * 512],
                AF.Square))
            ev_prev = tu
            cons += [tu, ts_]
        prev_mm_evac = ev_prev
        tm = None
        for c in range(8):
            tm = S.emit("tensor", lambda e, c=c: e.matmul(
                PS_SUM, onesc_sb[:, :], u16[:, c * 512:(c + 1) * 512],
                start=(c == 0), stop=(c == 7)), waits=cons if c == 0 else [])
        for c in range(8):
            tm = S.emit("tensor", lambda e, c=c: e.matmul(
                PS_SQ, onesc_sb[:, :], usq[:, c * 512:(c + 1) * 512],
                start=(c == 0), stop=(c == 7)))
        t1, t8, t9 = ln_rows(S, tm)
        tR = S.emit("tensor", lambda e: e.matmul(PS_R, ones1_sb[:, :],
                                                 srow16[0:1, 0:512],
                                                 start=True, stop=True), waits=[t8])
        tM = S.emit("tensor", lambda e: e.matmul(PS_M, ones1_sb[:, :],
                                                 srow16[0:1, 512:1024],
                                                 start=True, stop=True), waits=[t9])
        x1toks = []
        for c in range(8):
            S.emit("vector", lambda e, c=c: e.tensor_tensor(
                x1b[:, c * 512:(c + 1) * 512], u16[:, c * 512:(c + 1) * 512],
                PS_R, ALU.mult), waits=[tR, t_x])
            S.emit("vector", lambda e, c=c: e.tensor_tensor(
                x1b[:, c * 512:(c + 1) * 512], x1b[:, c * 512:(c + 1) * 512],
                PS_M, ALU.subtract), waits=[tM])
            ta = S.emit("vector", lambda e, c=c, db=db: e.tensor_tensor(
                x1b[:, c * 512:(c + 1) * 512], x1b[:, c * 512:(c + 1) * 512],
                xsb[:, db * 4096 + c * 512: db * 4096 + (c + 1) * 512], ALU.add))
            tb = S.emit("scalar", lambda e, c=c: e.activation(
                x1c[:, c * 512:(c + 1) * 512], x1b[:, c * 512:(c + 1) * 512],
                AF.Copy), waits=[ta])
            tc_ = S.emit("scalar", lambda e, c=c: e.activation(
                usq[:, c * 512:(c + 1) * 512], x1c[:, c * 512:(c + 1) * 512],
                AF.Square))
            x1toks += [ta, tb, tc_]
            cons += [ta, tb, tc_]
        tsx = None
        for c in range(8):
            tsx = S.dma("sync", x1o[c * 128:(c + 1) * 128, t0:t0 + 512],
                        x1b[:, c * 512:(c + 1) * 512],
                        waits=x1toks if c == 0 else [], ch=f"stx{db}")
        tm2 = None
        for c in range(8):
            tm2 = S.emit("tensor", lambda e, c=c: e.matmul(
                PS_SUM, onesc_sb[:, :], x1c[:, c * 512:(c + 1) * 512],
                start=(c == 0), stop=(c == 7)), waits=x1toks + [t1] if c == 0 else [])
        for c in range(8):
            tm2 = S.emit("tensor", lambda e, c=c: e.matmul(
                PS_SQ, onesc_sb[:, :], usq[:, c * 512:(c + 1) * 512],
                start=(c == 0), stop=(c == 7)))
        t1b, t8b, t9b = ln_rows(S, tm2)
        tR2 = S.emit("tensor", lambda e: e.matmul(PS_R, ones1_sb[:, :],
                                                  srow16[0:1, 0:512],
                                                  start=True, stop=True), waits=[t8b])
        tM2 = S.emit("tensor", lambda e: e.matmul(PS_M, ones1_sb[:, :],
                                                  srow16[0:1, 512:1024],
                                                  start=True, stop=True), waits=[t9b])
        z0toks = []
        for c in range(8):
            S.emit("vector", lambda e, c=c: e.tensor_tensor(
                z0b[:, c * 512:(c + 1) * 512], x1c[:, c * 512:(c + 1) * 512],
                PS_R, ALU.mult), waits=[tR2])
            tzb = S.emit("vector", lambda e, c=c: e.tensor_tensor(
                z0b[:, c * 512:(c + 1) * 512], z0b[:, c * 512:(c + 1) * 512],
                PS_M, ALU.subtract), waits=[tM2])
            z0toks.append(tzb)
            cons.append(tzb)
        tsz = None
        for c in range(8):
            tsz = S.dma("sync", z0o[c * 128:(c + 1) * 128, t0:t0 + 512],
                        z0b[:, c * 512:(c + 1) * 512],
                        waits=z0toks if c == 0 else [], ch=f"stz{db}")
        st_toks = [tsx, tsz]
        prev_R_readers = z0toks[-1]
        prev_cons = cons + z0toks + [tsx, tsz]
    S.emit("gpsimd", lambda e: e.engine_nop(), waits=st_toks, inc=0)
    S.finalize()
    return nc


def _l0a_out_host(inputs, y_even, y_odd):
    x = inputs["L0_attn_x"]
    ow = inputs["L0_attn_out_w"]
    ob = inputs["L0_attn_out_b"]
    owT = np.ascontiguousarray(ow.T)
    obc = np.ascontiguousarray(ob.reshape(8, 128).T)
    ones1 = np.ones((1, 128), dtype=BF)
    onesc = np.ones((128, 1), dtype=BF)
    in_maps = []
    for cid in range(NCORE):
        b = cid // 2
        h = cid % 2
        sl = slice(h * TC, (h + 1) * TC)
        in_maps.append({
            "ytop": np.ascontiguousarray(y_even[b][:, sl]),
            "ybot": np.ascontiguousarray(y_odd[b][:, sl]),
            "xT2": np.ascontiguousarray(x[b].T[:, sl]),
            "owT": owT, "obc": obc, "ones1": ones1, "onesc": onesc,
        })
    return in_maps




class _Arena:
    def __init__(self, nc, base, limit):
        self.nc, self.cur, self.limit = nc, base, limit

    def take(self, name, shape, dtype):
        t = self.nc.alloc_sbuf_tensor_at(name, shape, dtype, offset=self.cur)
        nbytes = shape[-1] * (4 if dtype == F32 else 2)
        self.cur += (nbytes + 31) // 32 * 32
        assert self.cur <= self.limit, f"SBUF arena overflow at {name}: {self.cur}"
        return t


def build_l0f_layer():
    nc = bass.Bass("TRN2", target_bir_lowering=False,
                   detect_race_conditions=False)
    NTA = 9
    NTB = 16
    KC2 = 22

    zin = nc.dram_tensor("zin", [D, TC], F32, kind="ExternalInput").ap()
    zhalo = nc.dram_tensor("zhalo", [D, 640], F32, kind="ExternalInput").ap()
    wqkv = nc.dram_tensor("wqkv", [D, 3072], F32, kind="ExternalInput").ap()
    wo = nc.dram_tensor("wo", [D, D], F32, kind="ExternalInput").ap()
    w1a = nc.dram_tensor("w1a", [D, 2730], F32, kind="ExternalInput").ap()
    w1g = nc.dram_tensor("w1g", [D, 2730], F32, kind="ExternalInput").ap()
    w2 = nc.dram_tensor("w2", [2730, D], F32, kind="ExternalInput").ap()
    qsc = nc.dram_tensor("qsc", [128, 1], F32, kind="ExternalInput").ap()
    ksc = nc.dram_tensor("ksc", [128, 1], F32, kind="ExternalInput").ap()
    b1ac = nc.dram_tensor("b1ac", [128, KC2], F32, kind="ExternalInput").ap()
    b1gc = nc.dram_tensor("b1gc", [128, KC2], F32, kind="ExternalInput").ap()
    lgc = nc.dram_tensor("lgc", [128, 8], F32, kind="ExternalInput").ap()
    lbc = nc.dram_tensor("lbc", [128, 8], F32, kind="ExternalInput").ap()
    b2c = nc.dram_tensor("b2c", [128, 8], F32, kind="ExternalInput").ap()
    cstab = nc.dram_tensor("cstab", [128, 4 * 4608], F32, kind="ExternalInput").ap()
    maskt = nc.dram_tensor("maskt", [128, 4 * 512], F32, kind="ExternalInput").ap()
    ones1 = nc.dram_tensor("ones1", [1, 128], BF16, kind="ExternalInput").ap()
    onesc = nc.dram_tensor("onesc", [128, 1], BF16, kind="ExternalInput").ap()
    e2t = nc.dram_tensor("e2t", [2, 128], BF16, kind="ExternalInput").ap()
    e2c = nc.dram_tensor("e2c", [128, 2], BF16, kind="ExternalInput").ap()
    ident = nc.dram_tensor("ident", [128, 128], BF16, kind="ExternalInput").ap()
    zout = nc.dram_tensor("zout", [D, TC], F32, kind="ExternalOutput").ap()
    x1s = nc.dram_tensor("x1s", [D, 4352], F32,
                         kind="ExternalOutput" if DEBUG_L0F else "Internal").ap()

    base = (nc.sbuf_base + 31) // 32 * 32
    LIM = base + 204 * 1024
    sh = _Arena(nc, base, LIM)
    mask_sb = sh.take("mask_sb", [128, 4 * 512], F32)
    bcols = sh.take("bcols", [128, 2 * KC2 + 26], F32)
    strow = sh.take("strow", [1, 5 * 256], F32)
    srow16 = sh.take("srow16", [1, 2 * 256], BF16)
    ones1_sb = sh.take("ones1_sb", [1, 128], BF16)
    onesc_sb = sh.take("onesc_sb", [128, 1], BF16)
    e2t_sb = sh.take("e2t_sb", [2, 128], BF16)
    e2c_sb = sh.take("e2c_sb", [128, 2], BF16)
    id_sb = sh.take("id_sb", [128, 128], BF16)
    nrow = sh.take("nrow", [2, 512], F32)
    nrow16 = sh.take("nrow16", [2, 512], BF16)
    mx = sh.take("mx", [128, 4], F32)
    rs = sh.take("rs", [128, 4], F32)
    shared_end = sh.cur

    pa = _Arena(nc, shared_end, LIM)
    wq16 = pa.take("wq16", [128, 8 * 3072], BF16)
    wo16 = pa.take("wo16", [128, 8 * 1024], BF16)
    cs16 = pa.take("cs16", [128, 4 * 512], F32)
    zf = pa.take("zf", [128, 8 * 513], F32)
    zs16 = pa.take("zs16", [128, 8 * 512], BF16)
    qsq = pa.take("qsq", [128, 512], BF16)
    q1t = pa.take("q1t", [128, 512], F32)
    qq = pa.take("qq", [128, 512], F32)
    qsw = pa.take("qsw", [128, 512], F32)
    qra = pa.take("qra", [128, 512], F32)
    kring = pa.take("kring", [128, 8 * 768], BF16)
    vring = pa.take("vring", [128, 8 * 768], BF16)
    qwin = pa.take("qwin", [128, 8 * 512], BF16)
    vt16 = pa.take("vt16", [128, 256], BF16)
    msc = pa.take("msc", [128, 512], F32)
    p16 = pa.take("p16", [128, 512], BF16)
    pt16 = pa.take("pt16", [128, 512], BF16)
    atm = pa.take("atm", [128, 64], BF16)
    afm = pa.take("afm", [128, 8 * 512], BF16)
    x1st = pa.take("x1st", [128, 8 * 512], F32)

    pb = _Arena(nc, shared_end, LIM)
    w1a16 = pb.take("w1a16", [128, 8 * 2730], BF16)
    w1g16 = pb.take("w1g16", [128, 8 * 2730], BF16)
    w216 = pb.take("w216", [128, KC2 * 1024], BF16)
    x1f = pb.take("x1f", [128, 8 * 257], F32)
    xv16 = pb.take("xv16", [128, 8 * 256], BF16)
    xsq16 = pb.take("xsq16", [128, 8 * 256], BF16)
    xn16 = pb.take("xn16", [128, 8 * 256], BF16)
    gg16 = pb.take("gg16", [128, KC2 * 256], BF16)
    ff16 = pb.take("ff16", [128, KC2 * 256], BF16)
    zst = pb.take("zst", [128, 8 * 256], F32)

    ps = nc.alloc_psum_tensor("ps", [128, 3584], F32)
    ps_t = nc.alloc_psum_tensor("ps_t", [128, 1024], BF16)
    PS_ACC = ps[:, 0:512]
    PS_NS = ps[0:2, 512:1024]
    PS_RB = ps[:, 1024:1536]
    PS_SC = [ps[:, 1536:2048], ps[:, 2048:2560]]
    PS_AV = [ps[0:128, 2560:2624], ps[0:128, 3072:3136]]
    PS_B_G = ps[:, 512:1024]
    PS_B_A = ps[:, 1024:1536]
    PS_B_ST = ps[0:1, 1536:2048]
    PS_B_R = ps[:, 2048:2560]
    PS_B_M = ps[:, 2560:3072]
    PT_P = ps_t[:, 0:512]
    PT_V = ps_t[:, 512:768]
    PT_O = ps_t[0:64, 768:896]

    S = Sched(nc)
    for c in range(8):
        S.dma("gpsimd", wq16[:, c * 3072:(c + 1) * 3072],
              wqkv[c * 128:(c + 1) * 128, :], ch="ldg")
    for c in range(8):
        S.dma("gpsimd", wo16[:, c * 1024:(c + 1) * 1024],
              wo[c * 128:(c + 1) * 128, :], ch="ldg")
    t_ldg = S.dma("gpsimd", nrow16[0:2, 0:128], e2t, ch="dummy_unused") \
        if False else None
    S.dma("sync", mask_sb[:, :], maskt, ch="lds")
    S.dma("sync", bcols[:, 2 * KC2:2 * KC2 + 1], qsc, ch="lds")
    S.dma("sync", bcols[:, 2 * KC2 + 1:2 * KC2 + 2], ksc, ch="lds")
    S.dma("sync", bcols[:, 0:KC2], b1ac, ch="lds")
    S.dma("sync", bcols[:, KC2:2 * KC2], b1gc, ch="lds")
    S.dma("sync", bcols[:, 2 * KC2 + 2:2 * KC2 + 10], lgc, ch="lds")
    S.dma("sync", bcols[:, 2 * KC2 + 10:2 * KC2 + 18], lbc, ch="lds")
    S.dma("sync", bcols[:, 2 * KC2 + 18:2 * KC2 + 26], b2c, ch="lds")
    S.dma("sync", ones1_sb[:, :], ones1, ch="lds")
    S.dma("sync", onesc_sb[:, :], onesc, ch="lds")
    S.dma("sync", e2t_sb[:, :], e2t, ch="lds")
    S.dma("sync", e2c_sb[:, :], e2c, ch="lds")
    t_lds = S.dma("sync", id_sb[:, :], ident, ch="lds")
    QS = bcols[:, 2 * KC2:2 * KC2 + 1]
    KS = bcols[:, 2 * KC2 + 1:2 * KC2 + 2]
    B1A = bcols[:, 0:KC2]
    B1G = bcols[:, KC2:2 * KC2]
    LG = bcols[:, 2 * KC2 + 2:2 * KC2 + 10]
    LB = bcols[:, 2 * KC2 + 10:2 * KC2 + 18]
    B2 = bcols[:, 2 * KC2 + 18:2 * KC2 + 26]
    pre = [t_lds]
    t_az = S.emit("vector", lambda e: e.memset(afm[:, :], 0.0))

    zf_cons = []
    zs_cons = []
    cs_cons = []
    qwin_lastread = None
    attn_done = {}
    ropebuf_gate = None
    acc_free = None
    ns_free = None
    rb_free = None
    ptp_free = None
    ptv_free = None
    pto_free = None
    msc_free = None
    p16_free = None
    pt16_free = None
    vt16_free = None
    av_free = [None, None]
    sc_free = [None, None]
    afm_lastread = t_az
    x1st_stored = None
    vt_free = None
    sc_idx = 0

    for j in range(NTA):
        g0 = -512 + 512 * j
        ldw = (pre + zf_cons) if j == 0 else zf_cons
        t_z = None
        for c in range(8):
            w = ldw if c == 0 else []
            if g0 - 1 >= 0:
                t_z = S.dma("sync", zf[:, c * 513:(c + 1) * 513],
                            zin[c * 128:(c + 1) * 128, g0 - 1:g0 + 512],
                            waits=w, ch="ldz")
            elif g0 + 512 <= 0:
                t_z = S.dma("sync", zf[:, c * 513:(c + 1) * 513],
                            zhalo[c * 128:(c + 1) * 128, 640 + g0 - 1:640 + g0 + 512],
                            waits=w, ch="ldz")
            else:
                nh = -(g0 - 1)
                S.dma("sync", zf[:, c * 513:c * 513 + nh],
                      zhalo[c * 128:(c + 1) * 128, 640 + g0 - 1:640],
                      waits=w, ch="ldz", slow=True)
                t_z = S.dma("sync", zf[:, c * 513 + nh:(c + 1) * 513],
                            zin[c * 128:(c + 1) * 128, 0:g0 + 512], ch="ldz")
        t_cs = None
        for tb in range(4):
            t_cs = S.dma("sync", cs16[:, tb * 512:(tb + 1) * 512],
                         cstab[:, tb * 4608 + j * 512: tb * 4608 + (j + 1) * 512],
                         waits=cs_cons if tb == 0 else [], ch="ldcs")
        CSQ = cs16[:, 0:512]; SNQ = cs16[:, 512:1024]
        CSK = cs16[:, 1024:1536]; SNK = cs16[:, 1536:2048]
        t_zs = []
        for c in range(8):
            off = 1 if c < 4 else 0
            t_zs.append(S.emit("scalar", lambda e, c=c, off=off: e.activation(
                zs16[:, c * 512:(c + 1) * 512],
                zf[:, c * 513 + off:c * 513 + off + 512],
                AF.Copy), waits=[t_z] + zs_cons))
        wA, wB = 2 * j, 2 * j + 1
        sA, sB = wA % 3, wB % 3
        cs_cons = []
        mm = None
        for oc in range(24):
            for kc in range(8):
                w = (t_zs + [acc_free]) if kc == 0 else []
                mm = S.emit("tensor", lambda e, oc=oc, kc=kc: e.matmul(
                    PS_ACC, wq16[:, kc * 3072 + oc * 128: kc * 3072 + (oc + 1) * 128],
                    zs16[:, kc * 512:(kc + 1) * 512],
                    start=(kc == 0), stop=(kc == 7)), waits=w)
            hpair = oc % 8
            hrow_even = 0
            if oc < 16:
                is_q = oc < 8
                CS, SN = (CSQ, SNQ) if is_q else (CSK, SNK)
                scol = QS if is_q else KS
                t_sq = S.emit("scalar", lambda e: e.activation(
                    qsq[:, :], PS_ACC, AF.Square), waits=[mm, ropebuf_gate])
                t_ns = S.emit("tensor", lambda e: e.matmul(
                    PS_NS, e2c_sb[:, :], qsq[:, :], start=True, stop=True),
                    waits=[t_sq, ns_free])
                t_n1 = S.emit("scalar", lambda e: e.activation(nrow[0:2, :], PS_NS,
                                                               AF.Sqrt),
                              waits=[t_ns])
                t_n2 = S.emit("vector", lambda e: e.tensor_scalar_max(
                    nrow[0:2, :], nrow[0:2, :], 1e-12), waits=[t_n1])
                t_n3 = S.emit("vector", lambda e: e.reciprocal(
                    nrow[0:2, :], nrow[0:2, :]))
                if is_q:
                    t_n3 = S.emit("vector", lambda e: e.tensor_scalar_mul(
                        nrow[0:2, :], nrow[0:2, :], 8.0))
                t_n4 = S.emit("scalar", lambda e: e.copy(nrow16[0:2, :],
                                                         nrow[0:2, :]),
                              waits=[t_n3])
                ns_free = t_n1
                t_rb = S.emit("tensor", lambda e: e.matmul(
                    PS_RB, e2t_sb[:, :], nrow16[0:2, :], start=True, stop=True),
                    waits=[t_n4, rb_free])
                t_q1 = S.emit("scalar", lambda e, scol=scol: e.activation(
                    q1t[:, :], PS_ACC, AF.Identity, scale=scol), waits=[mm])
                acc_free = t_q1
                t_qq = S.emit("vector", lambda e: e.tensor_tensor(
                    qq[:, :], q1t[:, :], PS_RB, ALU.mult), waits=[t_q1, t_rb])
                rb_free = t_qq
                t_sw = None
                for (d0, s0_) in ((0, 32), (32, 0), (64, 96), (96, 64)):
                    t_sw = S.emit("gpsimd", lambda e, d0=d0, s0_=s0_: e.tensor_copy(
                        qsw[d0:d0 + 32, :], qq[s0_:s0_ + 32, :]), waits=[t_qq])
                t_ra = S.emit("vector", lambda e, CS=CS: e.tensor_tensor(
                    qra[:, :], qq[:, :], CS, ALU.mult), waits=[t_cs])
                t_sn = S.emit("gpsimd", lambda e, SN=SN: e.tensor_tensor(
                    qsw[:, :], qsw[:, :], SN, ALU.mult), waits=[t_sw, t_cs])
                cs_cons += [t_ra, t_sn]
                if is_q:
                    t_out = S.emit("vector", lambda e, hpair=hpair: e.tensor_tensor(
                        qwin[:, hpair * 512:(hpair + 1) * 512], qra[:, :],
                        qsw[:, :], ALU.add),
                        waits=[t_sn, t_ra, qwin_lastread])
                else:
                    gate = [attn_done.get(2 * j - 2), t_sn, t_ra]
                    S.emit("vector", lambda e, hpair=hpair, sA=sA: e.tensor_tensor(
                        kring[:, hpair * 768 + sA * 256: hpair * 768 + (sA + 1) * 256],
                        qra[:, 0:256], qsw[:, 0:256], ALU.add), waits=gate)
                    t_out = S.emit("vector", lambda e, hpair=hpair, sB=sB:
                                   e.tensor_tensor(
                        kring[:, hpair * 768 + sB * 256: hpair * 768 + (sB + 1) * 256],
                        qra[:, 256:512], qsw[:, 256:512], ALU.add))
                ropebuf_gate = t_out
                acc_free = t_out
            else:
                gate = [mm, attn_done.get(2 * j - 2)]
                S.emit("scalar", lambda e, hpair=hpair, sA=sA: e.activation(
                    vring[:, hpair * 768 + sA * 256: hpair * 768 + (sA + 1) * 256],
                    PS_ACC[:, 0:256], AF.Copy), waits=gate)
                t_out = S.emit("scalar", lambda e, hpair=hpair, sB=sB: e.activation(
                    vring[:, hpair * 768 + sB * 256: hpair * 768 + (sB + 1) * 256],
                    PS_ACC[:, 256:512], AF.Copy))
                acc_free = t_out
        qkv_done = [mm, acc_free, ropebuf_gate]

        wo_wait = []
        for wloc, wabs in ((0, wA), (1, wB)):
            if wabs == 0:
                continue
            sw = wabs % 3
            sp = (wabs - 1) % 3
            mbase = (2 if wabs == 2 else 0)
            for h in range(16):
                hpair = h // 2
                hrow = (h % 2) * 64
                tvt = None
                for piece in range(4):
                    src_s = sp if piece < 2 else sw
                    pc = piece % 2
                    tvt = S.emit("tensor", lambda e, hpair=hpair, hrow=hrow,
                                 src_s=src_s, pc=pc: e.transpose(
                        ps_t[:, 512 + 0:512 + 64],
                        vring[hrow:hrow + 64,
                              hpair * 768 + src_s * 256 + pc * 128:
                              hpair * 768 + src_s * 256 + (pc + 1) * 128],
                        id_sb[hrow:hrow + 64, hrow:hrow + 64]),
                        waits=([vt_free, vt16_free] + qkv_done) if piece == 0 else [vt_free])
                    tve = S.emit("vector", lambda e, piece=piece: e.tensor_copy(
                        vt16[:, piece * 64:(piece + 1) * 64],
                        ps_t[:, 512:512 + 64]), waits=[tvt, vt16_free])
                    vt_free = tve
                for qt in range(2):
                    sc = sc_idx % 2
                    sc_idx += 1
                    qcol = wloc * 256 + qt * 128
                    t_m1 = S.emit("tensor", lambda e, hpair=hpair, hrow=hrow,
                                  qcol=qcol, sp=sp, sc=sc: e.matmul(
                        PS_SC[sc][:, 0:256],
                        qwin[hrow:hrow + 64, hpair * 512 + qcol: hpair * 512 + qcol + 128],
                        kring[hrow:hrow + 64, hpair * 768 + sp * 256: hpair * 768 + (sp + 1) * 256],
                        start=True, stop=True),
                        waits=[sc_free[sc]] + qkv_done)
                    t_m2 = S.emit("tensor", lambda e, hpair=hpair, hrow=hrow,
                                  qcol=qcol, sw=sw, sc=sc: e.matmul(
                        PS_SC[sc][:, 256:512],
                        qwin[hrow:hrow + 64, hpair * 512 + qcol: hpair * 512 + qcol + 128],
                        kring[hrow:hrow + 64, hpair * 768 + sw * 256: hpair * 768 + (sw + 1) * 256],
                        start=True, stop=True))
                    t_mk = S.emit("vector", lambda e, sc=sc, mbase=mbase, qt=qt:
                                  e.tensor_tensor(
                        msc[:, :], PS_SC[sc],
                        mask_sb[:, (mbase + qt) * 512:(mbase + qt + 1) * 512],
                        ALU.add), waits=[t_m2, t_m1, msc_free])
                    sc_free[sc] = t_mk
                    t_mx = S.emit("vector", lambda e: e.reduce_max(
                        mx[:, 0:1], msc[:, :], axis=mybir.AxisListType.X,
                        negate=True))
                    t_ex = S.emit("scalar", lambda e: e.activation(
                        p16[:, :], msc[:, :], AF.Exp, bias=mx[:, 0:1],
                        accum_out=rs[:, 0:1]), waits=[t_mx, pto_free, p16_free])
                    msc_free = t_ex
                    t_rc = S.emit("vector", lambda e: e.reciprocal(
                        rs[:, 0:1], rs[:, 0:1]), waits=[t_ex])
                    tpt = None
                    for pc in range(4):
                        tpt = S.emit("tensor", lambda e, pc=pc: e.transpose(
                            ps_t[:, pc * 128:(pc + 1) * 128],
                            p16[:, pc * 128:(pc + 1) * 128], id_sb[:, :]),
                            waits=[t_ex, ptp_free] if pc == 0 else [])
                    p16_free = tpt
                    t_pe = S.emit("vector", lambda e: e.tensor_copy(
                        pt16[:, :], ps_t[:, 0:512]), waits=[tpt, pt16_free])
                    ptp_free = t_pe
                    av = sc
                    tav = None
                    for pc in range(4):
                        tav = S.emit("tensor", lambda e, pc=pc, av=av: e.matmul(
                            PS_AV[av], pt16[:, pc * 128:(pc + 1) * 128],
                            vt16[:, pc * 64:(pc + 1) * 64],
                            start=(pc == 0), stop=(pc == 3)),
                            waits=[t_pe, av_free[av]] if pc == 0 else [])
                    t_at = S.emit("vector", lambda e, av=av: e.tensor_scalar_mul(
                        atm[:, :], PS_AV[av], rs[:, 0:1]), waits=[tav, t_rc, pto_free])
                    av_free[av] = t_at
                    pt16_free = tav
                    vt16_free = tav
                    t_ot = S.emit("tensor", lambda e: e.transpose(
                        ps_t[0:64, 768:896], atm[:, :], id_sb[:, :]),
                        waits=[t_at])
                    t_oe = S.emit("scalar", lambda e, hpair=hpair, hrow=hrow,
                                  qcol=qcol: e.activation(
                        afm[hrow:hrow + 64, hpair * 512 + qcol: hpair * 512 + qcol + 128],
                        ps_t[0:64, 768:896], AF.Copy),
                        waits=[t_ot, afm_lastread])
                    pto_free = t_oe
                    wo_wait.append(t_oe)
                    attn_done[wabs] = tav
        res_toks = []
        prev_res = None
        for fc in range(8):
            mmw = None
            for kc in range(8):
                w = (wo_wait + [x1st_stored, prev_res]) if kc == 0 else []
                mmw = S.emit("tensor", lambda e, fc=fc, kc=kc: e.matmul(
                    PS_ACC, wo16[:, kc * 1024 + fc * 128: kc * 1024 + (fc + 1) * 128],
                    afm[:, kc * 512:(kc + 1) * 512],
                    start=(kc == 0), stop=(kc == 7)), waits=w)
            t_rs = S.emit("vector", lambda e, fc=fc: e.tensor_tensor(
                x1st[:, fc * 512:(fc + 1) * 512], PS_ACC,
                zf[:, fc * 513 + 1: fc * 513 + 513], ALU.add), waits=[mmw])
            res_toks.append(t_rs)
            prev_res = t_rs
        afm_lastread = mmw
        acc_free = res_toks[-1]
        c0 = 512 * j - 256
        tst = None
        for fc in range(8):
            if j == 0:
                tst = S.dma("sync", x1s[fc * 128:(fc + 1) * 128, 0:256],
                            x1st[:, fc * 512 + 256:(fc + 1) * 512],
                            waits=res_toks if fc == 0 else [], ch="stx1")
            else:
                tst = S.dma("sync", x1s[fc * 128:(fc + 1) * 128, c0:c0 + 512],
                            x1st[:, fc * 512:(fc + 1) * 512],
                            waits=res_toks if fc == 0 else [], ch="stx1")
        x1st_stored = tst
        zf_cons = t_zs + res_toks
        zs_cons = [mm]
        qwin_lastread = attn_done.get(wB)

    passa_done = [x1st_stored, acc_free, ropebuf_gate, vt_free, pto_free]
    for c in range(8):
        S.dma("gpsimd", w1a16[:, c * 2730:(c + 1) * 2730],
              w1a[c * 128:(c + 1) * 128, :],
              waits=passa_done if c == 0 else [], ch="ldg")
    for c in range(8):
        S.dma("gpsimd", w1g16[:, c * 2730:(c + 1) * 2730],
              w1g[c * 128:(c + 1) * 128, :], ch="ldg")
    t_wb = None
    for c in range(KC2):
        r0 = c * 128
        r1 = min(2730, r0 + 128)
        t_wb = S.dma("gpsimd", w216[0:r1 - r0, c * 1024:(c + 1) * 1024],
                     w2[r0:r1, :], ch="ldg")
    x1_cons = []
    g_free = None
    a_free = None
    w2_free = None
    st_free = None
    r_free = None
    m_free = None
    gg_lastread = None
    ff_lastread = None
    zst_stored = [None, None]
    tstz = None
    for i in range(NTB):
        db = 0
        x1t = x1f[:, :]
        t_x1 = None
        for c in range(8):
            w = ([t_wb] + x1_cons) if (i < 2 and c == 0) else (x1_cons if c == 0 else [])
            t_x1 = S.dma("sync", x1t[:, c * 257:(c + 1) * 257],
                         x1s[c * 128:(c + 1) * 128, 255 + 256 * i: 512 + 256 * i],
                         waits=w, ch=f"ldxb{db}")
        t_xv = []
        for c in range(8):
            off = 1 if c < 4 else 0
            ta = S.emit("scalar", lambda e, c=c, off=off, x1t=x1t: e.activation(
                xv16[:, c * 256:(c + 1) * 256],
                x1t[:, c * 257 + off: c * 257 + off + 256], AF.Copy),
                waits=[t_x1, st_free])
            tb = S.emit("scalar", lambda e, c=c: e.activation(
                xsq16[:, c * 256:(c + 1) * 256], xv16[:, c * 256:(c + 1) * 256],
                AF.Square))
            t_xv += [ta, tb]
        tmškoly = None
        tm_ = None
        for c in range(8):
            tm_ = S.emit("tensor", lambda e, c=c: e.matmul(
                PS_B_ST[0:1, 0:256], onesc_sb[:, :], xv16[:, c * 256:(c + 1) * 256],
                start=(c == 0), stop=(c == 7)),
                waits=(t_xv + [st_free]) if c == 0 else [])
        for c in range(8):
            tm_ = S.emit("tensor", lambda e, c=c: e.matmul(
                PS_B_ST[0:1, 256:512], onesc_sb[:, :],
                xsq16[:, c * 256:(c + 1) * 256],
                start=(c == 0), stop=(c == 7)))
        RW = 256
        mu = strow[0:1, 0:RW]; ex2 = strow[0:1, RW:2 * RW]
        var = strow[0:1, 2 * RW:3 * RW]; rstd = strow[0:1, 3 * RW:4 * RW]
        mur = strow[0:1, 4 * RW:5 * RW]
        t1 = S.emit("vector", lambda e: e.tensor_scalar_mul(
            mu, PS_B_ST[0:1, 0:256], 1.0 / D), waits=[tm_])
        t1x = S.emit("vector", lambda e: e.tensor_scalar_mul(
            ex2, PS_B_ST[0:1, 256:512], 1.0 / D))
        S.emit("vector", lambda e: e.tensor_tensor(var, mu, mu, ALU.mult))
        S.emit("vector", lambda e: e.tensor_tensor(var, ex2, var, ALU.subtract))
        t4 = S.emit("vector", lambda e: e.tensor_scalar_add(var, var, 1e-5))
        t5 = S.emit("scalar", lambda e: e.activation(rstd, var, AF.Sqrt), waits=[t4])
        t6 = S.emit("vector", lambda e: e.reciprocal(rstd, rstd), waits=[t5])
        t7 = S.emit("vector", lambda e: e.tensor_tensor(mur, mu, rstd, ALU.mult))
        t8 = S.emit("scalar", lambda e: e.copy(srow16[0:1, 0:RW], rstd),
                    waits=[t6, r_free])
        t9 = S.emit("scalar", lambda e: e.copy(srow16[0:1, RW:2 * RW], mur),
                    waits=[t7, m_free])
        st_free = t1x
        tR = S.emit("tensor", lambda e: e.matmul(PS_B_R[:, 0:256], ones1_sb[:, :],
                                                 srow16[0:1, 0:RW],
                                                 start=True, stop=True),
                    waits=[t8, r_free])
        tM = S.emit("tensor", lambda e: e.matmul(PS_B_M[:, 0:256], ones1_sb[:, :],
                                                 srow16[0:1, RW:2 * RW],
                                                 start=True, stop=True),
                    waits=[t9, m_free])
        xn_toks = []
        for c in range(8):
            S.emit("vector", lambda e, c=c: e.tensor_tensor(
                xn16[:, c * 256:(c + 1) * 256], xv16[:, c * 256:(c + 1) * 256],
                PS_B_R[:, 0:256], ALU.mult), waits=[tR, g_free, a_free])
            S.emit("vector", lambda e, c=c: e.tensor_tensor(
                xn16[:, c * 256:(c + 1) * 256], xn16[:, c * 256:(c + 1) * 256],
                PS_B_M[:, 0:256], ALU.subtract), waits=[tM])
            tc_ = S.emit("vector", lambda e, c=c: e.tensor_scalar(
                xn16[:, c * 256:(c + 1) * 256], xn16[:, c * 256:(c + 1) * 256],
                LG[:, c:c + 1], LB[:, c:c + 1], ALU.mult, ALU.add))
            xn_toks.append(tc_)
        r_free = xn_toks[-1]
        m_free = xn_toks[-1]
        gg_toks = []
        prev_g_evac = None
        for oc in range(KC2):
            mo = min(128, 2730 - oc * 128)
            mmg = None
            for kc in range(8):
                w = (xn_toks + [gg_lastread, prev_g_evac]) if kc == 0 else []
                mmg = S.emit("tensor", lambda e, oc=oc, kc=kc, mo=mo: e.matmul(
                    PS_B_G[0:mo, 0:256],
                    w1g16[:, kc * 2730 + oc * 128: kc * 2730 + oc * 128 + mo],
                    xn16[:, kc * 256:(kc + 1) * 256],
                    start=(kc == 0), stop=(kc == 7)), waits=w)
            tg = S.emit("scalar", lambda e, oc=oc, mo=mo: e.activation(
                gg16[0:mo, oc * 256:(oc + 1) * 256], PS_B_G[0:mo, 0:256],
                AF.Gelu, bias=B1G[0:mo, oc:oc + 1]), waits=[mmg])
            gg_toks.append(tg)
            prev_g_evac = tg
        g_free = gg_toks[-1]
        ff_toks = []
        prev_f_evac = None
        for oc in range(KC2):
            mo = min(128, 2730 - oc * 128)
            mma = None
            for kc in range(8):
                w = ([gg_toks[oc], ff_lastread, prev_f_evac]) if kc == 0 else []
                mma = S.emit("tensor", lambda e, oc=oc, kc=kc, mo=mo: e.matmul(
                    PS_B_A[0:mo, 0:256],
                    w1a16[:, kc * 2730 + oc * 128: kc * 2730 + oc * 128 + mo],
                    xn16[:, kc * 256:(kc + 1) * 256],
                    start=(kc == 0), stop=(kc == 7)), waits=w)
            tf = S.emit("vector", lambda e, oc=oc, mo=mo: e.scalar_tensor_tensor(
                ff16[0:mo, oc * 256:(oc + 1) * 256], PS_B_A[0:mo, 0:256],
                B1A[0:mo, oc:oc + 1], gg16[0:mo, oc * 256:(oc + 1) * 256],
                ALU.add, ALU.mult), waits=[mma])
            ff_toks.append(tf)
            prev_f_evac = tf
        a_free = ff_toks[-1]
        z_toks = []
        prev_z_evac = None
        for fc in range(8):
            mmz = None
            for kc in range(KC2):
                ko = min(128, 2730 - kc * 128)
                w = (ff_toks + [zst_stored[db], prev_z_evac]) if kc == 0 else []                     if kc != 0 else (ff_toks + [zst_stored[db], prev_z_evac])
                mmz = S.emit("tensor", lambda e, fc=fc, kc=kc, ko=ko: e.matmul(
                    PS_B_G[:, 0:256] if False else PS_B_A[:, 0:256],
                    w216[0:ko, kc * 1024 + fc * 128: kc * 1024 + (fc + 1) * 128],
                    ff16[0:ko, kc * 256:(kc + 1) * 256],
                    start=(kc == 0), stop=(kc == KC2 - 1)), waits=w)
            tz_ = S.emit("vector", lambda e, fc=fc, x1t=x1t: e.scalar_tensor_tensor(
                zst[:, fc * 256:(fc + 1) * 256],
                PS_B_A[:, 0:256], B2[:, fc:fc + 1],
                x1t[:, fc * 257 + 1:(fc + 1) * 257], ALU.add, ALU.add),
                waits=[mmz, zst_stored[0]])
            z_toks.append(tz_)
            prev_z_evac = tz_
        ff_lastread = mmz
        gg_lastread = mma
        a_free = z_toks[-1]
        tstz = None
        for fc in range(8):
            tstz = S.dma("sync", zout[fc * 128:(fc + 1) * 128, 256 * i:256 * (i + 1)],
                         zst[:, fc * 256:(fc + 1) * 256],
                         waits=z_toks if fc == 0 else [], ch="stz")
        zst_stored[0] = tstz
        x1_cons = t_xv + z_toks
    S.emit("gpsimd", lambda e: e.engine_nop(),
           waits=[zst_stored[0]], inc=0)
    S.finalize()
    return nc




def _l0f_host_consts():
    ones1 = np.ones((1, 128), dtype=BF)
    onesc = np.ones((128, 1), dtype=BF)
    e2t = np.zeros((2, 128), dtype=BF)
    e2t[0, 0:64] = 1.0
    e2t[1, 64:128] = 1.0
    e2c = np.ascontiguousarray(e2t.T)
    ident = np.eye(128, dtype=BF)
    def mk_mask(qt, maskprev):
        m = np.zeros((128, 512), np.float32)
        for p in range(128):
            ql = qt * 128 + p
            for seg in range(2):
                j0 = seg * 256
                for jj in range(256):
                    if seg == 0:
                        bad = (jj < ql) or maskprev
                    else:
                        bad = jj > ql
                    if bad:
                        m[p, j0 + jj] = -1e30
        return m
    masks_even = np.concatenate([mk_mask(0, False), mk_mask(1, False),
                                 mk_mask(0, True), mk_mask(1, True)], axis=1)
    masks_odd = np.concatenate([mk_mask(0, False), mk_mask(1, False),
                                mk_mask(0, False), mk_mask(1, False)], axis=1)
    DIM_HEAD, WINDOW, n = 64, 256, 8192
    half_ = np.arange(0, DIM_HEAD, 2, dtype=np.float64)
    inv_freq = 1.0 / 10000.0 ** (half_ / DIM_HEAD)
    base_scale = (half_ + 0.4 * DIM_HEAD) / (1.4 * DIM_HEAD)
    tabs = []
    for hh in range(2):
        g0 = hh * 4096 - 512
        t = np.arange(g0, g0 + 4608, dtype=np.float64)
        ang = t[:, None] * inv_freq[None, :]
        power = (t - n // 2) / (WINDOW // 2)
        scale = base_scale[None, :] ** power[:, None]
        cs = np.cos(ang) * scale
        sn = np.sin(ang) * scale
        csk = np.cos(ang) / scale
        snk = np.sin(ang) / scale
        rows = np.arange(128) % 32
        sign = np.where((np.arange(128) % 64) < 32, -1.0, 1.0)[:, None]
        CSQ = cs.T[rows].astype(np.float32)
        SNQ = (sn.T[rows] * sign).astype(np.float32)
        CSK = csk.T[rows].astype(np.float32)
        SNK = (snk.T[rows] * sign).astype(np.float32)
        tabs.append(np.ascontiguousarray(np.concatenate([CSQ, SNQ, CSK, SNK], axis=1)))
    return {"ones1": ones1, "onesc": onesc, "e2t": e2t, "e2c": e2c,
            "ident": ident, "masks": (masks_even, masks_odd), "tabs": tabs}


def _l0f_host(inputs, layer, z_fm, consts):
    KC2 = 22
    Wq = inputs["L0_ffn_Wqkv"][layer]
    Wo_ = inputs["L0_ffn_Wo"][layer]
    W1 = inputs["L0_ffn_W1"][layer]
    b1 = inputs["L0_ffn_b1"][layer]
    W2_ = inputs["L0_ffn_W2"][layer]
    qs = inputs["L0_ffn_q_scale"][layer]
    ks = inputs["L0_ffn_k_scale"][layer]
    lg = inputs["L0_ffn_ln_g"][layer]
    lb = inputs["L0_ffn_ln_b"][layer]
    b2 = inputs["L0_ffn_b2"][layer]
    def colchunks(v, n, kc):
        out = np.zeros((128, kc), np.float32)
        for c in range(kc):
            r0 = c * 128
            r1 = min(n, r0 + 128)
            out[0:r1 - r0, c] = v[r0:r1]
        return out
    w1a = np.ascontiguousarray(W1[:, :2730])
    w1g = np.ascontiguousarray(W1[:, 2730:])
    base = {
        "wqkv": np.ascontiguousarray(Wq),
        "wo": np.ascontiguousarray(Wo_),
        "w1a": w1a, "w1g": w1g, "w2": np.ascontiguousarray(W2_),
        "qsc": np.ascontiguousarray(qs[np.arange(128) % 64].reshape(128, 1)),
        "ksc": np.ascontiguousarray(ks[np.arange(128) % 64].reshape(128, 1)),
        "b1ac": colchunks(b1[:2730], 2730, KC2),
        "b1gc": colchunks(b1[2730:], 2730, KC2),
        "lgc": np.ascontiguousarray(lg.reshape(8, 128).T),
        "lbc": np.ascontiguousarray(lb.reshape(8, 128).T),
        "b2c": np.ascontiguousarray(b2.reshape(8, 128).T),
        "ones1": consts["ones1"], "onesc": consts["onesc"],
        "e2t": consts["e2t"], "e2c": consts["e2c"], "ident": consts["ident"],
    }
    in_maps = []
    for cid in range(NCORE):
        b = cid // 2
        h = cid % 2
        m = dict(base)
        m["maskt"] = consts["masks"][0] if h == 0 else consts["masks"][1]
        m["cstab"] = consts["tabs"][h]
        m["zin"] = np.ascontiguousarray(z_fm[b][:, h * TC:(h + 1) * TC])
        if h == 0:
            m["zhalo"] = np.zeros((D, 640), np.float32)
        else:
            m["zhalo"] = np.ascontiguousarray(z_fm[b][:, TC - 640:TC])
        in_maps.append(m)
    return in_maps




def build_l0f_post():
    nc = bass.Bass("TRN2", target_bir_lowering=False,
                   detect_race_conditions=False)
    NT = TC // 512
    x1i = nc.dram_tensor("x1i", [D, TC], F32, kind="ExternalInput").ap()
    z6i = nc.dram_tensor("z6i", [D, TC], F32, kind="ExternalInput").ap()
    ones1 = nc.dram_tensor("ones1", [1, 128], BF16, kind="ExternalInput").ap()
    onesc = nc.dram_tensor("onesc", [128, 1], BF16, kind="ExternalInput").ap()
    x2o = nc.dram_tensor("x2o", [D, TC], F32, kind="ExternalOutput").ap()
    z7o = nc.dram_tensor("z7o", [D, TC], F32, kind="ExternalOutput").ap()

    ones1_sb = nc.alloc_sbuf_tensor("ones1_sb", [1, 128], BF16)
    onesc_sb = nc.alloc_sbuf_tensor("onesc_sb", [128, 1], BF16)
    xsb = nc.alloc_sbuf_tensor("xsb", [128, 2 * 8 * 512], F32)
    zsb = nc.alloc_sbuf_tensor("zsb", [128, 2 * 8 * 512], F32)
    a16 = nc.alloc_sbuf_tensor("a16", [128, 8 * 512], BF16)
    asq = nc.alloc_sbuf_tensor("asq", [128, 8 * 512], BF16)
    x2b = nc.alloc_sbuf_tensor("x2b", [128, 8 * 512], F32)
    z7b = nc.alloc_sbuf_tensor("z7b", [128, 8 * 512], F32)
    strow = nc.alloc_sbuf_tensor("strow", [1, 5 * 512], F32)
    srow16 = nc.alloc_sbuf_tensor("srow16", [1, 2 * 512], BF16)
    ps = nc.alloc_psum_tensor("ps", [128, 2048], F32)
    PS_SUM = ps[0:1, 0:512]
    PS_SQ = ps[0:1, 512:1024]
    PS_R = ps[:, 1024:1536]
    PS_M = ps[:, 1536:2048]

    S = Sched(nc)
    S.dma("sync", ones1_sb[:, :], ones1, ch="lds")
    t_lds = S.dma("sync", onesc_sb[:, :], onesc, ch="lds")
    pre = [t_lds]

    def ln_rows(S, t_stats):
        RW = 512
        mu = strow[0:1, 0:RW]; ex2 = strow[0:1, RW:2 * RW]
        var = strow[0:1, 2 * RW:3 * RW]; rstd = strow[0:1, 3 * RW:4 * RW]
        mur = strow[0:1, 4 * RW:5 * RW]
        S.emit("vector", lambda e: e.tensor_scalar_mul(mu, PS_SUM, 1.0 / D),
               waits=[t_stats])
        t1x = S.emit("vector", lambda e: e.tensor_scalar_mul(ex2, PS_SQ, 1.0 / D))
        S.emit("vector", lambda e: e.tensor_tensor(var, mu, mu, ALU.mult))
        S.emit("vector", lambda e: e.tensor_tensor(var, ex2, var, ALU.subtract))
        t4 = S.emit("vector", lambda e: e.tensor_scalar_add(var, var, 1e-5))
        t5 = S.emit("scalar", lambda e: e.activation(rstd, var, AF.Sqrt), waits=[t4])
        t6 = S.emit("vector", lambda e: e.reciprocal(rstd, rstd), waits=[t5])
        t7 = S.emit("vector", lambda e: e.tensor_tensor(mur, mu, rstd, ALU.mult))
        t8 = S.emit("scalar", lambda e: e.copy(srow16[0:1, 0:RW], rstd), waits=[t6])
        t9 = S.emit("scalar", lambda e: e.copy(srow16[0:1, RW:2 * RW], mur),
                    waits=[t7])
        return t1x, t8, t9

    prev_cons = []
    prev_R_reader = None
    st_toks = []
    for tt in range(NT):
        t0 = tt * 512
        db = tt % 2
        xb = xsb[:, db * 4096:(db + 1) * 4096]
        zb = zsb[:, db * 4096:(db + 1) * 4096]
        ldw = (pre + prev_cons) if tt < 2 else prev_cons
        t_x = None
        for c in range(8):
            t_x = S.dma("sync", xb[:, c * 512:(c + 1) * 512],
                        x1i[c * 128:(c + 1) * 128, t0:t0 + 512],
                        waits=ldw if c == 0 else [], ch=f"ld{db}")
        for c in range(8):
            t_x = S.dma("sync", zb[:, c * 512:(c + 1) * 512],
                        z6i[c * 128:(c + 1) * 128, t0:t0 + 512], ch=f"ld{db}")
        cons = []
        casts = []
        for c in range(8):
            ta = S.emit("scalar", lambda e, c=c, zb=zb: e.activation(
                a16[:, c * 512:(c + 1) * 512], zb[:, c * 512:(c + 1) * 512],
                AF.Copy), waits=[t_x, prev_R_reader])
            tb = S.emit("scalar", lambda e, c=c: e.activation(
                asq[:, c * 512:(c + 1) * 512], a16[:, c * 512:(c + 1) * 512],
                AF.Square))
            casts += [ta, tb]
        tm = None
        for c in range(8):
            tm = S.emit("tensor", lambda e, c=c: e.matmul(
                PS_SUM, onesc_sb[:, :], a16[:, c * 512:(c + 1) * 512],
                start=(c == 0), stop=(c == 7)),
                waits=(casts + [prev_R_reader]) if c == 0 else [])
        for c in range(8):
            tm = S.emit("tensor", lambda e, c=c: e.matmul(
                PS_SQ, onesc_sb[:, :], asq[:, c * 512:(c + 1) * 512],
                start=(c == 0), stop=(c == 7)))
        t1x, t8, t9 = ln_rows(S, tm)
        tR = S.emit("tensor", lambda e: e.matmul(PS_R, ones1_sb[:, :],
                                                 srow16[0:1, 0:512],
                                                 start=True, stop=True), waits=[t8])
        tM = S.emit("tensor", lambda e: e.matmul(PS_M, ones1_sb[:, :],
                                                 srow16[0:1, 512:1024],
                                                 start=True, stop=True), waits=[t9])
        x2toks = []
        for c in range(8):
            S.emit("vector", lambda e, c=c: e.tensor_tensor(
                x2b[:, c * 512:(c + 1) * 512], a16[:, c * 512:(c + 1) * 512],
                PS_R, ALU.mult), waits=[tR])
            S.emit("vector", lambda e, c=c: e.tensor_tensor(
                x2b[:, c * 512:(c + 1) * 512], x2b[:, c * 512:(c + 1) * 512],
                PS_M, ALU.subtract), waits=[tM])
            ta = S.emit("vector", lambda e, c=c, xb=xb: e.tensor_tensor(
                x2b[:, c * 512:(c + 1) * 512], x2b[:, c * 512:(c + 1) * 512],
                xb[:, c * 512:(c + 1) * 512], ALU.add))
            tb = S.emit("scalar", lambda e, c=c: e.activation(
                a16[:, c * 512:(c + 1) * 512], x2b[:, c * 512:(c + 1) * 512],
                AF.Copy), waits=[ta])
            tc_ = S.emit("scalar", lambda e, c=c: e.activation(
                asq[:, c * 512:(c + 1) * 512], a16[:, c * 512:(c + 1) * 512],
                AF.Square))
            x2toks += [ta, tb, tc_]
        tsx = None
        for c in range(8):
            tsx = S.dma("sync", x2o[c * 128:(c + 1) * 128, t0:t0 + 512],
                        x2b[:, c * 512:(c + 1) * 512],
                        waits=x2toks if c == 0 else [], ch=f"sx{db}")
        tm2 = None
        for c in range(8):
            tm2 = S.emit("tensor", lambda e, c=c: e.matmul(
                PS_SUM, onesc_sb[:, :], a16[:, c * 512:(c + 1) * 512],
                start=(c == 0), stop=(c == 7)),
                waits=(x2toks + [t1x]) if c == 0 else [])
        for c in range(8):
            tm2 = S.emit("tensor", lambda e, c=c: e.matmul(
                PS_SQ, onesc_sb[:, :], asq[:, c * 512:(c + 1) * 512],
                start=(c == 0), stop=(c == 7)))
        t1b, t8b, t9b = ln_rows(S, tm2)
        tR2 = S.emit("tensor", lambda e: e.matmul(PS_R, ones1_sb[:, :],
                                                  srow16[0:1, 0:512],
                                                  start=True, stop=True),
                     waits=[t8b])
        tM2 = S.emit("tensor", lambda e: e.matmul(PS_M, ones1_sb[:, :],
                                                  srow16[0:1, 512:1024],
                                                  start=True, stop=True),
                     waits=[t9b])
        z7toks = []
        for c in range(8):
            S.emit("vector", lambda e, c=c: e.tensor_tensor(
                z7b[:, c * 512:(c + 1) * 512], a16[:, c * 512:(c + 1) * 512],
                PS_R, ALU.mult), waits=[tR2])
            tzb = S.emit("vector", lambda e, c=c: e.tensor_tensor(
                z7b[:, c * 512:(c + 1) * 512], z7b[:, c * 512:(c + 1) * 512],
                PS_M, ALU.subtract), waits=[tM2])
            z7toks.append(tzb)
        tsz = None
        for c in range(8):
            tsz = S.dma("sync", z7o[c * 128:(c + 1) * 128, t0:t0 + 512],
                        z7b[:, c * 512:(c + 1) * 512],
                        waits=z7toks if c == 0 else [], ch=f"sz{db}")
        st_toks = [tsx, tsz]
        prev_R_reader = z7toks[-1]
        prev_cons = casts + x2toks + z7toks + [t1b]
    S.emit("gpsimd", lambda e: e.engine_nop(), waits=st_toks, inc=0)
    S.finalize()
    return nc


def build_l1a_qkv():
    nc = bass.Bass("TRN2", target_bir_lowering=False,
                   detect_race_conditions=False)
    NT = TC // 512
    z7i = nc.dram_tensor("z7i", [D, TC], F32, kind="ExternalInput").ap()
    wq = nc.dram_tensor("wq", [D, D], F32, kind="ExternalInput").ap()
    wk = nc.dram_tensor("wk", [D, D], F32, kind="ExternalInput").ap()
    wv = nc.dram_tensor("wv", [D, D], F32, kind="ExternalInput").ap()
    cst = nc.dram_tensor("cst", [128, 2 * TC], F32, kind="ExternalInput").ap()
    ident = nc.dram_tensor("ident", [128, 128], BF16, kind="ExternalInput").ap()
    qpo = nc.dram_tensor("qpo", [D, TC], BF16, kind="ExternalOutput").ap()
    vkto = nc.dram_tensor("vkto", [64, 16 * 65], F32, kind="ExternalOutput").ap()

    wq16 = nc.alloc_sbuf_tensor("wq16", [128, 8 * 1024], BF16)
    wk16 = nc.alloc_sbuf_tensor("wk16", [128, 8 * 1024], BF16)
    wv16 = nc.alloc_sbuf_tensor("wv16", [128, 8 * 1024], BF16)
    cs_sb = nc.alloc_sbuf_tensor("cs_sb", [128, 2 * TC], F32)
    id_sb = nc.alloc_sbuf_tensor("id_sb", [128, 128], BF16)
    z16 = nc.alloc_sbuf_tensor("z16", [128, 2 * 8 * 512], BF16)
    qraw = nc.alloc_sbuf_tensor("qraw", [128, 512], F32)
    qsw = nc.alloc_sbuf_tensor("qsw", [128, 512], F32)
    qra = nc.alloc_sbuf_tensor("qra", [128, 512], F32)
    qp16 = nc.alloc_sbuf_tensor("qp16", [128, 8 * 512], BF16)
    kp16 = nc.alloc_sbuf_tensor("kp16", [128, 8 * 512], BF16)
    v16 = nc.alloc_sbuf_tensor("v16", [128, 8 * 512], BF16)
    ktt = nc.alloc_sbuf_tensor("ktt", [128, 64], BF16)
    vtt = nc.alloc_sbuf_tensor("vtt", [128, 65], BF16)
    vkt_sb = nc.alloc_sbuf_tensor("vkt_sb", [64, 16 * 65], F32)

    ps = nc.alloc_psum_tensor("ps", [128, 2048], F32)
    PS_MM = ps[:, 0:512]
    PS_VK = ps[0:64, 512:1024]
    ps_t = nc.alloc_psum_tensor("ps_t", [128, 256], BF16)
    PT_K = ps_t[:, 0:64]
    PT_V = ps_t[:, 64:128]

    S = Sched(nc)
    for c in range(8):
        S.dma("gpsimd", wq16[:, c * 1024:(c + 1) * 1024],
              wq[c * 128:(c + 1) * 128, :], ch="ldg")
    for c in range(8):
        S.dma("gpsimd", wk16[:, c * 1024:(c + 1) * 1024],
              wk[c * 128:(c + 1) * 128, :], ch="ldg")
    t_ldg = None
    for c in range(8):
        t_ldg = S.dma("gpsimd", wv16[:, c * 1024:(c + 1) * 1024],
                      wv[c * 128:(c + 1) * 128, :], ch="ldg")
    S.dma("sync", cs_sb[:, :], cst, ch="lds")
    t_lds = S.dma("sync", id_sb[:, :], ident, ch="lds")
    t_z0 = S.emit("vector", lambda e: e.memset(vkt_sb[:, :], 0.0))
    t_v1 = S.emit("vector", lambda e: e.memset(vtt[:, 64:65], 1.0))
    pre = [t_ldg, t_lds, t_z0, t_v1]

    prev_cons = []
    rope_gate = None
    mm_free = None
    vk_free = t_z0
    kt_free = None
    vt_free = t_v1
    st_q = None
    vkt_accum = {}
    for tt in range(NT):
        t0 = tt * 512
        db = tt % 2
        zb = z16[:, db * 4096:(db + 1) * 4096]
        ldw = (pre + prev_cons) if tt < 2 else prev_cons
        t_z = None
        for c in range(8):
            t_z = S.dma("gpsimd", zb[:, c * 512:(c + 1) * 512],
                        z7i[c * 128:(c + 1) * 128, t0:t0 + 512],
                        waits=ldw if c == 0 else [], ch=f"ldz{db}")
        CS = cs_sb[:, t0:t0 + 512]
        SN = cs_sb[:, TC + t0:TC + t0 + 512]
        qp_toks = []
        for (w16_, dst, dma_out) in ((wq16, qp16, True), (wk16, kp16, False)):
            for oc in range(8):
                mm = None
                for kc in range(8):
                    w = [t_z, mm_free] if kc == 0 else []
                    mm = S.emit("tensor", lambda e, oc=oc, kc=kc, w16_=w16_,
                                zb=zb: e.matmul(
                        PS_MM,
                        w16_[:, kc * 1024 + oc * 128: kc * 1024 + (oc + 1) * 128],
                        zb[:, kc * 512:(kc + 1) * 512],
                        start=(kc == 0), stop=(kc == 7)), waits=w)
                t_qr = S.emit("scalar", lambda e: e.activation(
                    qraw[:, :], PS_MM, AF.Copy), waits=[mm, rope_gate])
                mm_free = t_qr
                t_sw = None
                for (d0, s0_) in ((0, 32), (32, 0), (64, 96), (96, 64)):
                    t_sw = S.emit("gpsimd", lambda e, d0=d0, s0_=s0_: e.tensor_copy(
                        qsw[d0:d0 + 32, :], qraw[s0_:s0_ + 32, :]), waits=[t_qr])
                t_ra = S.emit("vector", lambda e, CS=CS: e.tensor_tensor(
                    qra[:, :], qraw[:, :], CS, ALU.mult), waits=[t_qr])
                t_sn = S.emit("gpsimd", lambda e, SN=SN: e.tensor_tensor(
                    qsw[:, :], qsw[:, :], SN, ALU.mult), waits=[t_sw])
                t_ad = S.emit("vector", lambda e: e.tensor_tensor(
                    qra[:, :], qra[:, :], qsw[:, :], ALU.add), waits=[t_sn])
                t_rl = S.emit("vector", lambda e, oc=oc, dst=dst: e.tensor_scalar_max(
                    dst[:, oc * 512:(oc + 1) * 512], qra[:, :], 0.0))
                rope_gate = t_rl
                if dma_out:
                    tq = S.dma("sync", qpo[oc * 128:(oc + 1) * 128, t0:t0 + 512],
                               dst[:, oc * 512:(oc + 1) * 512], waits=[t_rl],
                               ch="stq")
                    qp_toks.append(tq)
                    st_q = tq
        v_toks = []
        for oc in range(8):
            mm = None
            for kc in range(8):
                w = [t_z, mm_free] if kc == 0 else []
                mm = S.emit("tensor", lambda e, oc=oc, kc=kc, zb=zb: e.matmul(
                    PS_MM, wv16[:, kc * 1024 + oc * 128: kc * 1024 + (oc + 1) * 128],
                    zb[:, kc * 512:(kc + 1) * 512],
                    start=(kc == 0), stop=(kc == 7)), waits=w)
            tv = S.emit("scalar", lambda e, oc=oc: e.activation(
                v16[:, oc * 512:(oc + 1) * 512], PS_MM, AF.Copy), waits=[mm])
            mm_free = tv
            v_toks.append(tv)
        for h in range(16):
            hp, hr = h // 2, (h % 2) * 64
            for pc in range(4):
                tk = S.emit("tensor", lambda e, hp=hp, hr=hr, pc=pc: e.transpose(
                    PT_K, kp16[hr:hr + 64, hp * 512 + pc * 128: hp * 512 + (pc + 1) * 128],
                    id_sb[hr:hr + 64, hr:hr + 64]),
                    waits=[rope_gate, kt_free] + v_toks if pc == 0 else [kt_free])
                tke = S.emit("vector", lambda e: e.tensor_copy(ktt[:, :], PT_K),
                             waits=[tk])
                tv_ = S.emit("tensor", lambda e, hp=hp, hr=hr, pc=pc: e.transpose(
                    PT_V, v16[hr:hr + 64, hp * 512 + pc * 128: hp * 512 + (pc + 1) * 128],
                    id_sb[hr:hr + 64, hr:hr + 64]), waits=[vt_free])
                tve = S.emit("vector", lambda e: e.tensor_copy(vtt[:, 0:64], PT_V),
                             waits=[tv_])
                mmvk = S.emit("tensor", lambda e: e.matmul(
                    PS_VK[0:64, 0:65], ktt[:, :], vtt[:, :],
                    start=True, stop=True), waits=[tke, tve, vk_free])
                kt_free = mmvk
                vt_free = mmvk
                tacc = S.emit("vector", lambda e, h=h: e.tensor_tensor(
                    vkt_sb[:, h * 65:(h + 1) * 65], vkt_sb[:, h * 65:(h + 1) * 65],
                    PS_VK[0:64, 0:65], ALU.add), waits=[mmvk])
                vk_free = tacc
        prev_cons = [mm_free, rope_gate, vk_free, st_q]
    tvk = S.dma("sync", vkto, vkt_sb[:, :], waits=[vk_free], ch="stvk")
    S.emit("gpsimd", lambda e: e.engine_nop(), waits=[tvk, st_q], inc=0)
    S.finalize()
    return nc




def build_l1a_fin():
    nc = bass.Bass("TRN2", target_bir_lowering=False,
                   detect_race_conditions=False)
    NT = TC // 512
    qpi = nc.dram_tensor("qpi", [D, TC], BF16, kind="ExternalInput").ap()
    vk2 = nc.dram_tensor("vk2", [2 * 64, 16 * 65], F32, kind="ExternalInput").ap()
    x2i = nc.dram_tensor("x2i", [D, TC], F32, kind="ExternalInput").ap()
    owT = nc.dram_tensor("owT", [D, D], F32, kind="ExternalInput").ap()
    ones1 = nc.dram_tensor("ones1", [1, 128], BF16, kind="ExternalInput").ap()
    onesc = nc.dram_tensor("onesc", [128, 1], BF16, kind="ExternalInput").ap()
    ident = nc.dram_tensor("ident", [128, 128], BF16, kind="ExternalInput").ap()
    x3o = nc.dram_tensor("x3o", [D, TC], F32, kind="ExternalOutput").ap()
    z8o = nc.dram_tensor("z8o", [D, TC], F32, kind="ExternalOutput").ap()

    w16 = nc.alloc_sbuf_tensor("w16", [128, 8 * 1024], BF16)
    ones1_sb = nc.alloc_sbuf_tensor("ones1_sb", [1, 128], BF16)
    onesc_sb = nc.alloc_sbuf_tensor("onesc_sb", [128, 1], BF16)
    id_sb = nc.alloc_sbuf_tensor("id_sb", [128, 128], BF16)
    vk_a = nc.alloc_sbuf_tensor("vk_a", [64, 16 * 65], F32)
    vk_b = nc.alloc_sbuf_tensor("vk_b", [64, 16 * 65], F32)
    vk16 = nc.alloc_sbuf_tensor("vk16", [128, 16 * 65], BF16)
    qsb = nc.alloc_sbuf_tensor("qsb", [128, 2 * 8 * 512], BF16)
    xsb = nc.alloc_sbuf_tensor("xsb", [128, 2 * 8 * 512], F32)
    att = nc.alloc_sbuf_tensor("att", [128, 64], BF16)
    rcol = nc.alloc_sbuf_tensor("rcol", [128, 2], F32)
    fb = nc.alloc_sbuf_tensor("fb", [128, 128], F32)
    afm = nc.alloc_sbuf_tensor("afm", [128, 8 * 512], BF16)
    u16 = nc.alloc_sbuf_tensor("u16", [128, 8 * 512], BF16)
    usq = nc.alloc_sbuf_tensor("usq", [128, 8 * 512], BF16)
    x3b = nc.alloc_sbuf_tensor("x3b", [128, 8 * 512], F32)
    z8b = nc.alloc_sbuf_tensor("z8b", [128, 8 * 512], F32)
    strow = nc.alloc_sbuf_tensor("strow", [1, 5 * 512], F32)
    srow16 = nc.alloc_sbuf_tensor("srow16", [1, 2 * 512], BF16)

    ps = nc.alloc_psum_tensor("ps", [128, 3072], F32)
    PS_NUM = [ps[:, 0:65], ps[:, 512:577]]
    PS_MM = ps[:, 1024:1536]
    PS_SUM = ps[0:1, 1536:2048]
    PS_SQ = ps[0:1, 2048:2560]
    PS_R = ps[:, 2560:3072]
    PS_M = ps[:, 1536:2048]
    ps_t = nc.alloc_psum_tensor("ps_t", [128, 256], BF16)
    PT_A = ps_t[0:64, 0:128]

    S = Sched(nc)
    for c in range(8):
        S.dma("gpsimd", w16[:, c * 1024:(c + 1) * 1024],
              owT[c * 128:(c + 1) * 128, :], ch="ldg")
    t_ldg = S.dma("gpsimd", id_sb[:, :], ident, ch="ldg")
    S.dma("sync", vk_a[:, :], vk2[0:64, :], ch="lds")
    S.dma("sync", vk_b[:, :], vk2[64:128, :], ch="lds")
    S.dma("sync", ones1_sb[:, :], ones1, ch="lds")
    t_lds = S.dma("sync", onesc_sb[:, :], onesc, ch="lds")
    t_s1 = S.emit("vector", lambda e: e.tensor_tensor(vk_a[:, :], vk_a[:, :],
                                                      vk_b[:, :], ALU.add),
                  waits=[t_lds, t_ldg])
    t_s2 = S.emit("scalar", lambda e: e.copy(vk16[0:64, :], vk_a[:, :]),
                  waits=[t_s1])
    t_s3 = S.emit("scalar", lambda e: e.copy(vk16[64:128, :], vk_a[:, :]))
    t_fb = S.emit("vector", lambda e: e.memset(fb[:, :], 0.0))
    pre = [t_s2, t_s3, t_fb]

    def ln_rows(S, t_stats):
        RW = 512
        mu = strow[0:1, 0:RW]; ex2 = strow[0:1, RW:2 * RW]
        var = strow[0:1, 2 * RW:3 * RW]; rstd = strow[0:1, 3 * RW:4 * RW]
        mur = strow[0:1, 4 * RW:5 * RW]
        t1 = S.emit("vector", lambda e: e.tensor_scalar_mul(mu, PS_SUM, 1.0 / D),
                    waits=[t_stats])
        S.emit("vector", lambda e: e.tensor_scalar_mul(ex2, PS_SQ, 1.0 / D))
        S.emit("vector", lambda e: e.tensor_tensor(var, mu, mu, ALU.mult))
        S.emit("vector", lambda e: e.tensor_tensor(var, ex2, var, ALU.subtract))
        t4 = S.emit("vector", lambda e: e.tensor_scalar_add(var, var, 1e-5))
        t5 = S.emit("scalar", lambda e: e.activation(rstd, var, AF.Sqrt), waits=[t4])
        t6 = S.emit("vector", lambda e: e.reciprocal(rstd, rstd), waits=[t5])
        t7 = S.emit("vector", lambda e: e.tensor_tensor(mur, mu, rstd, ALU.mult))
        t8 = S.emit("scalar", lambda e: e.copy(srow16[0:1, 0:RW], rstd), waits=[t6])
        t9 = S.emit("scalar", lambda e: e.copy(srow16[0:1, RW:2 * RW], mur),
                    waits=[t7])
        return t1, t8, t9

    prev_cons = []
    num_free = [pre[-1], pre[-1]]
    att_free = None
    pta_free = None
    afm_lastread = None
    mm_evac = None
    prev_R = None
    st_toks = []
    for tt in range(NT):
        t0 = tt * 512
        db = tt % 2
        qb = qsb[:, db * 4096:(db + 1) * 4096]
        xb = xsb[:, db * 4096:(db + 1) * 4096]
        ldw = (pre + prev_cons) if tt < 2 else prev_cons
        t_q = None
        for c in range(8):
            t_q = S.dma("sync", qb[:, c * 512:(c + 1) * 512],
                        qpi[c * 128:(c + 1) * 128, t0:t0 + 512],
                        waits=ldw if c == 0 else [], ch=f"ldq{db}")
        t_x = None
        for c in range(8):
            t_x = S.dma("sync", xb[:, c * 512:(c + 1) * 512],
                        x2i[c * 128:(c + 1) * 128, t0:t0 + 512], ch=f"ldq{db}")
        at_toks = []
        for h in range(16):
            hp, hr = h // 2, (h % 2) * 64
            for st4 in range(4):
                nb = (h * 4 + st4) % 2
                mmn = S.emit("tensor", lambda e, hp=hp, hr=hr, st4=st4, nb=nb,
                             qb=qb, h=h: e.matmul(
                    PS_NUM[nb][0:128, 0:65],
                    qb[hr:hr + 64, hp * 512 + st4 * 128: hp * 512 + (st4 + 1) * 128],
                    vk16[hr:hr + 64, h * 65:(h + 1) * 65],
                    start=True, stop=True),
                    waits=[t_q, num_free[nb]])
                tden = S.emit("vector", lambda e, nb=nb: e.tensor_scalar_add(
                    rcol[:, 0:1], PS_NUM[nb][0:128, 64:65], 1e-6), waits=[mmn])
                S.emit("vector", lambda e: e.tensor_copy(fb[:, 64:128],
                                                         fb[:, 0:64]))
                trc = S.emit("vector", lambda e: e.reciprocal(
                    rcol[:, 1:2], rcol[:, 0:1]))
                S.emit("vector", lambda e: e.tensor_copy(fb[:, 64:128],
                                                         fb[:, 0:64]))
                tat = S.emit("vector", lambda e, nb=nb: e.tensor_scalar_mul(
                    att[:, :], PS_NUM[nb][0:128, 0:64], rcol[:, 1:2]),
                    waits=[att_free])
                num_free[nb] = tat
                tot = S.emit("tensor", lambda e: e.transpose(
                    PT_A, att[:, :], id_sb[:, :]), waits=[tat, pta_free])
                toe = S.emit("scalar", lambda e, hp=hp, hr=hr, st4=st4: e.activation(
                    afm[hr:hr + 64, hp * 512 + st4 * 128: hp * 512 + (st4 + 1) * 128],
                    PT_A, AF.Copy), waits=[tot, afm_lastread])
                att_free = tot
                pta_free = toe
                at_toks.append(toe)
        cons = []
        ev_prev = mm_evac
        for fc in range(8):
            mm = None
            for kc in range(8):
                w = (at_toks + [ev_prev] + ([prev_R] if fc == 0 else [])) \
                    if kc == 0 else []
                mm = S.emit("tensor", lambda e, fc=fc, kc=kc: e.matmul(
                    PS_MM, w16[:, kc * 1024 + fc * 128: kc * 1024 + (fc + 1) * 128],
                    afm[:, kc * 512:(kc + 1) * 512],
                    start=(kc == 0), stop=(kc == 7)), waits=w)
            tu = S.emit("scalar", lambda e, fc=fc: e.activation(
                u16[:, fc * 512:(fc + 1) * 512], PS_MM, AF.Copy), waits=[mm])
            ts_ = S.emit("scalar", lambda e, fc=fc: e.activation(
                usq[:, fc * 512:(fc + 1) * 512], u16[:, fc * 512:(fc + 1) * 512],
                AF.Square))
            ev_prev = tu
            cons += [tu, ts_]
        mm_evac = ev_prev
        afm_lastread = mm
        tm = None
        for c in range(8):
            tm = S.emit("tensor", lambda e, c=c: e.matmul(
                PS_SUM, onesc_sb[:, :], u16[:, c * 512:(c + 1) * 512],
                start=(c == 0), stop=(c == 7)), waits=cons if c == 0 else [])
        for c in range(8):
            tm = S.emit("tensor", lambda e, c=c: e.matmul(
                PS_SQ, onesc_sb[:, :], usq[:, c * 512:(c + 1) * 512],
                start=(c == 0), stop=(c == 7)))
        t1, t8, t9 = ln_rows(S, tm)
        tR = S.emit("tensor", lambda e: e.matmul(PS_R, ones1_sb[:, :],
                                                 srow16[0:1, 0:512],
                                                 start=True, stop=True), waits=[t8])
        tM = S.emit("tensor", lambda e: e.matmul(PS_M, ones1_sb[:, :],
                                                 srow16[0:1, 512:1024],
                                                 start=True, stop=True),
                    waits=[t9, t1])
        x3toks = []
        for c in range(8):
            S.emit("vector", lambda e, c=c: e.tensor_tensor(
                x3b[:, c * 512:(c + 1) * 512], u16[:, c * 512:(c + 1) * 512],
                PS_R, ALU.mult), waits=[tR, t_x])
            S.emit("vector", lambda e, c=c: e.tensor_tensor(
                x3b[:, c * 512:(c + 1) * 512], x3b[:, c * 512:(c + 1) * 512],
                PS_M, ALU.subtract), waits=[tM])
            ta = S.emit("vector", lambda e, c=c, xb=xb: e.tensor_tensor(
                x3b[:, c * 512:(c + 1) * 512], x3b[:, c * 512:(c + 1) * 512],
                xb[:, c * 512:(c + 1) * 512], ALU.add))
            tb = S.emit("scalar", lambda e, c=c: e.activation(
                u16[:, c * 512:(c + 1) * 512], x3b[:, c * 512:(c + 1) * 512],
                AF.Copy), waits=[ta])
            tc_ = S.emit("scalar", lambda e, c=c: e.activation(
                usq[:, c * 512:(c + 1) * 512], u16[:, c * 512:(c + 1) * 512],
                AF.Square))
            x3toks += [ta, tb, tc_]
        tsx = None
        for c in range(8):
            tsx = S.dma("sync", x3o[c * 128:(c + 1) * 128, t0:t0 + 512],
                        x3b[:, c * 512:(c + 1) * 512],
                        waits=x3toks if c == 0 else [], ch=f"sx{db}")
        tm2 = None
        for c in range(8):
            tm2 = S.emit("tensor", lambda e, c=c: e.matmul(
                PS_SUM, onesc_sb[:, :], u16[:, c * 512:(c + 1) * 512],
                start=(c == 0), stop=(c == 7)), waits=x3toks if c == 0 else [])
        for c in range(8):
            tm2 = S.emit("tensor", lambda e, c=c: e.matmul(
                PS_SQ, onesc_sb[:, :], usq[:, c * 512:(c + 1) * 512],
                start=(c == 0), stop=(c == 7)))
        t1b, t8b, t9b = ln_rows(S, tm2)
        tR2 = S.emit("tensor", lambda e: e.matmul(PS_R, ones1_sb[:, :],
                                                  srow16[0:1, 0:512],
                                                  start=True, stop=True),
                     waits=[t8b])
        tM2 = S.emit("tensor", lambda e: e.matmul(PS_M, ones1_sb[:, :],
                                                  srow16[0:1, 512:1024],
                                                  start=True, stop=True),
                     waits=[t9b, t1b])
        z8toks = []
        for c in range(8):
            S.emit("vector", lambda e, c=c: e.tensor_tensor(
                z8b[:, c * 512:(c + 1) * 512], u16[:, c * 512:(c + 1) * 512],
                PS_R, ALU.mult), waits=[tR2])
            tzb = S.emit("vector", lambda e, c=c: e.tensor_tensor(
                z8b[:, c * 512:(c + 1) * 512], z8b[:, c * 512:(c + 1) * 512],
                PS_M, ALU.subtract), waits=[tM2])
            z8toks.append(tzb)
        tsz = None
        for c in range(8):
            tsz = S.dma("sync", z8o[c * 128:(c + 1) * 128, t0:t0 + 512],
                        z8b[:, c * 512:(c + 1) * 512],
                        waits=z8toks if c == 0 else [], ch=f"sz{db}")
        st_toks = [tsx, tsz]
        prev_R = z8toks[-1]
        prev_cons = cons + x3toks + z8toks
    S.emit("gpsimd", lambda e: e.engine_nop(), waits=st_toks, inc=0)
    S.finalize()
    return nc


def build_l1f():
    nc = bass.Bass("TRN2", target_bir_lowering=False,
                   detect_race_conditions=False)
    TTOK = 256
    NT = TC // TTOK
    z8i = nc.dram_tensor("z8i", [D, TC], F32, kind="ExternalInput").ap()
    x3i = nc.dram_tensor("x3i", [D, TC], F32, kind="ExternalInput").ap()
    gwT = nc.dram_tensor("gwT", [D, 8], F32, kind="ExternalInput").ap()
    ewT = nc.dram_tensor("ewT", [8 * D, D], F32, kind="ExternalInput").ap()
    ones1 = nc.dram_tensor("ones1", [1, 128], BF16, kind="ExternalInput").ap()
    onesc = nc.dram_tensor("onesc", [128, 1], BF16, kind="ExternalInput").ap()
    ident = nc.dram_tensor("ident", [128, 128], BF16, kind="ExternalInput").ap()
    outo = nc.dram_tensor("outo", [D, TC], F32, kind="ExternalOutput").ap()

    ew16 = nc.alloc_sbuf_tensor("ew16", [128, 8 * 8 * 1024], BF16)
    gw_sb = nc.alloc_sbuf_tensor("gw_sb", [128, 8 * 8], F32)
    ones1_sb = nc.alloc_sbuf_tensor("ones1_sb", [1, 128], BF16)
    onesc_sb = nc.alloc_sbuf_tensor("onesc_sb", [128, 1], BF16)
    id_sb = nc.alloc_sbuf_tensor("id_sb", [128, 128], BF16)
    zsb = nc.alloc_sbuf_tensor("zsb", [128, 8 * TTOK], F32)
    z16 = nc.alloc_sbuf_tensor("z16", [128, 8 * TTOK], BF16)
    xsb = nc.alloc_sbuf_tensor("xsb", [128, 8 * TTOK], F32)
    mrow = nc.alloc_sbuf_tensor("mrow", [1, 8 * TTOK], BF16)
    mbc16 = nc.alloc_sbuf_tensor("mbc16", [128, 8 * TTOK], BF16)
    mt = nc.alloc_sbuf_tensor("mt", [128, 8], F32)
    mxc = nc.alloc_sbuf_tensor("mxc", [128, 1], F32)
    id32 = nc.alloc_sbuf_tensor("id32", [128, 128], F32)
    fbg = nc.alloc_sbuf_tensor("fbg", [128, 128], F32)
    xe16 = nc.alloc_sbuf_tensor("xe16", [128, 8 * TTOK], BF16)
    u16 = nc.alloc_sbuf_tensor("u16", [128, 8 * TTOK], BF16)
    usq = nc.alloc_sbuf_tensor("usq", [128, 8 * TTOK], BF16)
    ob = nc.alloc_sbuf_tensor("ob", [128, 8 * TTOK], F32)
    strow = nc.alloc_sbuf_tensor("strow", [1, 5 * TTOK], F32)
    srow16 = nc.alloc_sbuf_tensor("srow16", [1, 2 * TTOK], BF16)

    ps = nc.alloc_psum_tensor("ps", [128, 4096], F32)
    PS_FC = [ps[:, fc * 512: fc * 512 + 256] for fc in range(8)]
    PS_G = ps[:, 256:264]
    PT_M = ps[0:1, 768:896]
    PS_ME = ps[:, 1280:1536]
    PS_SUM = ps[0:1, 1792:2048]
    PS_SQ = ps[0:1, 2304:2560]
    PS_R = ps[:, 2816:3072]
    PS_M = ps[:, 3328:3584]

    S = Sched(nc)
    for e_ in range(8):
        for c in range(8):
            S.dma("gpsimd", ew16[:, (e_ * 8 + c) * 1024:(e_ * 8 + c + 1) * 1024],
                  ewT[e_ * D + c * 128: e_ * D + (c + 1) * 128, :], ch="ldg")
    t_ldg = S.dma("gpsimd", id_sb[:, :], ident, ch="ldg")
    for c in range(8):
        S.dma("sync", gw_sb[:, c * 8:(c + 1) * 8],
              gwT[c * 128:(c + 1) * 128, :], ch="lds")
    S.dma("sync", ones1_sb[:, :], ones1, ch="lds")
    t_lds = S.dma("sync", onesc_sb[:, :], onesc, ch="lds")
    t_fb = S.emit("vector", lambda e: e.memset(fbg[:, :], 0.0))
    t_id = S.emit("scalar", lambda e: e.activation(id32[:, :], id_sb[:, :],
                                                   AF.Copy), waits=[t_ldg])
    pre = [t_ldg, t_lds, t_fb, t_id]

    def ln_rows(S, t_stats):
        RW = TTOK
        mu = strow[0:1, 0:RW]; ex2 = strow[0:1, RW:2 * RW]
        var = strow[0:1, 2 * RW:3 * RW]; rstd = strow[0:1, 3 * RW:4 * RW]
        mur = strow[0:1, 4 * RW:5 * RW]
        t1 = S.emit("vector", lambda e: e.tensor_scalar_mul(mu, PS_SUM, 1.0 / D),
                    waits=[t_stats])
        t1x = S.emit("vector", lambda e: e.tensor_scalar_mul(ex2, PS_SQ, 1.0 / D))
        S.emit("vector", lambda e: e.tensor_tensor(var, mu, mu, ALU.mult))
        S.emit("vector", lambda e: e.tensor_tensor(var, ex2, var, ALU.subtract))
        t4 = S.emit("vector", lambda e: e.tensor_scalar_add(var, var, 1e-5))
        t5 = S.emit("scalar", lambda e: e.activation(rstd, var, AF.Sqrt), waits=[t4])
        t6 = S.emit("vector", lambda e: e.reciprocal(rstd, rstd), waits=[t5])
        t7 = S.emit("vector", lambda e: e.tensor_tensor(mur, mu, rstd, ALU.mult))
        t8 = S.emit("scalar", lambda e: e.copy(srow16[0:1, 0:RW], rstd), waits=[t6])
        t9 = S.emit("scalar", lambda e: e.copy(srow16[0:1, RW:2 * RW], mur),
                    waits=[t7])
        return t1x, t8, t9

    prev_cons = []
    g_free = None
    me_free = None
    mt_free = None
    ptm_free = None
    mrow_lastread = None
    mbc_lastread = None
    xe_lastread = None
    fc_free = []
    st_tok = None
    prev_R = None
    for tt in range(NT):
        t0 = tt * TTOK
        ldw = (pre + prev_cons) if tt == 0 else prev_cons
        t_z = None
        for c in range(8):
            t_z = S.dma("sync", zsb[:, c * TTOK:(c + 1) * TTOK],
                        z8i[c * 128:(c + 1) * 128, t0:t0 + TTOK],
                        waits=ldw if c == 0 else [], ch="ldz")
        t_x = None
        for c in range(8):
            t_x = S.dma("sync", xsb[:, c * TTOK:(c + 1) * TTOK],
                        x3i[c * 128:(c + 1) * 128, t0:t0 + TTOK], ch="ldz")
        zc_toks = []
        for c in range(8):
            tz16 = S.emit("scalar", lambda e, c=c: e.activation(
                z16[:, c * TTOK:(c + 1) * TTOK], zsb[:, c * TTOK:(c + 1) * TTOK],
                AF.Copy), waits=[t_z, xe_lastread])
            zc_toks.append(tz16)
        mrow_toks = []
        for st4 in range(TTOK // 128):
            mmg = None
            for kc in range(8):
                w = ([t_z, g_free]) if kc == 0 else []
                mmg = S.emit("tensor", lambda e, st4=st4, kc=kc: e.matmul(
                    PS_G, zsb[:, kc * TTOK + st4 * 128: kc * TTOK + st4 * 128 + 128],
                    gw_sb[:, kc * 8:(kc + 1) * 8],
                    start=(kc == 0), stop=(kc == 7)), waits=w)
            tmx = S.emit("vector", lambda e: e.reduce_max(
                mxc[:, 0:1], PS_G, axis=mybir.AxisListType.X), waits=[mmg, mt_free])
            S.emit("vector", lambda e: e.tensor_copy(fbg[:, 64:128],
                                                     fbg[:, 0:64]))
            tge = S.emit("vector", lambda e: e.tensor_scalar(
                mt[:, :], PS_G, mxc[:, 0:1], None, ALU.is_ge))
            g_free = tge
            for e_ in range(8):
                ttr = S.emit("tensor", lambda e, e_=e_: e.transpose(
                    PT_M, mt[:, e_:e_ + 1], id32[:, :]),
                    waits=[tge, ptm_free])
                mt_free = ttr
                tme = S.emit("scalar", lambda e, e_=e_, st4=st4: e.activation(
                    mrow[0:1, e_ * TTOK + st4 * 128: e_ * TTOK + (st4 + 1) * 128],
                    PT_M, AF.Copy), waits=[ttr, mrow_lastread])
                ptm_free = tme
                mrow_toks.append(tme)
        mbc_toks = []
        tmb_prev = None
        for e_ in range(8):
            tmb = S.emit("tensor", lambda e, e_=e_: e.matmul(
                PS_ME, ones1_sb[:, :], mrow[0:1, e_ * TTOK:(e_ + 1) * TTOK],
                start=True, stop=True),
                waits=(mrow_toks + [me_free]) if e_ == 0 else [me_free])
            tmc = S.emit("vector", lambda e, e_=e_: e.tensor_copy(
                mbc16[:, e_ * TTOK:(e_ + 1) * TTOK], PS_ME),
                waits=[tmb, mbc_lastread])
            me_free = tmc
            mbc_toks.append(tmc)
        mrow_lastread = tmb
        u_toks = []
        mm = None
        for e_ in range(8):
            xet = []
            for kc in range(8):
                tx = S.emit("vector", lambda e, e_=e_, kc=kc: e.tensor_tensor(
                    xe16[:, kc * TTOK:(kc + 1) * TTOK],
                    z16[:, kc * TTOK:(kc + 1) * TTOK],
                    mbc16[:, e_ * TTOK:(e_ + 1) * TTOK], ALU.mult),
                    waits=([mbc_toks[e_], xe_lastread] + zc_toks) if kc == 0
                    else [])
                xet.append(tx)
            for kc in range(8):
                for fc in range(8):
                    w = []
                    if kc == 0 and fc == 0:
                        w = xet + (fc_free if e_ == 0 else [])
                    mm = S.emit("tensor", lambda e, fc=fc, kc=kc, e_=e_: e.matmul(
                        PS_FC[fc],
                        ew16[:, (e_ * 8 + kc) * 1024 + fc * 128:
                             (e_ * 8 + kc) * 1024 + (fc + 1) * 128],
                        xe16[:, kc * TTOK:(kc + 1) * TTOK],
                        start=(e_ == 0 and kc == 0), stop=(e_ == 7 and kc == 7)),
                        waits=w)
            xe_lastread = mm
        fc_free = []
        for fc in range(8):
            tu = S.emit("vector", lambda e, fc=fc: e.tensor_tensor(
                u16[:, fc * TTOK:(fc + 1) * TTOK], PS_FC[fc],
                zsb[:, fc * TTOK:(fc + 1) * TTOK], ALU.add), waits=[mm])
            tc_ = S.emit("scalar", lambda e, fc=fc: e.activation(
                usq[:, fc * TTOK:(fc + 1) * TTOK], u16[:, fc * TTOK:(fc + 1) * TTOK],
                AF.Square), waits=[tu])
            u_toks += [tu, tc_]
            fc_free.append(tu)
        tm = None
        for c in range(8):
            tm = S.emit("tensor", lambda e, c=c: e.matmul(
                PS_SUM, onesc_sb[:, :], u16[:, c * TTOK:(c + 1) * TTOK],
                start=(c == 0), stop=(c == 7)),
                waits=(u_toks + [prev_R]) if c == 0 else [])
        for c in range(8):
            tm = S.emit("tensor", lambda e, c=c: e.matmul(
                PS_SQ, onesc_sb[:, :], usq[:, c * TTOK:(c + 1) * TTOK],
                start=(c == 0), stop=(c == 7)))
        t1x, t8, t9 = ln_rows(S, tm)
        tR = S.emit("tensor", lambda e: e.matmul(PS_R, ones1_sb[:, :],
                                                 srow16[0:1, 0:TTOK],
                                                 start=True, stop=True), waits=[t8])
        tM = S.emit("tensor", lambda e: e.matmul(PS_M, ones1_sb[:, :],
                                                 srow16[0:1, TTOK:2 * TTOK],
                                                 start=True, stop=True),
                    waits=[t9, t1x])
        otoks = []
        for c in range(8):
            S.emit("vector", lambda e, c=c: e.tensor_tensor(
                ob[:, c * TTOK:(c + 1) * TTOK], u16[:, c * TTOK:(c + 1) * TTOK],
                PS_R, ALU.mult), waits=[tR, t_x, st_tok])
            S.emit("vector", lambda e, c=c: e.tensor_tensor(
                ob[:, c * TTOK:(c + 1) * TTOK], ob[:, c * TTOK:(c + 1) * TTOK],
                PS_M, ALU.subtract), waits=[tM])
            ta = S.emit("vector", lambda e, c=c: e.tensor_tensor(
                ob[:, c * TTOK:(c + 1) * TTOK], ob[:, c * TTOK:(c + 1) * TTOK],
                xsb[:, c * TTOK:(c + 1) * TTOK], ALU.add))
            otoks.append(ta)
        for c in range(8):
            st_tok = S.dma("sync", outo[c * 128:(c + 1) * 128, t0:t0 + TTOK],
                           ob[:, c * TTOK:(c + 1) * TTOK],
                           waits=otoks if c == 0 else [], ch="sto")
        prev_R = otoks[-1]
        prev_cons = u_toks + otoks + zc_toks + [t1x]
    S.emit("gpsimd", lambda e: e.engine_nop(), waits=[st_tok], inc=0)
    S.finalize()
    return nc




def _l0f_post_host(inputs, x1_cores, z6_cores, consts):
    in_maps = []
    for cid in range(NCORE):
        in_maps.append({
            "x1i": x1_cores[cid], "z6i": z6_cores[cid],
            "ones1": consts["ones1"], "onesc": consts["onesc"],
        })
    return in_maps


def _l1a_qkv_host(inputs, z7_cores, consts):
    Wq = inputs["L1_attn_Wq"]; Wk = inputs["L1_attn_Wk"]; Wv = inputs["L1_attn_Wv"]
    perm = np.concatenate([np.arange(h * 64, (h + 1) * 64).reshape(32, 2).T.reshape(64)
                           for h in range(16)])
    perm = np.concatenate([np.concatenate([np.arange(h * 64, (h + 1) * 64, 2),
                                           np.arange(h * 64 + 1, (h + 1) * 64, 2)])
                           for h in range(16)])
    Wqp = np.ascontiguousarray(Wq[:, perm])
    Wkp = np.ascontiguousarray(Wk[:, perm])
    m = np.arange(128) % 32
    inv_freq = 1.0 / 10000.0 ** (2 * np.arange(32, dtype=np.float64) / 64)
    sign = np.where((np.arange(128) % 64) < 32, -1.0, 1.0)[:, None]
    tabs = []
    for hh in range(2):
        t = np.arange(hh * TC, (hh + 1) * TC, dtype=np.float64)
        ang = np.outer(inv_freq[m], t)
        cs = np.cos(ang)
        sn = np.sin(ang) * sign
        tabs.append(np.ascontiguousarray(
            np.concatenate([cs, sn], axis=1).astype(np.float32)))
    in_maps = []
    for cid in range(NCORE):
        h = cid % 2
        in_maps.append({
            "z7i": z7_cores[cid], "wq": Wqp, "wk": Wkp,
            "wv": np.ascontiguousarray(Wv),
            "cst": tabs[h], "ident": consts["ident"],
        })
    return in_maps


def _l1a_fin_host(inputs, qp_cores, vkt_cores, x2_cores, consts):
    Wo_ = np.ascontiguousarray(inputs["L1_attn_Wo"])
    in_maps = []
    for cid in range(NCORE):
        b = cid // 2
        vk2 = np.concatenate([vkt_cores[2 * b], vkt_cores[2 * b + 1]], axis=0)
        in_maps.append({
            "qpi": qp_cores[cid], "vk2": np.ascontiguousarray(vk2),
            "x2i": x2_cores[cid], "owT": Wo_,
            "ones1": consts["ones1"], "onesc": consts["onesc"],
            "ident": consts["ident"],
        })
    return in_maps


def _l1f_host(inputs, z8_cores, x3_cores, consts):
    gwT = np.ascontiguousarray(inputs["L1_ffn_gate_w"].T)
    ewT = np.ascontiguousarray(
        inputs["L1_ffn_expert_w"].transpose(0, 2, 1).reshape(8 * D, D))
    in_maps = []
    for cid in range(NCORE):
        in_maps.append({
            "z8i": z8_cores[cid], "x3i": x3_cores[cid],
            "gwT": gwT, "ewT": ewT,
            "ones1": consts["ones1"], "onesc": consts["onesc"],
            "ident": consts["ident"],
        })
    return in_maps


_NC_CACHE = {}


def _get_nc(name):
    if name not in _NC_CACHE:
        _NC_CACHE[name] = {
            "l0a_scan": build_l0a_scan, "l0a_out": build_l0a_out,
            "l0f_layer": build_l0f_layer, "l0f_post": build_l0f_post,
            "l1a_qkv": build_l1a_qkv, "l1a_fin": build_l1a_fin,
            "l1f": build_l1f,
        }[name]()
    return _NC_CACHE[name]


def kernel(**inputs):
    consts = _l0f_host_consts()
    nc = _get_nc("l0a_scan")
    res = run_bass_kernel_spmd(nc, _l0a_scan_host(inputs),
                               core_ids=list(range(NCORE)))
    ycore = [np.asarray(res.results[c]["y16"]) for c in range(NCORE)]
    y_even = [ycore[2 * b] for b in range(4)]
    y_odd = [ycore[2 * b + 1] for b in range(4)]
    nc = _get_nc("l0a_out")
    res = run_bass_kernel_spmd(nc, _l0a_out_host(inputs, y_even, y_odd),
                               core_ids=list(range(NCORE)))
    x1_cores = [np.asarray(res.results[c]["x1o"]) for c in range(NCORE)]
    z_cores = [np.asarray(res.results[c]["z0o"]) for c in range(NCORE)]
    nc = _get_nc("l0f_layer")
    for layer in range(6):
        z_fm = [np.concatenate([z_cores[2 * b], z_cores[2 * b + 1]], axis=1)
                for b in range(4)]
        res = run_bass_kernel_spmd(nc, _l0f_host(inputs, layer, z_fm, consts),
                                   core_ids=list(range(NCORE)))
        z_cores = [np.asarray(res.results[c]["zout"]) for c in range(NCORE)]
    nc = _get_nc("l0f_post")
    res = run_bass_kernel_spmd(nc, _l0f_post_host(inputs, x1_cores, z_cores,
                                                  consts),
                               core_ids=list(range(NCORE)))
    x2_cores = [np.asarray(res.results[c]["x2o"]) for c in range(NCORE)]
    z7_cores = [np.asarray(res.results[c]["z7o"]) for c in range(NCORE)]
    nc = _get_nc("l1a_qkv")
    res = run_bass_kernel_spmd(nc, _l1a_qkv_host(inputs, z7_cores, consts),
                               core_ids=list(range(NCORE)))
    qp_cores = [np.asarray(res.results[c]["qpo"]) for c in range(NCORE)]
    vkt_cores = [np.asarray(res.results[c]["vkto"]) for c in range(NCORE)]
    nc = _get_nc("l1a_fin")
    res = run_bass_kernel_spmd(nc, _l1a_fin_host(inputs, qp_cores, vkt_cores,
                                                 x2_cores, consts),
                               core_ids=list(range(NCORE)))
    x3_cores = [np.asarray(res.results[c]["x3o"]) for c in range(NCORE)]
    z8_cores = [np.asarray(res.results[c]["z8o"]) for c in range(NCORE)]
    nc = _get_nc("l1f")
    res = run_bass_kernel_spmd(nc, _l1f_host(inputs, z8_cores, x3_cores,
                                             consts),
                               core_ids=list(range(NCORE)))
    out = np.zeros((BATCH, SEQ, D), np.float32)
    for cid in range(NCORE):
        b, h = cid // 2, cid % 2
        out[b, h * TC:(h + 1) * TC, :] = np.asarray(
            res.results[cid]["outo"]).T
    return out



# version 02
# speedup vs baseline: 1.2842x; 1.2842x over previous
import math
import numpy as np
import ml_dtypes
import concourse.bass as bass
import concourse.mybir as mybir
from concourse.bass import broadcast_tensor_aps
from concourse.bass_utils import run_bass_kernel_spmd

F32 = mybir.dt.float32
BF16 = mybir.dt.bfloat16
AF = mybir.ActivationFunctionType
ALU = mybir.AluOpType
BF = ml_dtypes.bfloat16

BATCH, SEQ, D = 4, 8192, 1024
DEBUG_L0A = False
DEBUG_L0F = False
NCORE = 8
TC = SEQ // 2


class Sched:

    ENGINES = ("sync", "vector", "scalar", "tensor", "gpsimd")

    def __init__(self, nc):
        self.nc = nc
        self.ops = {e: [] for e in self.ENGINES}
        self.counts = {e: 0 for e in self.ENGINES}
        self.sems = {e: nc.alloc_semaphore(f"sem_{e}") for e in self.ENGINES}

    def _sem(self, name):
        if name not in self.sems:
            self.sems[name] = self.nc.alloc_semaphore(f"sem_{name}")
            self.counts[name] = 0
        return self.sems[name]

    def emit(self, eng, fn, waits=(), inc=1, ch=None):
        ws = tuple(w for w in waits if w is not None)
        sem = ch if ch is not None else eng
        self._sem(sem)
        self.ops[eng].append((fn, ws, inc, sem))
        self.counts[sem] += inc
        return (sem, self.counts[sem])

    def dma(self, eng, out, in_, waits=(), ch=None, slow=False):
        assert ch is not None, "DMA needs a dedicated channel"
        return self.emit(eng, lambda e, o=out, i=in_, s=slow: e.dma_start(
            out=o, in_=i, allow_slow_non_contiguous=s), waits, inc=16, ch=ch)

    def finalize(self):
        nc = self.nc
        with nc.Block() as blk:
            for ename in self.ENGINES:
                if not self.ops[ename]:
                    continue
                def mk(ename):
                    def body(eng):
                        hiwater = {}
                        for fn, waits, inc, sem in self.ops[ename]:
                            for (we, wv) in waits:
                                if we == ename:
                                    continue
                                if hiwater.get(we, -1) >= wv:
                                    continue
                                eng.wait_ge(self.sems[we], wv)
                                hiwater[we] = wv
                            inst = fn(eng)
                            if inst is not None and inc:
                                inst.then_inc(self.sems[sem], inc)
                    return body
                getattr(blk, ename)(mk(ename))


def build_l0a_scan():
    nc = bass.Bass("TRN2", target_bir_lowering=False,
                   detect_race_conditions=False)
    HL = 8
    SC = 1024
    NSC = SEQ // SC
    TT = 512
    NTT = SC // TT

    xT = nc.dram_tensor("xT", [D, SEQ], F32, kind="ExternalInput").ap()
    wqkvT = nc.dram_tensor("wqkvT", [D, 1536], F32, kind="ExternalInput").ap()
    bqkv = nc.dram_tensor("bqkv", [128, 12], F32, kind="ExternalInput").ap()
    lam_t = nc.dram_tensor("lam_t", [128, 256], F32, kind="ExternalInput").ap()
    gam_t = nc.dram_tensor("gam_t", [128, 256], F32, kind="ExternalInput").ap()
    wsT = nc.dram_tensor("wsT", [64, 512], F32, kind="ExternalInput").ap()
    perm = nc.dram_tensor("perm", [128, 64 * 128], BF16, kind="ExternalInput").ap()
    e2 = nc.dram_tensor("e2", [128, 32 * 64], BF16, kind="ExternalInput").ap()
    ones1 = nc.dram_tensor("ones1", [1, 128], BF16, kind="ExternalInput").ap()
    onesc = nc.dram_tensor("onesc", [128, 1], BF16, kind="ExternalInput").ap()
    y16 = nc.dram_tensor("y16", [512, SEQ], BF16, kind="ExternalOutput").ap()
    if DEBUG_L0A:
        dbg_qd = nc.dram_tensor("dbg_qd", [128, 8 * 1024], BF16, kind="ExternalOutput").ap()
        dbg_kd = nc.dram_tensor("dbg_kd", [128, 8 * 1024], BF16, kind="ExternalOutput").ap()
        dbg_v = nc.dram_tensor("dbg_v", [128, 4 * 1024], BF16, kind="ExternalOutput").ap()
        dbg_carry = nc.dram_tensor("dbg_carry", [128, 256], F32, kind="ExternalOutput").ap()
        dbg_rr = nc.dram_tensor("dbg_rr", [128, 256], F32, kind="ExternalOutput").ap()
        dbg_g1 = nc.dram_tensor("dbg_g1", [128, 2 * 1024], BF16, kind="ExternalOutput").ap()
        dbg_g2 = nc.dram_tensor("dbg_g2", [128, 2 * 1024], F32, kind="ExternalOutput").ap()
        dbg_s = nc.dram_tensor("dbg_s", [128, 2 * 1024], F32, kind="ExternalOutput").ap()
        dbg_z = nc.dram_tensor("dbg_z", [128, 2 * 1024], BF16, kind="ExternalOutput").ap()

    w16 = nc.alloc_sbuf_tensor("w16", [128, 8 * 1536], BF16)
    bq_sb = nc.alloc_sbuf_tensor("bq_sb", [128, 12], F32)
    lam_sb = nc.alloc_sbuf_tensor("lam_sb", [128, 256], F32)
    rr_sb = nc.alloc_sbuf_tensor("rr_sb", [128, 256], F32)
    gam_sb = nc.alloc_sbuf_tensor("gam_sb", [128, 256], F32)
    ws16 = nc.alloc_sbuf_tensor("ws16", [64, 512], BF16)
    perm_sb = nc.alloc_sbuf_tensor("perm_sb", [128, 64 * 128], BF16)
    e2_sb = nc.alloc_sbuf_tensor("e2_sb", [128, 32 * 64], BF16)
    ones1_sb = nc.alloc_sbuf_tensor("ones1_sb", [1, 128], BF16)
    onesc_sb = nc.alloc_sbuf_tensor("onesc_sb", [128, 1], BF16)

    x16 = nc.alloc_sbuf_tensor("x16", [128, 8 * TT], BF16)
    xtmp = nc.alloc_sbuf_tensor("xtmp", [128, 8 * TT], BF16)
    strow = nc.alloc_sbuf_tensor("strow", [1, 5 * TT], F32)
    srow16 = nc.alloc_sbuf_tensor("srow16", [1, 2 * TT], BF16)

    qd = nc.alloc_sbuf_tensor("qd", [128, HL * SC], BF16)
    kd = nc.alloc_sbuf_tensor("kd", [128, HL * SC], BF16)
    v16 = nc.alloc_sbuf_tensor("v16", [128, 4 * SC], BF16)
    g1 = nc.alloc_sbuf_tensor("g1", [128, 2 * SC], BF16)
    g2 = nc.alloc_sbuf_tensor("g2", [128, 2 * SC], F32)
    s_t = nc.alloc_sbuf_tensor("s_t", [128, 2 * SC], F32)
    z_t = nc.alloc_sbuf_tensor("z_t", [128, 2 * SC], BF16)
    qdf = nc.alloc_sbuf_tensor("qdf", [128, 2 * SC], F32)
    carry = nc.alloc_sbuf_tensor("carry", [128, 256], F32)
    ystage = nc.alloc_sbuf_tensor("ystage", [64, 2 * SC], BF16)

    ps = nc.alloc_psum_tensor("ps", [128, 4096], F32)

    S = Sched(nc)

    for c in range(8):
        S.dma("gpsimd", w16[:, c * 1536:(c + 1) * 1536],
              wqkvT[c * 128:(c + 1) * 128, :], ch="ldg")
    S.dma("sync", bq_sb[:, :], bqkv, ch="lds")
    S.dma("sync", lam_sb[:, :], lam_t, ch="lds")
    S.dma("sync", gam_sb[:, :], gam_t, ch="lds")
    t_ldg = S.dma("gpsimd", ws16[:, :], wsT, ch="ldg")
    S.dma("sync", perm_sb[:, :], perm, ch="lds")
    S.dma("sync", e2_sb[:, :], e2, ch="lds")
    S.dma("sync", ones1_sb[:, :], ones1, ch="lds")
    t_lds = S.dma("sync", onesc_sb[:, :], onesc, ch="lds")
    t_rr = S.emit("scalar", lambda e: e.activation(rr_sb[:, :], lam_sb[:, :],
                                                   AF.Sigmoid, scale=-1.0),
                  waits=[t_lds])
    pre = [t_ldg, t_lds, t_rr]

    PS_QKV = ps[:, 3072:3584]
    PS_SUM = ps[0:1, 3584:4096]
    PS_SQ = ps[0:1, 0:512]
    PS_R = ps[:, 0:512]
    PS_M = ps[:, 512:1024]
    PS_VD = [ps[:, 0:SC], ps[:, SC:2 * SC]]
    NSP = SC // 512
    PS_Y = [ps[0:64, 2048 + s * 512: 2048 + (s + 1) * 512] for s in range(NSP)]

    prev_gamma_last = None
    prev_tg2_last = None
    prev_tz_last = None
    prev_vd_last = None
    prev_sq_last = None
    prev_stats = None
    prev_xn = []
    prev_mm_last = None
    ye_prev = None
    prev_oc_evacs = []
    qdf_last_reader = [None, None]
    pending_tcar = None
    ucount = 0
    tg2 = tz = tsc = tg1 = tvd = None
    tg2_hist = {}
    tz_hist = {}
    tsc_hist = {}
    tg1_hist = {}
    emm_hist = {}
    tlast = None
    st_half = [None, None]
    st_tok = None

    for sc in range(NSC):
        sc_evacs = []
        for tt in range(NTT):
            t0 = sc * SC + tt * TT
            ld_waits = list(pre) if (sc == 0 and tt == 0) else \
                ([prev_sq_last, prev_stats, prev_mm_last] + prev_xn)
            t_x = None
            for c in range(8):
                t_x = S.dma("gpsimd", x16[:, c * TT:(c + 1) * TT],
                            xT[c * 128:(c + 1) * 128, t0:t0 + TT],
                            waits=ld_waits if c == 0 else [], ch="ldx")
            t_sq = []
            for c in range(8):
                t_sq.append(S.emit("scalar", lambda e, c=c: e.activation(
                    xtmp[:, c * TT:(c + 1) * TT], x16[:, c * TT:(c + 1) * TT],
                    AF.Square), waits=[t_x, prev_mm_last]))
            prev_sq_last = t_sq[-1]
            stat_waits = [t_x] + t_sq + [prev_gamma_last]
            tm = None
            for c in range(8):
                tm = S.emit("tensor", lambda e, c=c: e.matmul(
                    PS_SUM, onesc_sb[:, :], x16[:, c * TT:(c + 1) * TT],
                    start=(c == 0), stop=(c == 7)),
                    waits=stat_waits if c == 0 else [])
            for c in range(8):
                tm = S.emit("tensor", lambda e, c=c: e.matmul(
                    PS_SQ, onesc_sb[:, :], xtmp[:, c * TT:(c + 1) * TT],
                    start=(c == 0), stop=(c == 7)))
            t_stats = tm
            prev_stats = t_stats
            RW = TT
            mu = strow[0:1, 0:RW]; ex2 = strow[0:1, RW:2 * RW]
            var = strow[0:1, 2 * RW:3 * RW]; rstd = strow[0:1, 3 * RW:4 * RW]
            mur = strow[0:1, 4 * RW:5 * RW]
            t1 = S.emit("vector", lambda e: e.tensor_scalar_mul(mu, PS_SUM, 1.0 / D),
                        waits=[t_stats])
            S.emit("vector", lambda e: e.tensor_scalar_mul(ex2, PS_SQ, 1.0 / D))
            S.emit("vector", lambda e: e.tensor_tensor(var, mu, mu, ALU.mult))
            S.emit("vector", lambda e: e.tensor_tensor(var, ex2, var, ALU.subtract))
            t4 = S.emit("vector", lambda e: e.tensor_scalar_add(var, var, 1e-5))
            t5 = S.emit("scalar", lambda e: e.activation(rstd, var, AF.Sqrt),
                        waits=[t4])
            t6 = S.emit("vector", lambda e: e.reciprocal(rstd, rstd), waits=[t5])
            t7 = S.emit("vector", lambda e: e.tensor_tensor(mur, mu, rstd, ALU.mult))
            t8 = S.emit("scalar", lambda e: e.copy(srow16[0:1, 0:RW], rstd), waits=[t6])
            t9 = S.emit("scalar", lambda e: e.copy(srow16[0:1, RW:2 * RW], mur),
                        waits=[t7])
            tR = S.emit("tensor", lambda e: e.matmul(PS_R, ones1_sb[:, :],
                                                     srow16[0:1, 0:RW],
                                                     start=True, stop=True),
                        waits=[t8, prev_gamma_last])
            tM = S.emit("tensor", lambda e: e.matmul(PS_M, ones1_sb[:, :],
                                                     srow16[0:1, RW:2 * RW],
                                                     start=True, stop=True),
                        waits=[t9, t1, prev_gamma_last])
            t_xn = []
            for c in range(8):
                S.emit("vector", lambda e, c=c: e.tensor_tensor(
                    xtmp[:, c * TT:(c + 1) * TT], x16[:, c * TT:(c + 1) * TT],
                    PS_R, ALU.mult), waits=[tR])
                tb = S.emit("vector", lambda e, c=c: e.tensor_tensor(
                    xtmp[:, c * TT:(c + 1) * TT], xtmp[:, c * TT:(c + 1) * TT],
                    PS_M, ALU.subtract), waits=[tM])
                t_xn.append(tb)
            prev_xn = t_xn
            col = tt * TT
            mm_last = None
            prev_evac = list(prev_oc_evacs)
            for oc in range(12):
                for kc in range(8):
                    w = (t_xn + prev_evac) if kc == 0 else []
                    mm_last = S.emit("tensor", lambda e, oc=oc, kc=kc: e.matmul(
                        PS_QKV,
                        w16[:, kc * 1536 + oc * 128: kc * 1536 + (oc + 1) * 128],
                        xtmp[:, kc * TT:(kc + 1) * TT],
                        start=(kc == 0), stop=(kc == 7)), waits=w)
                if oc < 4:
                    hA, hB = 2 * oc, 2 * oc + 1
                    for (h, src0) in ((hA, 0), (hB, 64)):
                        for dst0 in (0, 64):
                            tq = S.emit("vector", lambda e, h=h, src0=src0, dst0=dst0,
                                        oc=oc, col=col: e.tensor_scalar_add(
                                qd[dst0:dst0 + 64, h * SC + col: h * SC + col + TT],
                                PS_QKV[src0:src0 + 64, :],
                                bq_sb[src0:src0 + 64, oc:oc + 1]),
                                waits=[mm_last, prev_tz_last])
                            sc_evacs.append(tq)
                elif oc < 8:
                    hA, hB = 2 * (oc - 4), 2 * (oc - 4) + 1
                    for (h, src0) in ((hA, 0), (hB, 64)):
                        for dst0 in (0, 64):
                            tk = S.emit("vector", lambda e, h=h, src0=src0, dst0=dst0,
                                        oc=oc, col=col: e.tensor_scalar(
                                kd[dst0:dst0 + 64, h * SC + col: h * SC + col + TT],
                                PS_QKV[src0:src0 + 64, :],
                                bq_sb[src0:src0 + 64, oc:oc + 1], 0.125,
                                ALU.add, ALU.mult),
                                waits=[mm_last, prev_tg2_last])
                            sc_evacs.append(tk)
                else:
                    vc = oc - 8
                    tv = S.emit("scalar", lambda e, vc=vc, oc=oc, col=col: e.activation(
                        v16[:, vc * SC + col: vc * SC + col + TT], PS_QKV,
                        AF.Identity, bias=bq_sb[:, oc:oc + 1]),
                        waits=[mm_last, prev_vd_last])
                    sc_evacs.append(tv)
                prev_evac = sc_evacs[-1:] if oc >= 8 else sc_evacs[-1:]
            prev_oc_evacs = prev_evac
            prev_mm_last = mm_last

        scan_waits = list(sc_evacs)
        for hl in range(HL):
            p2 = hl % 2
            qoc = hl // 2
            hslot = hl * 32
            qp = hl % 2
            tqdf = S.emit("scalar", lambda e, hl=hl, qp=qp: e.activation(
                qdf[:, qp * SC:(qp + 1) * SC], qd[:, hl * SC:(hl + 1) * SC],
                AF.Copy), waits=[qdf_last_reader[qp]] +
                (list(scan_waits) if hl == 0 else []))
            for s in range(NSP):
                w = list(scan_waits) if (hl == 0 and s == 0) else []
                if s == 0 and ye_prev is not None:
                    w.append(ye_prev)
                S.emit("tensor", lambda e, s=s, hl=hl: e.matmul(
                    PS_Y[s], ws16[:, hl * 64:(hl + 1) * 64],
                    qd[0:64, hl * SC + s * TT: hl * SC + (s + 1) * TT],
                    start=True, stop=False), waits=w)
            for u in range(32):
                pslot = (p2 * 32 + u) * 128
                half = ucount % 2
                wv = [tg1_hist.get(ucount - 2)]
                tvd = None
                for s4 in range(NSP):
                    tvd = S.emit("tensor", lambda e, s4=s4, pslot=pslot, qoc=qoc,
                                 half=half: e.matmul(
                        PS_VD[half][:, s4 * 512:(s4 + 1) * 512],
                        perm_sb[:, pslot:pslot + 128],
                        v16[:, qoc * SC + s4 * 512: qoc * SC + (s4 + 1) * 512],
                        start=True, stop=True), waits=wv if s4 == 0 else [])
                tg1 = S.emit("scalar", lambda e, hslot=hslot, u=u, half=half:
                             e.activation(
                    g1[:, half * SC:(half + 1) * SC], PS_VD[half], AF.Identity,
                    scale=gam_sb[:, hslot + u: hslot + u + 1]),
                    waits=[tvd, tg2_hist.get(ucount - 2)])
                tg1_hist[ucount] = tg1
                tg2 = S.emit("gpsimd", lambda e, hl=hl, half=half: e.tensor_tensor(
                    g2[:, half * SC:(half + 1) * SC], g1[:, half * SC:(half + 1) * SC],
                    kd[:, hl * SC:(hl + 1) * SC], ALU.mult),
                    waits=[tg1, tsc_hist.get(ucount - 2)])
                tg2_hist[ucount] = tg2
                rcol = rr_sb[:, hslot + u: hslot + u + 1]
                init = 0.0 if sc == 0 else carry[:, hslot + u: hslot + u + 1]
                def mk_scan(rcol=rcol, init=init, half=half):
                    def f(e):
                        g2h = g2[:, half * SC:(half + 1) * SC]
                        _, rrep = broadcast_tensor_aps(g2h, rcol)
                        return e.tensor_tensor_scan(
                            s_t[:, half * SC:(half + 1) * SC], rrep, g2h, init,
                            ALU.mult, ALU.add)
                    return f
                tsc = S.emit("vector", mk_scan(),
                             waits=[tg2, tz_hist.get(ucount - 2)])
                tsc_hist[ucount] = tsc
                tcar = None
                if pending_tcar is not None:
                    slot_p, half_p = pending_tcar
                    tcar = S.emit("vector", lambda e, slot_p=slot_p, half_p=half_p:
                                  e.tensor_copy(
                        carry[:, slot_p: slot_p + 1],
                        s_t[:, half_p * SC + SC - 1: half_p * SC + SC]))
                pending_tcar = (hslot + u, half)
                tz = S.emit("vector", lambda e, qp=qp, half=half: e.tensor_tensor(
                    z_t[:, half * SC:(half + 1) * SC],
                    s_t[:, half * SC:(half + 1) * SC],
                    qdf[:, qp * SC:(qp + 1) * SC], ALU.mult),
                    waits=[tqdf, emm_hist.get(ucount - 2)])
                tz_hist[ucount] = tz
                for s4 in range(NSP):
                    tlast = S.emit("tensor", lambda e, s4=s4, u=u, half=half: e.matmul(
                        PS_Y[s4], e2_sb[:, u * 64:(u + 1) * 64],
                        z_t[:, half * SC + s4 * 512: half * SC + (s4 + 1) * 512],
                        start=False, stop=(u == 31)), waits=[tz] if s4 == 0 else [])
                emm_hist[ucount] = tlast
                ucount += 1
            qdf_last_reader[qp] = tz
            ybuf = (sc * HL + hl) % 2
            ye = []
            for s4 in range(NSP):
                ye.append(S.emit("scalar", lambda e, s4=s4, ybuf=ybuf: e.activation(
                    ystage[:, ybuf * SC + s4 * 512: ybuf * SC + (s4 + 1) * 512],
                    PS_Y[s4], AF.Copy), waits=[tlast, st_half[ybuf]]))
            ye_prev = ye[-1]
            row0 = hl * 64
            st_tok = S.dma("sync", y16[row0:row0 + 64, sc * SC:(sc + 1) * SC],
                           ystage[:, ybuf * SC:(ybuf + 1) * SC], waits=ye,
                           ch=f"sty{ybuf}")
            st_half[ybuf] = st_tok
        prev_gamma_last = tg1
        prev_tg2_last = tg2
        prev_tz_last = tz
        prev_vd_last = tvd

    if pending_tcar is not None:
        slot_p, half_p = pending_tcar
        S.emit("vector", lambda e: e.tensor_copy(qdf[:, 0:64], s_t[:, 0:64]))
        tcar_f = S.emit("vector", lambda e, slot_p=slot_p, half_p=half_p:
                        e.tensor_copy(carry[:, slot_p: slot_p + 1],
                                      s_t[:, half_p * SC + SC - 1: half_p * SC + SC]))
    fin = [st_half[0], st_half[1]]
    if DEBUG_L0A:
        S.dma("sync", dbg_qd, qd[:, :], waits=[tz, st_tok], ch="dbg")
        S.dma("sync", dbg_kd, kd[:, :], waits=[], ch="dbg")
        S.dma("sync", dbg_v, v16[:, :], waits=[], ch="dbg")
        S.dma("sync", dbg_carry, carry[:, :], waits=[tcar_f], ch="dbg")
        S.dma("sync", dbg_rr, rr_sb[:, :], waits=[], ch="dbg")
        S.dma("sync", dbg_g1, g1[:, :], waits=[], ch="dbg")
        S.dma("sync", dbg_g2, g2[:, :], waits=[], ch="dbg")
        S.dma("sync", dbg_s, s_t[:, :], waits=[], ch="dbg")
        fin.append(S.dma("sync", dbg_z, z_t[:, :], waits=[], ch="dbg"))
    S.emit("gpsimd", lambda e: e.engine_nop(), waits=fin, inc=0)
    S.finalize()
    return nc


def _l0a_scan_host(inputs):
    x = inputs["L0_attn_x"]
    Wq = inputs["L0_attn_Wqkv_w"]
    bq = inputs["L0_attn_Wqkv_b"]
    lam = inputs["L0_attn_Lambda_raw"]
    gam = inputs["L0_attn_Gamma"]
    ws = inputs["L0_attn_W_static"]

    WqT = np.ascontiguousarray(Wq.T)
    perm = np.zeros((128, 64 * 128), dtype=BF)
    for p2 in range(2):
        for u in range(32):
            m = np.zeros((128, 128), dtype=np.float32)
            for p in range(128):
                r = p2 * 64 + 2 * u + p // 64
                m[r, p] = 1.0
            perm[:, (p2 * 32 + u) * 128:(p2 * 32 + u + 1) * 128] = m.astype(BF)
    e2m = np.zeros((128, 32 * 64), dtype=BF)
    for u in range(32):
        m = np.zeros((128, 64), dtype=np.float32)
        for p in range(128):
            m[p, 2 * u + p // 64] = 1.0
        e2m[:, u * 64:(u + 1) * 64] = m.astype(BF)
    ones1 = np.ones((1, 128), dtype=BF)
    onesc = np.ones((128, 1), dtype=BF)

    in_maps = []
    for cid in range(NCORE):
        b = cid // 2
        g = cid % 2
        H0 = 8 * g
        cols = np.concatenate([np.arange(H0 * 64, (H0 + 8) * 64),
                               1024 + np.arange(H0 * 64, (H0 + 8) * 64),
                               2048 + np.arange(H0 * 64, (H0 + 8) * 64)])
        wq_slice = np.ascontiguousarray(WqT[:, cols])
        bq_slice = np.ascontiguousarray(bq[cols].reshape(12, 128).T)
        lam_s = lam[H0:H0 + 8].reshape(8, 32, 128)
        lam_tiled = np.ascontiguousarray(lam_s.transpose(2, 0, 1).reshape(128, 256))
        gam_s = gam[H0:H0 + 8].reshape(8, 32, 128)
        gam_tiled = np.ascontiguousarray(gam_s.transpose(2, 0, 1).reshape(128, 256))
        wsT_s = np.ascontiguousarray(
            ws[H0:H0 + 8].transpose(0, 2, 1).transpose(1, 0, 2).reshape(64, 512))
        in_maps.append({
            "xT": np.ascontiguousarray(x[b].T),
            "wqkvT": wq_slice,
            "bqkv": bq_slice,
            "lam_t": lam_tiled,
            "gam_t": gam_tiled,
            "wsT": wsT_s,
            "perm": perm, "e2": e2m, "ones1": ones1, "onesc": onesc,
        })
    return in_maps


def run_l0a_scan(inputs):
    nc = build_l0a_scan()
    in_maps = _l0a_scan_host(inputs)
    res = run_bass_kernel_spmd(nc, in_maps, core_ids=list(range(NCORE)))
    y = np.zeros((BATCH, SEQ, D), dtype=np.float32)
    for cid in range(NCORE):
        b = cid // 2
        g = cid % 2
        yc = np.asarray(res.results[cid]["y16"], dtype=np.float32)
        y[b, :, g * 512:(g + 1) * 512] = yc.T
    return y




def build_l0a_out():
    nc = bass.Bass("TRN2", target_bir_lowering=False,
                   detect_race_conditions=False)
    NT = TC // 512

    ytop = nc.dram_tensor("ytop", [512, TC], BF16, kind="ExternalInput").ap()
    ybot = nc.dram_tensor("ybot", [512, TC], BF16, kind="ExternalInput").ap()
    xT2 = nc.dram_tensor("xT2", [D, TC], F32, kind="ExternalInput").ap()
    owT = nc.dram_tensor("owT", [D, D], F32, kind="ExternalInput").ap()
    obc = nc.dram_tensor("obc", [128, 8], F32, kind="ExternalInput").ap()
    ones1 = nc.dram_tensor("ones1", [1, 128], BF16, kind="ExternalInput").ap()
    onesc = nc.dram_tensor("onesc", [128, 1], BF16, kind="ExternalInput").ap()
    x1o = nc.dram_tensor("x1o", [D, TC], F32, kind="ExternalOutput").ap()
    z0o = nc.dram_tensor("z0o", [D, TC], F32, kind="ExternalOutput").ap()

    w16 = nc.alloc_sbuf_tensor("w16", [128, 8 * 1024], BF16)
    ob_sb = nc.alloc_sbuf_tensor("ob_sb", [128, 8], F32)
    ones1_sb = nc.alloc_sbuf_tensor("ones1_sb", [1, 128], BF16)
    onesc_sb = nc.alloc_sbuf_tensor("onesc_sb", [128, 1], BF16)
    ysb = nc.alloc_sbuf_tensor("ysb", [128, 2 * 8 * 512], BF16)
    xsb = nc.alloc_sbuf_tensor("xsb", [128, 2 * 8 * 512], F32)
    u16 = nc.alloc_sbuf_tensor("u16", [128, 8 * 512], BF16)
    usq = nc.alloc_sbuf_tensor("usq", [128, 8 * 512], BF16)
    x1b = nc.alloc_sbuf_tensor("x1b", [128, 8 * 512], F32)
    x1c = nc.alloc_sbuf_tensor("x1c", [128, 8 * 512], BF16)
    z0b = nc.alloc_sbuf_tensor("z0b", [128, 8 * 512], F32)
    strow = nc.alloc_sbuf_tensor("strow", [1, 5 * 512], F32)
    srow16 = nc.alloc_sbuf_tensor("srow16", [1, 2 * 512], BF16)

    ps = nc.alloc_psum_tensor("ps", [128, 4096], F32)
    PS_MM = ps[:, 0:512]
    PS_SUM = ps[0:1, 512:1024]
    PS_SQ = ps[0:1, 1024:1536]
    PS_R = ps[:, 1536:2048]
    PS_M = ps[:, 2048:2560]

    S = Sched(nc)
    for c in range(8):
        S.dma("gpsimd", w16[:, c * 1024:(c + 1) * 1024],
              owT[c * 128:(c + 1) * 128, :], ch="ldg")
    t_ldg = S.dma("gpsimd", ob_sb[:, :], obc, ch="ldg")
    S.dma("sync", ones1_sb[:, :], ones1, ch="lds")
    t_lds = S.dma("sync", onesc_sb[:, :], onesc, ch="lds")
    pre = [t_ldg, t_lds]

    def ln_rows(S, t_stats, eps=1e-5):
        RW = 512
        mu = strow[0:1, 0:RW]; ex2 = strow[0:1, RW:2 * RW]
        var = strow[0:1, 2 * RW:3 * RW]; rstd = strow[0:1, 3 * RW:4 * RW]
        mur = strow[0:1, 4 * RW:5 * RW]
        t1 = S.emit("vector", lambda e: e.tensor_scalar_mul(mu, PS_SUM, 1.0 / D),
                    waits=[t_stats])
        S.emit("vector", lambda e: e.tensor_scalar_mul(ex2, PS_SQ, 1.0 / D))
        S.emit("vector", lambda e: e.tensor_tensor(var, mu, mu, ALU.mult))
        S.emit("vector", lambda e: e.tensor_tensor(var, ex2, var, ALU.subtract))
        t4 = S.emit("vector", lambda e: e.tensor_scalar_add(var, var, eps))
        t5 = S.emit("scalar", lambda e: e.activation(rstd, var, AF.Sqrt), waits=[t4])
        t6 = S.emit("vector", lambda e: e.reciprocal(rstd, rstd), waits=[t5])
        t7 = S.emit("vector", lambda e: e.tensor_tensor(mur, mu, rstd, ALU.mult))
        t8 = S.emit("scalar", lambda e: e.copy(srow16[0:1, 0:RW], rstd), waits=[t6])
        t9 = S.emit("scalar", lambda e: e.copy(srow16[0:1, RW:2 * RW], mur),
                    waits=[t7])
        return t1, t8, t9

    prev_cons = []
    prev_R_readers = None
    prev_mm_evac = None
    st_toks = []
    for tt in range(NT):
        t0 = tt * 512
        db = tt % 2
        yb = ysb[:, db * 4096:(db + 1) * 4096]
        xb = xsb[:, db * 4096:(db + 1) * 4096]
        ldw = (pre + prev_cons) if tt < 2 else prev_cons
        t_y = None
        for c in range(4):
            t_y = S.dma("sync", yb[:, c * 512:(c + 1) * 512],
                        ytop[c * 128:(c + 1) * 128, t0:t0 + 512],
                        waits=ldw if c == 0 else [], ch=f"ldy{db}")
        for c in range(4):
            t_y = S.dma("sync", yb[:, (4 + c) * 512:(5 + c) * 512],
                        ybot[c * 128:(c + 1) * 128, t0:t0 + 512], ch=f"ldy{db}")
        t_x = None
        for c in range(8):
            t_x = S.dma("sync", xb[:, c * 512:(c + 1) * 512],
                        xT2[c * 128:(c + 1) * 128, t0:t0 + 512], ch=f"ldx{db}")
        cons = []
        ev_prev = prev_mm_evac
        for fc in range(8):
            mm = None
            for kc in range(8):
                w = ([t_y, ev_prev] + ([prev_R_readers] if fc == 0 else []))
                mm = S.emit("tensor", lambda e, fc=fc, kc=kc, yb=yb: e.matmul(
                    PS_MM, w16[:, kc * 1024 + fc * 128: kc * 1024 + (fc + 1) * 128],
                    yb[:, kc * 512:(kc + 1) * 512],
                    start=(kc == 0), stop=(kc == 7)), waits=w if kc == 0 else [])
            tu = S.emit("scalar", lambda e, fc=fc: e.activation(
                u16[:, fc * 512:(fc + 1) * 512], PS_MM, AF.Identity,
                bias=ob_sb[:, fc:fc + 1]), waits=[mm])
            ts_ = S.emit("scalar", lambda e, fc=fc: e.activation(
                usq[:, fc * 512:(fc + 1) * 512], u16[:, fc * 512:(fc + 1) * 512],
                AF.Square))
            ev_prev = tu
            cons += [tu, ts_]
        prev_mm_evac = ev_prev
        tm = None
        for c in range(8):
            tm = S.emit("tensor", lambda e, c=c: e.matmul(
                PS_SUM, onesc_sb[:, :], u16[:, c * 512:(c + 1) * 512],
                start=(c == 0), stop=(c == 7)), waits=cons if c == 0 else [])
        for c in range(8):
            tm = S.emit("tensor", lambda e, c=c: e.matmul(
                PS_SQ, onesc_sb[:, :], usq[:, c * 512:(c + 1) * 512],
                start=(c == 0), stop=(c == 7)))
        t1, t8, t9 = ln_rows(S, tm)
        tR = S.emit("tensor", lambda e: e.matmul(PS_R, ones1_sb[:, :],
                                                 srow16[0:1, 0:512],
                                                 start=True, stop=True), waits=[t8])
        tM = S.emit("tensor", lambda e: e.matmul(PS_M, ones1_sb[:, :],
                                                 srow16[0:1, 512:1024],
                                                 start=True, stop=True), waits=[t9])
        x1toks = []
        for c in range(8):
            S.emit("vector", lambda e, c=c: e.tensor_tensor(
                x1b[:, c * 512:(c + 1) * 512], u16[:, c * 512:(c + 1) * 512],
                PS_R, ALU.mult), waits=[tR, t_x])
            S.emit("vector", lambda e, c=c: e.tensor_tensor(
                x1b[:, c * 512:(c + 1) * 512], x1b[:, c * 512:(c + 1) * 512],
                PS_M, ALU.subtract), waits=[tM])
            ta = S.emit("vector", lambda e, c=c, db=db: e.tensor_tensor(
                x1b[:, c * 512:(c + 1) * 512], x1b[:, c * 512:(c + 1) * 512],
                xsb[:, db * 4096 + c * 512: db * 4096 + (c + 1) * 512], ALU.add))
            tb = S.emit("scalar", lambda e, c=c: e.activation(
                x1c[:, c * 512:(c + 1) * 512], x1b[:, c * 512:(c + 1) * 512],
                AF.Copy), waits=[ta])
            tc_ = S.emit("scalar", lambda e, c=c: e.activation(
                usq[:, c * 512:(c + 1) * 512], x1c[:, c * 512:(c + 1) * 512],
                AF.Square))
            x1toks += [ta, tb, tc_]
            cons += [ta, tb, tc_]
        tsx = None
        for c in range(8):
            tsx = S.dma("sync", x1o[c * 128:(c + 1) * 128, t0:t0 + 512],
                        x1b[:, c * 512:(c + 1) * 512],
                        waits=x1toks if c == 0 else [], ch=f"stx{db}")
        tm2 = None
        for c in range(8):
            tm2 = S.emit("tensor", lambda e, c=c: e.matmul(
                PS_SUM, onesc_sb[:, :], x1c[:, c * 512:(c + 1) * 512],
                start=(c == 0), stop=(c == 7)), waits=x1toks + [t1] if c == 0 else [])
        for c in range(8):
            tm2 = S.emit("tensor", lambda e, c=c: e.matmul(
                PS_SQ, onesc_sb[:, :], usq[:, c * 512:(c + 1) * 512],
                start=(c == 0), stop=(c == 7)))
        t1b, t8b, t9b = ln_rows(S, tm2)
        tR2 = S.emit("tensor", lambda e: e.matmul(PS_R, ones1_sb[:, :],
                                                  srow16[0:1, 0:512],
                                                  start=True, stop=True), waits=[t8b])
        tM2 = S.emit("tensor", lambda e: e.matmul(PS_M, ones1_sb[:, :],
                                                  srow16[0:1, 512:1024],
                                                  start=True, stop=True), waits=[t9b])
        z0toks = []
        for c in range(8):
            S.emit("vector", lambda e, c=c: e.tensor_tensor(
                z0b[:, c * 512:(c + 1) * 512], x1c[:, c * 512:(c + 1) * 512],
                PS_R, ALU.mult), waits=[tR2])
            tzb = S.emit("vector", lambda e, c=c: e.tensor_tensor(
                z0b[:, c * 512:(c + 1) * 512], z0b[:, c * 512:(c + 1) * 512],
                PS_M, ALU.subtract), waits=[tM2])
            z0toks.append(tzb)
            cons.append(tzb)
        tsz = None
        for c in range(8):
            tsz = S.dma("sync", z0o[c * 128:(c + 1) * 128, t0:t0 + 512],
                        z0b[:, c * 512:(c + 1) * 512],
                        waits=z0toks if c == 0 else [], ch=f"stz{db}")
        st_toks = [tsx, tsz]
        prev_R_readers = z0toks[-1]
        prev_cons = cons + z0toks + [tsx, tsz]
    S.emit("gpsimd", lambda e: e.engine_nop(), waits=st_toks, inc=0)
    S.finalize()
    return nc


def _l0a_out_host(inputs, y_even, y_odd):
    x = inputs["L0_attn_x"]
    ow = inputs["L0_attn_out_w"]
    ob = inputs["L0_attn_out_b"]
    owT = np.ascontiguousarray(ow.T)
    obc = np.ascontiguousarray(ob.reshape(8, 128).T)
    ones1 = np.ones((1, 128), dtype=BF)
    onesc = np.ones((128, 1), dtype=BF)
    in_maps = []
    for cid in range(NCORE):
        b = cid // 2
        h = cid % 2
        sl = slice(h * TC, (h + 1) * TC)
        in_maps.append({
            "ytop": np.ascontiguousarray(y_even[b][:, sl]),
            "ybot": np.ascontiguousarray(y_odd[b][:, sl]),
            "xT2": np.ascontiguousarray(x[b].T[:, sl]),
            "owT": owT, "obc": obc, "ones1": ones1, "onesc": onesc,
        })
    return in_maps




class _Arena:
    def __init__(self, nc, base, limit):
        self.nc, self.cur, self.limit = nc, base, limit

    def take(self, name, shape, dtype):
        t = self.nc.alloc_sbuf_tensor_at(name, shape, dtype, offset=self.cur)
        nbytes = shape[-1] * (4 if dtype == F32 else 2)
        self.cur += (nbytes + 31) // 32 * 32
        assert self.cur <= self.limit, f"SBUF arena overflow at {name}: {self.cur}"
        return t


def build_l0f_layer():
    nc = bass.Bass("TRN2", target_bir_lowering=False,
                   detect_race_conditions=False)
    NTA = 9
    NTB = 16
    KC2 = 22

    zin = nc.dram_tensor("zin", [D, TC], F32, kind="ExternalInput").ap()
    zhalo = nc.dram_tensor("zhalo", [D, 640], F32, kind="ExternalInput").ap()
    wqkv = nc.dram_tensor("wqkv", [D, 3072], F32, kind="ExternalInput").ap()
    wo = nc.dram_tensor("wo", [D, D], F32, kind="ExternalInput").ap()
    w1a = nc.dram_tensor("w1a", [D, 2730], F32, kind="ExternalInput").ap()
    w1g = nc.dram_tensor("w1g", [D, 2730], F32, kind="ExternalInput").ap()
    w2 = nc.dram_tensor("w2", [2730, D], F32, kind="ExternalInput").ap()
    qsc = nc.dram_tensor("qsc", [128, 1], F32, kind="ExternalInput").ap()
    ksc = nc.dram_tensor("ksc", [128, 1], F32, kind="ExternalInput").ap()
    b1ac = nc.dram_tensor("b1ac", [128, KC2], F32, kind="ExternalInput").ap()
    b1gc = nc.dram_tensor("b1gc", [128, KC2], F32, kind="ExternalInput").ap()
    lgc = nc.dram_tensor("lgc", [128, 8], F32, kind="ExternalInput").ap()
    lbc = nc.dram_tensor("lbc", [128, 8], F32, kind="ExternalInput").ap()
    b2c = nc.dram_tensor("b2c", [128, 8], F32, kind="ExternalInput").ap()
    cstab = nc.dram_tensor("cstab", [128, 4 * 4608], F32, kind="ExternalInput").ap()
    maskt = nc.dram_tensor("maskt", [128, 4 * 512], F32, kind="ExternalInput").ap()
    ones1 = nc.dram_tensor("ones1", [1, 128], BF16, kind="ExternalInput").ap()
    onesc = nc.dram_tensor("onesc", [128, 1], BF16, kind="ExternalInput").ap()
    e2t = nc.dram_tensor("e2t", [2, 128], BF16, kind="ExternalInput").ap()
    e2c = nc.dram_tensor("e2c", [128, 2], BF16, kind="ExternalInput").ap()
    ident = nc.dram_tensor("ident", [128, 128], BF16, kind="ExternalInput").ap()
    zout = nc.dram_tensor("zout", [D, TC], F32, kind="ExternalOutput").ap()
    x1s = nc.dram_tensor("x1s", [D, 4352], F32,
                         kind="ExternalOutput" if DEBUG_L0F else "Internal").ap()

    base = (nc.sbuf_base + 31) // 32 * 32
    LIM = base + 204 * 1024
    sh = _Arena(nc, base, LIM)
    mask_sb = sh.take("mask_sb", [128, 4 * 512], F32)
    bcols = sh.take("bcols", [128, 2 * KC2 + 26], F32)
    strow = sh.take("strow", [1, 5 * 256], F32)
    srow16 = sh.take("srow16", [1, 2 * 256], BF16)
    ones1_sb = sh.take("ones1_sb", [1, 128], BF16)
    onesc_sb = sh.take("onesc_sb", [128, 1], BF16)
    e2t_sb = sh.take("e2t_sb", [2, 128], BF16)
    e2c_sb = sh.take("e2c_sb", [128, 2], BF16)
    id_sb = sh.take("id_sb", [128, 128], BF16)
    nrow = sh.take("nrow", [2, 512], F32)
    nrow16 = sh.take("nrow16", [2, 512], BF16)
    mx = sh.take("mx", [128, 4], F32)
    rs = sh.take("rs", [128, 4], F32)
    shared_end = sh.cur

    pa = _Arena(nc, shared_end, LIM)
    wq16 = pa.take("wq16", [128, 8 * 3072], BF16)
    wo16 = pa.take("wo16", [128, 8 * 1024], BF16)
    cs16 = pa.take("cs16", [128, 4 * 512], F32)
    zf = pa.take("zf", [128, 8 * 513], F32)
    zs16 = pa.take("zs16", [128, 8 * 512], BF16)
    qsq = pa.take("qsq", [128, 512], BF16)
    q1t = pa.take("q1t", [128, 512], F32)
    qq = pa.take("qq", [128, 512], F32)
    qsw = pa.take("qsw", [128, 512], F32)
    qra = pa.take("qra", [128, 512], F32)
    kring = pa.take("kring", [128, 8 * 768], BF16)
    vring = pa.take("vring", [128, 8 * 768], BF16)
    qwin = pa.take("qwin", [128, 8 * 512], BF16)
    vt16 = pa.take("vt16", [128, 2 * 256], BF16)
    msc = pa.take("msc", [128, 2 * 512], F32)
    p16 = pa.take("p16", [128, 2 * 512], BF16)
    pt16 = pa.take("pt16", [128, 2 * 512], BF16)
    atm = pa.take("atm", [128, 2 * 64], BF16)
    afm = pa.take("afm", [128, 8 * 512], BF16)
    x1st = pa.take("x1st", [128, 8 * 512], F32)

    pb = _Arena(nc, shared_end, LIM)
    w1a16 = pb.take("w1a16", [128, 8 * 2730], BF16)
    w1g16 = pb.take("w1g16", [128, 8 * 2730], BF16)
    w216 = pb.take("w216", [128, KC2 * 1024], BF16)
    x1f = pb.take("x1f", [128, 8 * 257], F32)
    xv16 = pb.take("xv16", [128, 8 * 256], BF16)
    xsq16 = pb.take("xsq16", [128, 8 * 256], BF16)
    xn16 = pb.take("xn16", [128, 8 * 256], BF16)
    gg16 = pb.take("gg16", [128, KC2 * 256], BF16)
    ff16 = pb.take("ff16", [128, KC2 * 256], BF16)
    zst = pb.take("zst", [128, 8 * 256], F32)

    ps = nc.alloc_psum_tensor("ps", [128, 2560], F32)
    ps_t1a = nc.alloc_psum_tensor("ps_t1a", [128, 1024], BF16)
    ps_t1b = nc.alloc_psum_tensor("ps_t1b", [128, 1024], BF16)
    ps_t2 = nc.alloc_psum_tensor("ps_t2", [128, 1024], BF16)
    PS_ACC = ps[:, 0:512]
    PS_NS = ps[0:2, 512:1024]
    PS_RB = ps[:, 512:1024]
    PS_SC = [ps[:, 1024:1536], ps[:, 1536:2048]]
    PS_AV = [ps[0:128, 2048:2112], ps[0:128, 2048:2112]]
    PS_PT = [ps_t1a, ps_t1b]
    PS_B_G = [ps[:, 512:1024], ps[:, 1024:1536]]
    PS_B_A = [ps[:, 1536:2048], ps[:, 2048:2560]]
    PS_B_W2 = [ps[:, 512:1024], ps[:, 1024:1536]]
    PS_B_ST = ps[0:1, 512:1024]
    PS_B_R = ps[:, 0:256]
    PS_B_M = ps[:, 256:512]

    S = Sched(nc)
    for c in range(8):
        S.dma("gpsimd", wq16[:, c * 3072:(c + 1) * 3072],
              wqkv[c * 128:(c + 1) * 128, :], ch="ldg")
    for c in range(8):
        S.dma("gpsimd", wo16[:, c * 1024:(c + 1) * 1024],
              wo[c * 128:(c + 1) * 128, :], ch="ldg")
    t_ldg = S.dma("gpsimd", nrow16[0:2, 0:128], e2t, ch="dummy_unused") \
        if False else None
    S.dma("sync", mask_sb[:, :], maskt, ch="lds")
    S.dma("sync", bcols[:, 2 * KC2:2 * KC2 + 1], qsc, ch="lds")
    S.dma("sync", bcols[:, 2 * KC2 + 1:2 * KC2 + 2], ksc, ch="lds")
    S.dma("sync", bcols[:, 0:KC2], b1ac, ch="lds")
    S.dma("sync", bcols[:, KC2:2 * KC2], b1gc, ch="lds")
    S.dma("sync", bcols[:, 2 * KC2 + 2:2 * KC2 + 10], lgc, ch="lds")
    S.dma("sync", bcols[:, 2 * KC2 + 10:2 * KC2 + 18], lbc, ch="lds")
    S.dma("sync", bcols[:, 2 * KC2 + 18:2 * KC2 + 26], b2c, ch="lds")
    S.dma("sync", ones1_sb[:, :], ones1, ch="lds")
    S.dma("sync", onesc_sb[:, :], onesc, ch="lds")
    S.dma("sync", e2t_sb[:, :], e2t, ch="lds")
    S.dma("sync", e2c_sb[:, :], e2c, ch="lds")
    t_lds = S.dma("sync", id_sb[:, :], ident, ch="lds")
    QS = bcols[:, 2 * KC2:2 * KC2 + 1]
    KS = bcols[:, 2 * KC2 + 1:2 * KC2 + 2]
    B1A = bcols[:, 0:KC2]
    B1G = bcols[:, KC2:2 * KC2]
    LG = bcols[:, 2 * KC2 + 2:2 * KC2 + 10]
    LB = bcols[:, 2 * KC2 + 10:2 * KC2 + 18]
    B2 = bcols[:, 2 * KC2 + 18:2 * KC2 + 26]
    pre = [t_lds]
    t_az = S.emit("vector", lambda e: e.memset(afm[:, :], 0.0))

    zf_cons = []
    zs_cons = []
    cs_cons = []
    qwin_lastread = None
    ropebuf_gate = None
    rope_ns_prev = None
    rope_qq_prev = None
    rope_sw_prev = None
    rope_ra_prev = None
    acc_free = None
    ns_free = None
    rb_free = None
    x1st_stored = None
    self_state = {
        "sc_free": [None, None], "av_free": None, "msc_free": [None, None],
        "p16_free": [None, None], "pt16_free": [None, None],
        "vt16_free": [None, None], "ptp_free": [None, None],
        "pto_free": [None, None], "afm_lastread": t_az, "vt_free": None,
        "attn_done": {},
    }

    for j in range(NTA):
        g0 = -512 + 512 * j
        ldw = (pre + zf_cons) if j == 0 else zf_cons
        t_z = None
        for c in range(8):
            w = ldw if c == 0 else []
            if g0 - 1 >= 0:
                t_z = S.dma("sync", zf[:, c * 513:(c + 1) * 513],
                            zin[c * 128:(c + 1) * 128, g0 - 1:g0 + 512],
                            waits=w, ch="ldz")
            elif g0 + 512 <= 0:
                t_z = S.dma("sync", zf[:, c * 513:(c + 1) * 513],
                            zhalo[c * 128:(c + 1) * 128, 640 + g0 - 1:640 + g0 + 512],
                            waits=w, ch="ldz")
            else:
                nh = -(g0 - 1)
                S.dma("sync", zf[:, c * 513:c * 513 + nh],
                      zhalo[c * 128:(c + 1) * 128, 640 + g0 - 1:640],
                      waits=w, ch="ldz", slow=True)
                t_z = S.dma("sync", zf[:, c * 513 + nh:(c + 1) * 513],
                            zin[c * 128:(c + 1) * 128, 0:g0 + 512], ch="ldz")
        t_cs = None
        for tb in range(4):
            t_cs = S.dma("sync", cs16[:, tb * 512:(tb + 1) * 512],
                         cstab[:, tb * 4608 + j * 512: tb * 4608 + (j + 1) * 512],
                         waits=cs_cons if tb == 0 else [], ch="ldcs")
        CSQ = cs16[:, 0:512]; SNQ = cs16[:, 512:1024]
        CSK = cs16[:, 1024:1536]; SNK = cs16[:, 1536:2048]
        t_zs = []
        for c in range(8):
            off = 1 if c < 4 else 0
            t_zs.append(S.emit("scalar", lambda e, c=c, off=off: e.activation(
                zs16[:, c * 512:(c + 1) * 512],
                zf[:, c * 513 + off:c * 513 + off + 512],
                AF.Copy), waits=[t_z] + zs_cons))
        wA, wB = 2 * j, 2 * j + 1
        sA, sB = wA % 3, wB % 3
        cs_cons = []
        mm = None
        for oc in range(24):
            for kc in range(8):
                w = (t_zs + [acc_free]) if kc == 0 else []
                mm = S.emit("tensor", lambda e, oc=oc, kc=kc: e.matmul(
                    PS_ACC, wq16[:, kc * 3072 + oc * 128: kc * 3072 + (oc + 1) * 128],
                    zs16[:, kc * 512:(kc + 1) * 512],
                    start=(kc == 0), stop=(kc == 7)), waits=w)
            hpair = oc % 8
            hrow_even = 0
            if oc < 16:
                is_q = oc < 8
                CS, SN = (CSQ, SNQ) if is_q else (CSK, SNK)
                scol = QS if is_q else KS
                t_sq = S.emit("scalar", lambda e: e.activation(
                    qsq[:, :], PS_ACC, AF.Square), waits=[mm, rope_ns_prev])
                t_ns = S.emit("tensor", lambda e: e.matmul(
                    PS_NS, e2c_sb[:, :], qsq[:, :], start=True, stop=True),
                    waits=[t_sq, ns_free, rope_qq_prev])
                rope_ns_prev = t_ns
                t_n1 = S.emit("scalar", lambda e: e.activation(nrow[0:2, :], PS_NS,
                                                               AF.Sqrt),
                              waits=[t_ns])
                t_n2 = S.emit("vector", lambda e: e.tensor_scalar_max(
                    nrow[0:2, :], nrow[0:2, :], 1e-12), waits=[t_n1])
                t_n3 = S.emit("vector", lambda e: e.reciprocal(
                    nrow[0:2, :], nrow[0:2, :]))
                if is_q:
                    t_n3 = S.emit("vector", lambda e: e.tensor_scalar_mul(
                        nrow[0:2, :], nrow[0:2, :], 8.0))
                t_n4 = S.emit("scalar", lambda e: e.copy(nrow16[0:2, :],
                                                         nrow[0:2, :]),
                              waits=[t_n3])
                ns_free = t_n1
                t_rb = S.emit("tensor", lambda e: e.matmul(
                    PS_RB, e2t_sb[:, :], nrow16[0:2, :], start=True, stop=True),
                    waits=[t_n4, rb_free])
                t_q1 = S.emit("scalar", lambda e, scol=scol: e.activation(
                    q1t[:, :], PS_ACC, AF.Identity, scale=scol),
                    waits=[mm, rope_qq_prev])
                acc_free = t_q1
                t_qq = S.emit("vector", lambda e: e.tensor_tensor(
                    qq[:, :], q1t[:, :], PS_RB, ALU.mult),
                    waits=[t_q1, t_rb, rope_sw_prev, rope_ra_prev])
                rb_free = t_qq
                rope_qq_prev = t_qq
                t_sw = None
                for (d0, s0_) in ((0, 32), (32, 0), (64, 96), (96, 64)):
                    t_sw = S.emit("gpsimd", lambda e, d0=d0, s0_=s0_: e.tensor_copy(
                        qsw[d0:d0 + 32, :], qq[s0_:s0_ + 32, :]), waits=[t_qq])
                t_ra = S.emit("vector", lambda e, CS=CS: e.tensor_tensor(
                    qra[:, :], qq[:, :], CS, ALU.mult), waits=[t_cs])
                rope_ra_prev = t_ra
                t_sn = S.emit("gpsimd", lambda e, SN=SN: e.tensor_tensor(
                    qsw[:, :], qsw[:, :], SN, ALU.mult), waits=[t_sw, t_cs])
                rope_sw_prev = t_sn
                cs_cons += [t_ra, t_sn]
                if is_q:
                    t_out = S.emit("vector", lambda e, hpair=hpair: e.tensor_tensor(
                        qwin[:, hpair * 512:(hpair + 1) * 512], qra[:, :],
                        qsw[:, :], ALU.add),
                        waits=[t_sn, t_ra, qwin_lastread])
                else:
                    gate = [self_state["attn_done"].get(2 * j - 2), t_sn, t_ra]
                    S.emit("vector", lambda e, hpair=hpair, sA=sA: e.tensor_tensor(
                        kring[:, hpair * 768 + sA * 256: hpair * 768 + (sA + 1) * 256],
                        qra[:, 0:256], qsw[:, 0:256], ALU.add), waits=gate)
                    t_out = S.emit("vector", lambda e, hpair=hpair, sB=sB:
                                   e.tensor_tensor(
                        kring[:, hpair * 768 + sB * 256: hpair * 768 + (sB + 1) * 256],
                        qra[:, 256:512], qsw[:, 256:512], ALU.add))
                ropebuf_gate = t_out
            else:
                gate = [mm, self_state["attn_done"].get(2 * j - 2)]
                S.emit("scalar", lambda e, hpair=hpair, sA=sA: e.activation(
                    vring[:, hpair * 768 + sA * 256: hpair * 768 + (sA + 1) * 256],
                    PS_ACC[:, 0:256], AF.Copy), waits=gate)
                t_out = S.emit("scalar", lambda e, hpair=hpair, sB=sB: e.activation(
                    vring[:, hpair * 768 + sB * 256: hpair * 768 + (sB + 1) * 256],
                    PS_ACC[:, 256:512], AF.Copy))
                acc_free = t_out
        qkv_done = [mm, acc_free, ropebuf_gate]

        wo_wait = []
        ctxs = []
        for wloc, wabs in ((0, wA), (1, wB)):
            if wabs == 0:
                continue
            for h in range(16):
                for qt in range(2):
                    ctxs.append(dict(
                        wloc=wloc, wabs=wabs, sw=wabs % 3, sp=(wabs - 1) % 3,
                        mbase=(2 if wabs == 2 else 0), h=h, hpair=h // 2,
                        hrow=(h % 2) * 64, parv=h % 2, qt=qt,
                        qcol=wloc * 256 + qt * 128))
        for i, c in enumerate(ctxs):
            c["par"] = i % 2

        def st_qk(c):
            hpair, hrow, qcol = c["hpair"], c["hrow"], c["qcol"]
            sp, sw, par = c["sp"], c["sw"], c["par"]
            if c["qt"] == 0:
                parv = c["parv"]
                tvt = None
                for piece in range(4):
                    src_s = sp if piece < 2 else sw
                    pc = piece % 2
                    tvt = S.emit("tensor", lambda e, hpair=hpair, hrow=hrow,
                                 src_s=src_s, pc=pc, parv=parv: e.transpose(
                        ps_t2[:, parv * 64: parv * 64 + 64],
                        vring[hrow:hrow + 64,
                              hpair * 768 + src_s * 256 + pc * 128:
                              hpair * 768 + src_s * 256 + (pc + 1) * 128],
                        id_sb[hrow:hrow + 64, hrow:hrow + 64]),
                        waits=([self_state["vt_free"], self_state["vt16_free"][parv]]
                               + qkv_done) if piece == 0 else [self_state["vt_free"]])
                    tve = S.emit("vector", lambda e, piece=piece, parv=parv:
                                 e.tensor_copy(
                        vt16[:, parv * 256 + piece * 64: parv * 256 + (piece + 1) * 64],
                        ps_t2[:, parv * 64: parv * 64 + 64]),
                        waits=[tvt, self_state["vt16_free"][parv]])
                    self_state["vt_free"] = tve
            t_m1 = S.emit("tensor", lambda e, hpair=hpair, hrow=hrow, qcol=qcol,
                          sp=sp, par=par: e.matmul(
                PS_SC[par][:, 0:256],
                qwin[hrow:hrow + 64, hpair * 512 + qcol: hpair * 512 + qcol + 128],
                kring[hrow:hrow + 64, hpair * 768 + sp * 256: hpair * 768 + (sp + 1) * 256],
                start=True, stop=True),
                waits=[self_state["sc_free"][par]] + qkv_done)
            c["m2"] = S.emit("tensor", lambda e, hpair=hpair, hrow=hrow, qcol=qcol,
                             sw=sw, par=par: e.matmul(
                PS_SC[par][:, 256:512],
                qwin[hrow:hrow + 64, hpair * 512 + qcol: hpair * 512 + qcol + 128],
                kring[hrow:hrow + 64, hpair * 768 + sw * 256: hpair * 768 + (sw + 1) * 256],
                start=True, stop=True))

        def st_mask(c):
            par, mbase, qt = c["par"], c["mbase"], c["qt"]
            c["mk"] = S.emit("vector", lambda e, par=par, mbase=mbase, qt=qt:
                             e.tensor_tensor(
                msc[:, par * 512:(par + 1) * 512], PS_SC[par],
                mask_sb[:, (mbase + qt) * 512:(mbase + qt + 1) * 512],
                ALU.add), waits=[c["m2"], self_state["msc_free"][par]])
            self_state["sc_free"][par] = c["mk"]

        def st_max(c):
            par = c["par"]
            c["mx"] = S.emit("vector", lambda e, par=par: e.reduce_max(
                mx[:, par:par + 1], msc[:, par * 512:(par + 1) * 512],
                axis=mybir.AxisListType.X, negate=True), waits=[c["mk"]])

        def st_exp(c):
            par = c["par"]
            c["ex"] = S.emit("scalar", lambda e, par=par: e.activation(
                p16[:, par * 512:(par + 1) * 512],
                msc[:, par * 512:(par + 1) * 512], AF.Exp,
                bias=mx[:, par:par + 1], accum_out=rs[:, par:par + 1]),
                waits=[c["mx"], self_state["p16_free"][par]])
            self_state["msc_free"][par] = c["ex"]

        def st_rc(c):
            par = c["par"]
            c["rc"] = S.emit("vector", lambda e, par=par: e.reciprocal(
                rs[:, 2 + par:3 + par], rs[:, par:par + 1]), waits=[c["ex"]])

        def st_pt(c):
            par = c["par"]
            tpt = None
            for pc in range(4):
                tpt = S.emit("tensor", lambda e, pc=pc, par=par: e.transpose(
                    PS_PT[par][:, pc * 128:(pc + 1) * 128],
                    p16[:, par * 512 + pc * 128: par * 512 + (pc + 1) * 128],
                    id_sb[:, :]),
                    waits=[c["ex"], self_state["ptp_free"][par]] if pc == 0 else [])
            c["pt"] = tpt
            self_state["p16_free"][par] = tpt

        def st_pe(c):
            par = c["par"]
            c["pe"] = S.emit("vector", lambda e, par=par: e.tensor_copy(
                pt16[:, par * 512:(par + 1) * 512], PS_PT[par][:, 0:512]),
                waits=[c["pt"], self_state["pt16_free"][par]])

        def st_av(c):
            par, parv = c["par"], c["parv"]
            tav = None
            for pc in range(4):
                tav = S.emit("tensor", lambda e, pc=pc, par=par, parv=parv:
                             e.matmul(
                    PS_AV[0], pt16[:, par * 512 + pc * 128: par * 512 + (pc + 1) * 128],
                    vt16[:, parv * 256 + pc * 64: parv * 256 + (pc + 1) * 64],
                    start=(pc == 0), stop=(pc == 3)),
                    waits=[c["pe"], self_state["av_free"]] if pc == 0 else [])
            c["av"] = tav
            self_state["pt16_free"][par] = tav
            self_state["vt16_free"][parv] = tav
            self_state["attn_done"][c["wabs"]] = tav

        def st_at(c):
            par = c["par"]
            c["at"] = S.emit("vector", lambda e, par=par: e.tensor_scalar_mul(
                atm[:, par * 64:(par + 1) * 64], PS_AV[0],
                rs[:, 2 + par:3 + par]), waits=[c["av"], c["rc"]])
            self_state["av_free"] = c["at"]

        def st_ot(c):
            par = c["par"]
            c["ot"] = S.emit("tensor", lambda e, par=par: e.transpose(
                PS_PT[par][0:64, 512:640],
                atm[:, par * 64:(par + 1) * 64], id_sb[:, :]),
                waits=[c["at"], self_state["pto_free"][par]])
            self_state["ptp_free"][par] = c["ot"]

        def st_oe(c):
            par, hpair, hrow, qcol = c["par"], c["hpair"], c["hrow"], c["qcol"]
            t_oe = S.emit("scalar", lambda e, hpair=hpair, hrow=hrow, qcol=qcol,
                          par=par: e.activation(
                afm[hrow:hrow + 64, hpair * 512 + qcol: hpair * 512 + qcol + 128],
                PS_PT[par][0:64, 512:640], AF.Copy),
                waits=[c["ot"], self_state["afm_lastread"]])
            self_state["pto_free"][par] = t_oe
            wo_wait.append(t_oe)

        stages = [st_qk, st_mask, st_max, st_exp, st_rc, st_pt, st_pe, st_av,
                  st_at, st_ot, st_oe]
        NS_ = len(stages)
        SKEW = 2
        for i in range(0, len(ctxs), 2):
            A = ctxs[i]
            B = ctxs[i + 1] if i + 1 < len(ctxs) else None
            for k in range(NS_ + SKEW):
                if k < NS_:
                    stages[k](A)
                if B is not None and SKEW <= k:
                    stages[k - SKEW](B)
        tlast_attn = self_state["attn_done"].get(wB) or self_state["attn_done"].get(wA)
        res_toks = []
        prev_res = None
        for fc in range(8):
            mmw = None
            for kc in range(8):
                w = (wo_wait + [x1st_stored, prev_res]) if kc == 0 else []
                mmw = S.emit("tensor", lambda e, fc=fc, kc=kc: e.matmul(
                    PS_ACC, wo16[:, kc * 1024 + fc * 128: kc * 1024 + (fc + 1) * 128],
                    afm[:, kc * 512:(kc + 1) * 512],
                    start=(kc == 0), stop=(kc == 7)), waits=w)
            t_rs = S.emit("vector", lambda e, fc=fc: e.tensor_tensor(
                x1st[:, fc * 512:(fc + 1) * 512], PS_ACC,
                zf[:, fc * 513 + 1: fc * 513 + 513], ALU.add), waits=[mmw])
            res_toks.append(t_rs)
            prev_res = t_rs
        self_state["afm_lastread"] = mmw
        acc_free = res_toks[-1]
        c0 = 512 * j - 256
        tst = None
        for fc in range(8):
            if j == 0:
                tst = S.dma("sync", x1s[fc * 128:(fc + 1) * 128, 0:256],
                            x1st[:, fc * 512 + 256:(fc + 1) * 512],
                            waits=res_toks if fc == 0 else [], ch="stx1")
            else:
                tst = S.dma("sync", x1s[fc * 128:(fc + 1) * 128, c0:c0 + 512],
                            x1st[:, fc * 512:(fc + 1) * 512],
                            waits=res_toks if fc == 0 else [], ch="stx1")
        x1st_stored = tst
        zf_cons = t_zs + res_toks
        zs_cons = [mm]
        qwin_lastread = self_state["attn_done"].get(wB)

    passa_done = [x1st_stored, acc_free, ropebuf_gate,
                  self_state["vt_free"], self_state["pto_free"][0],
                  self_state["pto_free"][1]]
    for c in range(8):
        S.dma("gpsimd", w1a16[:, c * 2730:(c + 1) * 2730],
              w1a[c * 128:(c + 1) * 128, :],
              waits=passa_done if c == 0 else [], ch="ldg")
    for c in range(8):
        S.dma("gpsimd", w1g16[:, c * 2730:(c + 1) * 2730],
              w1g[c * 128:(c + 1) * 128, :], ch="ldg")
    t_wb = None
    for c in range(KC2):
        r0 = c * 128
        r1 = min(2730, r0 + 128)
        t_wb = S.dma("gpsimd", w216[0:r1 - r0, c * 1024:(c + 1) * 1024],
                     w2[r0:r1, :], ch="ldg")
    x1_cons = []
    g_free = None
    a_free = None
    w2_free = None
    st_free = None
    r_free = None
    m_free = None
    gg_lastread = None
    ff_lastread = None
    zst_stored = [None, None]
    tstz = None
    for i in range(NTB):
        db = 0
        x1t = x1f[:, :]
        t_x1 = None
        for c in range(8):
            w = ([t_wb] + x1_cons) if (i < 2 and c == 0) else (x1_cons if c == 0 else [])
            t_x1 = S.dma("sync", x1t[:, c * 257:(c + 1) * 257],
                         x1s[c * 128:(c + 1) * 128, 255 + 256 * i: 512 + 256 * i],
                         waits=w, ch=f"ldxb{db}")
        t_xv = []
        for c in range(8):
            off = 1 if c < 4 else 0
            ta = S.emit("scalar", lambda e, c=c, off=off, x1t=x1t: e.activation(
                xv16[:, c * 256:(c + 1) * 256],
                x1t[:, c * 257 + off: c * 257 + off + 256], AF.Copy),
                waits=[t_x1, st_free])
            tb = S.emit("scalar", lambda e, c=c: e.activation(
                xsq16[:, c * 256:(c + 1) * 256], xv16[:, c * 256:(c + 1) * 256],
                AF.Square))
            t_xv += [ta, tb]
        tmškoly = None
        tm_ = None
        for c in range(8):
            tm_ = S.emit("tensor", lambda e, c=c: e.matmul(
                PS_B_ST[0:1, 0:256], onesc_sb[:, :], xv16[:, c * 256:(c + 1) * 256],
                start=(c == 0), stop=(c == 7)),
                waits=(t_xv + [st_free, gg_lastread]) if c == 0 else [])
        for c in range(8):
            tm_ = S.emit("tensor", lambda e, c=c: e.matmul(
                PS_B_ST[0:1, 256:512], onesc_sb[:, :],
                xsq16[:, c * 256:(c + 1) * 256],
                start=(c == 0), stop=(c == 7)))
        RW = 256
        mu = strow[0:1, 0:RW]; ex2 = strow[0:1, RW:2 * RW]
        var = strow[0:1, 2 * RW:3 * RW]; rstd = strow[0:1, 3 * RW:4 * RW]
        mur = strow[0:1, 4 * RW:5 * RW]
        t1 = S.emit("vector", lambda e: e.tensor_scalar_mul(
            mu, PS_B_ST[0:1, 0:256], 1.0 / D), waits=[tm_])
        t1x = S.emit("vector", lambda e: e.tensor_scalar_mul(
            ex2, PS_B_ST[0:1, 256:512], 1.0 / D))
        S.emit("vector", lambda e: e.tensor_tensor(var, mu, mu, ALU.mult))
        S.emit("vector", lambda e: e.tensor_tensor(var, ex2, var, ALU.subtract))
        t4 = S.emit("vector", lambda e: e.tensor_scalar_add(var, var, 1e-5))
        t5 = S.emit("scalar", lambda e: e.activation(rstd, var, AF.Sqrt), waits=[t4])
        t6 = S.emit("vector", lambda e: e.reciprocal(rstd, rstd), waits=[t5])
        t7 = S.emit("vector", lambda e: e.tensor_tensor(mur, mu, rstd, ALU.mult))
        t8 = S.emit("scalar", lambda e: e.copy(srow16[0:1, 0:RW], rstd),
                    waits=[t6, r_free])
        t9 = S.emit("scalar", lambda e: e.copy(srow16[0:1, RW:2 * RW], mur),
                    waits=[t7, m_free])
        st_free = t1x
        tR = S.emit("tensor", lambda e: e.matmul(PS_B_R, ones1_sb[:, :],
                                                 srow16[0:1, 0:RW],
                                                 start=True, stop=True),
                    waits=[t8, r_free])
        tM = S.emit("tensor", lambda e: e.matmul(PS_B_M, ones1_sb[:, :],
                                                 srow16[0:1, RW:2 * RW],
                                                 start=True, stop=True),
                    waits=[t9, m_free])
        xn_toks = []
        for c in range(8):
            S.emit("vector", lambda e, c=c: e.tensor_tensor(
                xn16[:, c * 256:(c + 1) * 256], xv16[:, c * 256:(c + 1) * 256],
                PS_B_R, ALU.mult), waits=[tR, tM, g_free, a_free])
            S.emit("vector", lambda e, c=c: e.tensor_tensor(
                xn16[:, c * 256:(c + 1) * 256], xn16[:, c * 256:(c + 1) * 256],
                PS_B_M, ALU.subtract), waits=[tM])
            tc_ = S.emit("vector", lambda e, c=c: e.tensor_scalar(
                xn16[:, c * 256:(c + 1) * 256], xn16[:, c * 256:(c + 1) * 256],
                LG[:, c:c + 1], LB[:, c:c + 1], ALU.mult, ALU.add))
            xn_toks.append(tc_)
        r_free = xn_toks[-1]
        m_free = xn_toks[-1]
        gg_toks = []
        g_evac = [None, None]
        for oc in range(KC2):
            mo = min(128, 2730 - oc * 128)
            gb = oc % 2
            mmg = None
            for kc in range(8):
                w = (xn_toks + [gg_lastread, g_evac[gb], t1x]) if kc == 0 else []
                mmg = S.emit("tensor", lambda e, oc=oc, kc=kc, mo=mo, gb=gb:
                             e.matmul(
                    PS_B_G[gb][0:mo, 0:256],
                    w1g16[:, kc * 2730 + oc * 128: kc * 2730 + oc * 128 + mo],
                    xn16[:, kc * 256:(kc + 1) * 256],
                    start=(kc == 0), stop=(kc == 7)), waits=w)
            tg = S.emit("scalar", lambda e, oc=oc, mo=mo, gb=gb: e.activation(
                gg16[0:mo, oc * 256:(oc + 1) * 256], PS_B_G[gb][0:mo, 0:256],
                AF.Gelu, bias=B1G[0:mo, oc:oc + 1]), waits=[mmg])
            gg_toks.append(tg)
            g_evac[gb] = tg
        g_free = gg_toks[-1]
        ff_toks = []
        f_evac = [None, None]
        for oc in range(KC2):
            mo = min(128, 2730 - oc * 128)
            ab = oc % 2
            mma = None
            for kc in range(8):
                w = ([gg_toks[oc], ff_lastread, f_evac[ab]]) if kc == 0 else []
                mma = S.emit("tensor", lambda e, oc=oc, kc=kc, mo=mo, ab=ab:
                             e.matmul(
                    PS_B_A[ab][0:mo, 0:256],
                    w1a16[:, kc * 2730 + oc * 128: kc * 2730 + oc * 128 + mo],
                    xn16[:, kc * 256:(kc + 1) * 256],
                    start=(kc == 0), stop=(kc == 7)), waits=w)
            tf = S.emit("vector", lambda e, oc=oc, mo=mo, ab=ab:
                        e.scalar_tensor_tensor(
                ff16[0:mo, oc * 256:(oc + 1) * 256], PS_B_A[ab][0:mo, 0:256],
                B1A[0:mo, oc:oc + 1], gg16[0:mo, oc * 256:(oc + 1) * 256],
                ALU.add, ALU.mult), waits=[mma])
            ff_toks.append(tf)
            f_evac[ab] = tf
        a_free = ff_toks[-1]
        z_toks = []
        z_evac = [None, None]
        for fc in range(8):
            zb_ = fc % 2
            mmz = None
            for kc in range(KC2):
                ko = min(128, 2730 - kc * 128)
                w = (ff_toks + [zst_stored[0], z_evac[zb_]]) if kc == 0 else []
                mmz = S.emit("tensor", lambda e, fc=fc, kc=kc, ko=ko, zb_=zb_:
                             e.matmul(
                    PS_B_W2[zb_][:, 0:256],
                    w216[0:ko, kc * 1024 + fc * 128: kc * 1024 + (fc + 1) * 128],
                    ff16[0:ko, kc * 256:(kc + 1) * 256],
                    start=(kc == 0), stop=(kc == KC2 - 1)), waits=w)
            tz_ = S.emit("vector", lambda e, fc=fc, x1t=x1t, zb_=zb_:
                         e.scalar_tensor_tensor(
                zst[:, fc * 256:(fc + 1) * 256],
                PS_B_W2[zb_][:, 0:256], B2[:, fc:fc + 1],
                x1t[:, fc * 257 + 1:(fc + 1) * 257], ALU.add, ALU.add),
                waits=[mmz, zst_stored[0]])
            z_toks.append(tz_)
            z_evac[zb_] = tz_
        ff_lastread = mmz
        gg_lastread = mma
        a_free = z_toks[-1]
        tstz = None
        for fc in range(8):
            tstz = S.dma("sync", zout[fc * 128:(fc + 1) * 128, 256 * i:256 * (i + 1)],
                         zst[:, fc * 256:(fc + 1) * 256],
                         waits=z_toks if fc == 0 else [], ch="stz")
        zst_stored[0] = tstz
        x1_cons = t_xv + z_toks
    S.emit("gpsimd", lambda e: e.engine_nop(),
           waits=[zst_stored[0]], inc=0)
    S.finalize()
    return nc




def _l0f_host_consts():
    ones1 = np.ones((1, 128), dtype=BF)
    onesc = np.ones((128, 1), dtype=BF)
    e2t = np.zeros((2, 128), dtype=BF)
    e2t[0, 0:64] = 1.0
    e2t[1, 64:128] = 1.0
    e2c = np.ascontiguousarray(e2t.T)
    ident = np.eye(128, dtype=BF)
    def mk_mask(qt, maskprev):
        m = np.zeros((128, 512), np.float32)
        for p in range(128):
            ql = qt * 128 + p
            for seg in range(2):
                j0 = seg * 256
                for jj in range(256):
                    if seg == 0:
                        bad = (jj < ql) or maskprev
                    else:
                        bad = jj > ql
                    if bad:
                        m[p, j0 + jj] = -1e30
        return m
    masks_even = np.concatenate([mk_mask(0, False), mk_mask(1, False),
                                 mk_mask(0, True), mk_mask(1, True)], axis=1)
    masks_odd = np.concatenate([mk_mask(0, False), mk_mask(1, False),
                                mk_mask(0, False), mk_mask(1, False)], axis=1)
    DIM_HEAD, WINDOW, n = 64, 256, 8192
    half_ = np.arange(0, DIM_HEAD, 2, dtype=np.float64)
    inv_freq = 1.0 / 10000.0 ** (half_ / DIM_HEAD)
    base_scale = (half_ + 0.4 * DIM_HEAD) / (1.4 * DIM_HEAD)
    tabs = []
    for hh in range(2):
        g0 = hh * 4096 - 512
        t = np.arange(g0, g0 + 4608, dtype=np.float64)
        ang = t[:, None] * inv_freq[None, :]
        power = (t - n // 2) / (WINDOW // 2)
        scale = base_scale[None, :] ** power[:, None]
        cs = np.cos(ang) * scale
        sn = np.sin(ang) * scale
        csk = np.cos(ang) / scale
        snk = np.sin(ang) / scale
        rows = np.arange(128) % 32
        sign = np.where((np.arange(128) % 64) < 32, -1.0, 1.0)[:, None]
        CSQ = cs.T[rows].astype(np.float32)
        SNQ = (sn.T[rows] * sign).astype(np.float32)
        CSK = csk.T[rows].astype(np.float32)
        SNK = (snk.T[rows] * sign).astype(np.float32)
        tabs.append(np.ascontiguousarray(np.concatenate([CSQ, SNQ, CSK, SNK], axis=1)))
    return {"ones1": ones1, "onesc": onesc, "e2t": e2t, "e2c": e2c,
            "ident": ident, "masks": (masks_even, masks_odd), "tabs": tabs}


def _l0f_host(inputs, layer, z_fm, consts):
    KC2 = 22
    Wq = inputs["L0_ffn_Wqkv"][layer]
    Wo_ = inputs["L0_ffn_Wo"][layer]
    W1 = inputs["L0_ffn_W1"][layer]
    b1 = inputs["L0_ffn_b1"][layer]
    W2_ = inputs["L0_ffn_W2"][layer]
    qs = inputs["L0_ffn_q_scale"][layer]
    ks = inputs["L0_ffn_k_scale"][layer]
    lg = inputs["L0_ffn_ln_g"][layer]
    lb = inputs["L0_ffn_ln_b"][layer]
    b2 = inputs["L0_ffn_b2"][layer]
    def colchunks(v, n, kc):
        out = np.zeros((128, kc), np.float32)
        for c in range(kc):
            r0 = c * 128
            r1 = min(n, r0 + 128)
            out[0:r1 - r0, c] = v[r0:r1]
        return out
    w1a = np.ascontiguousarray(W1[:, :2730])
    w1g = np.ascontiguousarray(W1[:, 2730:])
    base = {
        "wqkv": np.ascontiguousarray(Wq),
        "wo": np.ascontiguousarray(Wo_),
        "w1a": w1a, "w1g": w1g, "w2": np.ascontiguousarray(W2_),
        "qsc": np.ascontiguousarray(qs[np.arange(128) % 64].reshape(128, 1)),
        "ksc": np.ascontiguousarray(ks[np.arange(128) % 64].reshape(128, 1)),
        "b1ac": colchunks(b1[:2730], 2730, KC2),
        "b1gc": colchunks(b1[2730:], 2730, KC2),
        "lgc": np.ascontiguousarray(lg.reshape(8, 128).T),
        "lbc": np.ascontiguousarray(lb.reshape(8, 128).T),
        "b2c": np.ascontiguousarray(b2.reshape(8, 128).T),
        "ones1": consts["ones1"], "onesc": consts["onesc"],
        "e2t": consts["e2t"], "e2c": consts["e2c"], "ident": consts["ident"],
    }
    in_maps = []
    for cid in range(NCORE):
        b = cid // 2
        h = cid % 2
        m = dict(base)
        m["maskt"] = consts["masks"][0] if h == 0 else consts["masks"][1]
        m["cstab"] = consts["tabs"][h]
        m["zin"] = np.ascontiguousarray(z_fm[b][:, h * TC:(h + 1) * TC])
        if h == 0:
            m["zhalo"] = np.zeros((D, 640), np.float32)
        else:
            m["zhalo"] = np.ascontiguousarray(z_fm[b][:, TC - 640:TC])
        in_maps.append(m)
    return in_maps




def build_l0f_post():
    nc = bass.Bass("TRN2", target_bir_lowering=False,
                   detect_race_conditions=False)
    NT = TC // 512
    x1i = nc.dram_tensor("x1i", [D, TC], F32, kind="ExternalInput").ap()
    z6i = nc.dram_tensor("z6i", [D, TC], F32, kind="ExternalInput").ap()
    ones1 = nc.dram_tensor("ones1", [1, 128], BF16, kind="ExternalInput").ap()
    onesc = nc.dram_tensor("onesc", [128, 1], BF16, kind="ExternalInput").ap()
    x2o = nc.dram_tensor("x2o", [D, TC], F32, kind="ExternalOutput").ap()
    z7o = nc.dram_tensor("z7o", [D, TC], F32, kind="ExternalOutput").ap()

    ones1_sb = nc.alloc_sbuf_tensor("ones1_sb", [1, 128], BF16)
    onesc_sb = nc.alloc_sbuf_tensor("onesc_sb", [128, 1], BF16)
    xsb = nc.alloc_sbuf_tensor("xsb", [128, 2 * 8 * 512], F32)
    zsb = nc.alloc_sbuf_tensor("zsb", [128, 2 * 8 * 512], F32)
    a16 = nc.alloc_sbuf_tensor("a16", [128, 8 * 512], BF16)
    asq = nc.alloc_sbuf_tensor("asq", [128, 8 * 512], BF16)
    x2b = nc.alloc_sbuf_tensor("x2b", [128, 8 * 512], F32)
    z7b = nc.alloc_sbuf_tensor("z7b", [128, 8 * 512], F32)
    strow = nc.alloc_sbuf_tensor("strow", [1, 5 * 512], F32)
    srow16 = nc.alloc_sbuf_tensor("srow16", [1, 2 * 512], BF16)
    ps = nc.alloc_psum_tensor("ps", [128, 2048], F32)
    PS_SUM = ps[0:1, 0:512]
    PS_SQ = ps[0:1, 512:1024]
    PS_R = ps[:, 1024:1536]
    PS_M = ps[:, 1536:2048]

    S = Sched(nc)
    S.dma("sync", ones1_sb[:, :], ones1, ch="lds")
    t_lds = S.dma("sync", onesc_sb[:, :], onesc, ch="lds")
    pre = [t_lds]

    def ln_rows(S, t_stats):
        RW = 512
        mu = strow[0:1, 0:RW]; ex2 = strow[0:1, RW:2 * RW]
        var = strow[0:1, 2 * RW:3 * RW]; rstd = strow[0:1, 3 * RW:4 * RW]
        mur = strow[0:1, 4 * RW:5 * RW]
        S.emit("vector", lambda e: e.tensor_scalar_mul(mu, PS_SUM, 1.0 / D),
               waits=[t_stats])
        t1x = S.emit("vector", lambda e: e.tensor_scalar_mul(ex2, PS_SQ, 1.0 / D))
        S.emit("vector", lambda e: e.tensor_tensor(var, mu, mu, ALU.mult))
        S.emit("vector", lambda e: e.tensor_tensor(var, ex2, var, ALU.subtract))
        t4 = S.emit("vector", lambda e: e.tensor_scalar_add(var, var, 1e-5))
        t5 = S.emit("scalar", lambda e: e.activation(rstd, var, AF.Sqrt), waits=[t4])
        t6 = S.emit("vector", lambda e: e.reciprocal(rstd, rstd), waits=[t5])
        t7 = S.emit("vector", lambda e: e.tensor_tensor(mur, mu, rstd, ALU.mult))
        t8 = S.emit("scalar", lambda e: e.copy(srow16[0:1, 0:RW], rstd), waits=[t6])
        t9 = S.emit("scalar", lambda e: e.copy(srow16[0:1, RW:2 * RW], mur),
                    waits=[t7])
        return t1x, t8, t9

    prev_cons = []
    prev_R_reader = None
    st_toks = []
    for tt in range(NT):
        t0 = tt * 512
        db = tt % 2
        xb = xsb[:, db * 4096:(db + 1) * 4096]
        zb = zsb[:, db * 4096:(db + 1) * 4096]
        ldw = (pre + prev_cons) if tt < 2 else prev_cons
        t_x = None
        for c in range(8):
            t_x = S.dma("sync", xb[:, c * 512:(c + 1) * 512],
                        x1i[c * 128:(c + 1) * 128, t0:t0 + 512],
                        waits=ldw if c == 0 else [], ch=f"ld{db}")
        for c in range(8):
            t_x = S.dma("sync", zb[:, c * 512:(c + 1) * 512],
                        z6i[c * 128:(c + 1) * 128, t0:t0 + 512], ch=f"ld{db}")
        cons = []
        casts = []
        for c in range(8):
            ta = S.emit("scalar", lambda e, c=c, zb=zb: e.activation(
                a16[:, c * 512:(c + 1) * 512], zb[:, c * 512:(c + 1) * 512],
                AF.Copy), waits=[t_x, prev_R_reader])
            tb = S.emit("scalar", lambda e, c=c: e.activation(
                asq[:, c * 512:(c + 1) * 512], a16[:, c * 512:(c + 1) * 512],
                AF.Square))
            casts += [ta, tb]
        tm = None
        for c in range(8):
            tm = S.emit("tensor", lambda e, c=c: e.matmul(
                PS_SUM, onesc_sb[:, :], a16[:, c * 512:(c + 1) * 512],
                start=(c == 0), stop=(c == 7)),
                waits=(casts + [prev_R_reader]) if c == 0 else [])
        for c in range(8):
            tm = S.emit("tensor", lambda e, c=c: e.matmul(
                PS_SQ, onesc_sb[:, :], asq[:, c * 512:(c + 1) * 512],
                start=(c == 0), stop=(c == 7)))
        t1x, t8, t9 = ln_rows(S, tm)
        tR = S.emit("tensor", lambda e: e.matmul(PS_R, ones1_sb[:, :],
                                                 srow16[0:1, 0:512],
                                                 start=True, stop=True), waits=[t8])
        tM = S.emit("tensor", lambda e: e.matmul(PS_M, ones1_sb[:, :],
                                                 srow16[0:1, 512:1024],
                                                 start=True, stop=True), waits=[t9])
        x2toks = []
        for c in range(8):
            S.emit("vector", lambda e, c=c: e.tensor_tensor(
                x2b[:, c * 512:(c + 1) * 512], a16[:, c * 512:(c + 1) * 512],
                PS_R, ALU.mult), waits=[tR])
            S.emit("vector", lambda e, c=c: e.tensor_tensor(
                x2b[:, c * 512:(c + 1) * 512], x2b[:, c * 512:(c + 1) * 512],
                PS_M, ALU.subtract), waits=[tM])
            ta = S.emit("vector", lambda e, c=c, xb=xb: e.tensor_tensor(
                x2b[:, c * 512:(c + 1) * 512], x2b[:, c * 512:(c + 1) * 512],
                xb[:, c * 512:(c + 1) * 512], ALU.add))
            tb = S.emit("scalar", lambda e, c=c: e.activation(
                a16[:, c * 512:(c + 1) * 512], x2b[:, c * 512:(c + 1) * 512],
                AF.Copy), waits=[ta])
            tc_ = S.emit("scalar", lambda e, c=c: e.activation(
                asq[:, c * 512:(c + 1) * 512], a16[:, c * 512:(c + 1) * 512],
                AF.Square))
            x2toks += [ta, tb, tc_]
        tsx = None
        for c in range(8):
            tsx = S.dma("sync", x2o[c * 128:(c + 1) * 128, t0:t0 + 512],
                        x2b[:, c * 512:(c + 1) * 512],
                        waits=x2toks if c == 0 else [], ch=f"sx{db}")
        tm2 = None
        for c in range(8):
            tm2 = S.emit("tensor", lambda e, c=c: e.matmul(
                PS_SUM, onesc_sb[:, :], a16[:, c * 512:(c + 1) * 512],
                start=(c == 0), stop=(c == 7)),
                waits=(x2toks + [t1x]) if c == 0 else [])
        for c in range(8):
            tm2 = S.emit("tensor", lambda e, c=c: e.matmul(
                PS_SQ, onesc_sb[:, :], asq[:, c * 512:(c + 1) * 512],
                start=(c == 0), stop=(c == 7)))
        t1b, t8b, t9b = ln_rows(S, tm2)
        tR2 = S.emit("tensor", lambda e: e.matmul(PS_R, ones1_sb[:, :],
                                                  srow16[0:1, 0:512],
                                                  start=True, stop=True),
                     waits=[t8b])
        tM2 = S.emit("tensor", lambda e: e.matmul(PS_M, ones1_sb[:, :],
                                                  srow16[0:1, 512:1024],
                                                  start=True, stop=True),
                     waits=[t9b])
        z7toks = []
        for c in range(8):
            S.emit("vector", lambda e, c=c: e.tensor_tensor(
                z7b[:, c * 512:(c + 1) * 512], a16[:, c * 512:(c + 1) * 512],
                PS_R, ALU.mult), waits=[tR2])
            tzb = S.emit("vector", lambda e, c=c: e.tensor_tensor(
                z7b[:, c * 512:(c + 1) * 512], z7b[:, c * 512:(c + 1) * 512],
                PS_M, ALU.subtract), waits=[tM2])
            z7toks.append(tzb)
        tsz = None
        for c in range(8):
            tsz = S.dma("sync", z7o[c * 128:(c + 1) * 128, t0:t0 + 512],
                        z7b[:, c * 512:(c + 1) * 512],
                        waits=z7toks if c == 0 else [], ch=f"sz{db}")
        st_toks = [tsx, tsz]
        prev_R_reader = z7toks[-1]
        prev_cons = casts + x2toks + z7toks + [t1b]
    S.emit("gpsimd", lambda e: e.engine_nop(), waits=st_toks, inc=0)
    S.finalize()
    return nc


def build_l1a_qkv():
    nc = bass.Bass("TRN2", target_bir_lowering=False,
                   detect_race_conditions=False)
    NT = TC // 512
    z7i = nc.dram_tensor("z7i", [D, TC], F32, kind="ExternalInput").ap()
    wq = nc.dram_tensor("wq", [D, D], F32, kind="ExternalInput").ap()
    wk = nc.dram_tensor("wk", [D, D], F32, kind="ExternalInput").ap()
    wv = nc.dram_tensor("wv", [D, D], F32, kind="ExternalInput").ap()
    cst = nc.dram_tensor("cst", [128, 2 * TC], F32, kind="ExternalInput").ap()
    ident = nc.dram_tensor("ident", [128, 128], BF16, kind="ExternalInput").ap()
    qpo = nc.dram_tensor("qpo", [D, TC], BF16, kind="ExternalOutput").ap()
    vkto = nc.dram_tensor("vkto", [64, 16 * 65], F32, kind="ExternalOutput").ap()

    wq16 = nc.alloc_sbuf_tensor("wq16", [128, 8 * 1024], BF16)
    wk16 = nc.alloc_sbuf_tensor("wk16", [128, 8 * 1024], BF16)
    wv16 = nc.alloc_sbuf_tensor("wv16", [128, 8 * 1024], BF16)
    cs_sb = nc.alloc_sbuf_tensor("cs_sb", [128, 2 * TC], F32)
    id_sb = nc.alloc_sbuf_tensor("id_sb", [128, 128], BF16)
    z16 = nc.alloc_sbuf_tensor("z16", [128, 2 * 8 * 512], BF16)
    qraw = nc.alloc_sbuf_tensor("qraw", [128, 512], F32)
    qsw = nc.alloc_sbuf_tensor("qsw", [128, 512], F32)
    qra = nc.alloc_sbuf_tensor("qra", [128, 512], F32)
    qp16 = nc.alloc_sbuf_tensor("qp16", [128, 8 * 512], BF16)
    kp16 = nc.alloc_sbuf_tensor("kp16", [128, 8 * 512], BF16)
    v16 = nc.alloc_sbuf_tensor("v16", [128, 8 * 512], BF16)
    ktt = nc.alloc_sbuf_tensor("ktt", [128, 64], BF16)
    vtt = nc.alloc_sbuf_tensor("vtt", [128, 65], BF16)
    vkt_sb = nc.alloc_sbuf_tensor("vkt_sb", [64, 16 * 65], F32)

    ps = nc.alloc_psum_tensor("ps", [128, 2048], F32)
    PS_MM = ps[:, 0:512]
    PS_VK = ps[0:64, 512:1024]
    ps_t = nc.alloc_psum_tensor("ps_t", [128, 256], BF16)
    PT_K = ps_t[:, 0:64]
    PT_V = ps_t[:, 64:128]

    S = Sched(nc)
    for c in range(8):
        S.dma("gpsimd", wq16[:, c * 1024:(c + 1) * 1024],
              wq[c * 128:(c + 1) * 128, :], ch="ldg")
    for c in range(8):
        S.dma("gpsimd", wk16[:, c * 1024:(c + 1) * 1024],
              wk[c * 128:(c + 1) * 128, :], ch="ldg")
    t_ldg = None
    for c in range(8):
        t_ldg = S.dma("gpsimd", wv16[:, c * 1024:(c + 1) * 1024],
                      wv[c * 128:(c + 1) * 128, :], ch="ldg")
    S.dma("sync", cs_sb[:, :], cst, ch="lds")
    t_lds = S.dma("sync", id_sb[:, :], ident, ch="lds")
    t_z0 = S.emit("vector", lambda e: e.memset(vkt_sb[:, :], 0.0))
    t_v1 = S.emit("vector", lambda e: e.memset(vtt[:, 64:65], 1.0))
    pre = [t_ldg, t_lds, t_z0, t_v1]

    prev_cons = []
    rope_gate = None
    mm_free = None
    vk_free = t_z0
    kt_free = None
    vt_free = t_v1
    st_q = None
    vkt_accum = {}
    for tt in range(NT):
        t0 = tt * 512
        db = tt % 2
        zb = z16[:, db * 4096:(db + 1) * 4096]
        ldw = (pre + prev_cons) if tt < 2 else prev_cons
        t_z = None
        for c in range(8):
            t_z = S.dma("gpsimd", zb[:, c * 512:(c + 1) * 512],
                        z7i[c * 128:(c + 1) * 128, t0:t0 + 512],
                        waits=ldw if c == 0 else [], ch=f"ldz{db}")
        CS = cs_sb[:, t0:t0 + 512]
        SN = cs_sb[:, TC + t0:TC + t0 + 512]
        qp_toks = []
        for (w16_, dst, dma_out) in ((wq16, qp16, True), (wk16, kp16, False)):
            for oc in range(8):
                mm = None
                for kc in range(8):
                    w = [t_z, mm_free] if kc == 0 else []
                    mm = S.emit("tensor", lambda e, oc=oc, kc=kc, w16_=w16_,
                                zb=zb: e.matmul(
                        PS_MM,
                        w16_[:, kc * 1024 + oc * 128: kc * 1024 + (oc + 1) * 128],
                        zb[:, kc * 512:(kc + 1) * 512],
                        start=(kc == 0), stop=(kc == 7)), waits=w)
                t_qr = S.emit("scalar", lambda e: e.activation(
                    qraw[:, :], PS_MM, AF.Copy), waits=[mm, rope_gate])
                mm_free = t_qr
                t_sw = None
                for (d0, s0_) in ((0, 32), (32, 0), (64, 96), (96, 64)):
                    t_sw = S.emit("gpsimd", lambda e, d0=d0, s0_=s0_: e.tensor_copy(
                        qsw[d0:d0 + 32, :], qraw[s0_:s0_ + 32, :]), waits=[t_qr])
                t_ra = S.emit("vector", lambda e, CS=CS: e.tensor_tensor(
                    qra[:, :], qraw[:, :], CS, ALU.mult), waits=[t_qr])
                t_sn = S.emit("gpsimd", lambda e, SN=SN: e.tensor_tensor(
                    qsw[:, :], qsw[:, :], SN, ALU.mult), waits=[t_sw])
                t_ad = S.emit("vector", lambda e: e.tensor_tensor(
                    qra[:, :], qra[:, :], qsw[:, :], ALU.add), waits=[t_sn])
                t_rl = S.emit("vector", lambda e, oc=oc, dst=dst: e.tensor_scalar_max(
                    dst[:, oc * 512:(oc + 1) * 512], qra[:, :], 0.0))
                rope_gate = t_rl
                if dma_out:
                    tq = S.dma("sync", qpo[oc * 128:(oc + 1) * 128, t0:t0 + 512],
                               dst[:, oc * 512:(oc + 1) * 512], waits=[t_rl],
                               ch="stq")
                    qp_toks.append(tq)
                    st_q = tq
        v_toks = []
        for oc in range(8):
            mm = None
            for kc in range(8):
                w = [t_z, mm_free] if kc == 0 else []
                mm = S.emit("tensor", lambda e, oc=oc, kc=kc, zb=zb: e.matmul(
                    PS_MM, wv16[:, kc * 1024 + oc * 128: kc * 1024 + (oc + 1) * 128],
                    zb[:, kc * 512:(kc + 1) * 512],
                    start=(kc == 0), stop=(kc == 7)), waits=w)
            tv = S.emit("scalar", lambda e, oc=oc: e.activation(
                v16[:, oc * 512:(oc + 1) * 512], PS_MM, AF.Copy), waits=[mm])
            mm_free = tv
            v_toks.append(tv)
        for h in range(16):
            hp, hr = h // 2, (h % 2) * 64
            for pc in range(4):
                tk = S.emit("tensor", lambda e, hp=hp, hr=hr, pc=pc: e.transpose(
                    PT_K, kp16[hr:hr + 64, hp * 512 + pc * 128: hp * 512 + (pc + 1) * 128],
                    id_sb[hr:hr + 64, hr:hr + 64]),
                    waits=[rope_gate, kt_free] + v_toks if pc == 0 else [kt_free])
                tke = S.emit("vector", lambda e: e.tensor_copy(ktt[:, :], PT_K),
                             waits=[tk])
                tv_ = S.emit("tensor", lambda e, hp=hp, hr=hr, pc=pc: e.transpose(
                    PT_V, v16[hr:hr + 64, hp * 512 + pc * 128: hp * 512 + (pc + 1) * 128],
                    id_sb[hr:hr + 64, hr:hr + 64]), waits=[vt_free, tke])
                tve = S.emit("vector", lambda e: e.tensor_copy(vtt[:, 0:64], PT_V),
                             waits=[tv_])
                mmvk = S.emit("tensor", lambda e: e.matmul(
                    PS_VK[0:64, 0:65], ktt[:, :], vtt[:, :],
                    start=True, stop=True), waits=[tke, tve, vk_free])
                kt_free = mmvk
                vt_free = mmvk
                tacc = S.emit("vector", lambda e, h=h: e.tensor_tensor(
                    vkt_sb[:, h * 65:(h + 1) * 65], vkt_sb[:, h * 65:(h + 1) * 65],
                    PS_VK[0:64, 0:65], ALU.add), waits=[mmvk])
                vk_free = tacc
        prev_cons = [mm_free, rope_gate, vk_free, st_q]
    tvk = S.dma("sync", vkto, vkt_sb[:, :], waits=[vk_free], ch="stvk")
    S.emit("gpsimd", lambda e: e.engine_nop(), waits=[tvk, st_q], inc=0)
    S.finalize()
    return nc




def build_l1a_fin():
    nc = bass.Bass("TRN2", target_bir_lowering=False,
                   detect_race_conditions=False)
    NT = TC // 512
    qpi = nc.dram_tensor("qpi", [D, TC], BF16, kind="ExternalInput").ap()
    vk2 = nc.dram_tensor("vk2", [2 * 64, 16 * 65], F32, kind="ExternalInput").ap()
    x2i = nc.dram_tensor("x2i", [D, TC], F32, kind="ExternalInput").ap()
    owT = nc.dram_tensor("owT", [D, D], F32, kind="ExternalInput").ap()
    ones1 = nc.dram_tensor("ones1", [1, 128], BF16, kind="ExternalInput").ap()
    onesc = nc.dram_tensor("onesc", [128, 1], BF16, kind="ExternalInput").ap()
    ident = nc.dram_tensor("ident", [128, 128], BF16, kind="ExternalInput").ap()
    x3o = nc.dram_tensor("x3o", [D, TC], F32, kind="ExternalOutput").ap()
    z8o = nc.dram_tensor("z8o", [D, TC], F32, kind="ExternalOutput").ap()

    w16 = nc.alloc_sbuf_tensor("w16", [128, 8 * 1024], BF16)
    ones1_sb = nc.alloc_sbuf_tensor("ones1_sb", [1, 128], BF16)
    onesc_sb = nc.alloc_sbuf_tensor("onesc_sb", [128, 1], BF16)
    id_sb = nc.alloc_sbuf_tensor("id_sb", [128, 128], BF16)
    vk_a = nc.alloc_sbuf_tensor("vk_a", [64, 16 * 65], F32)
    vk_b = nc.alloc_sbuf_tensor("vk_b", [64, 16 * 65], F32)
    vk16 = nc.alloc_sbuf_tensor("vk16", [128, 16 * 65], BF16)
    qsb = nc.alloc_sbuf_tensor("qsb", [128, 2 * 8 * 512], BF16)
    xsb = nc.alloc_sbuf_tensor("xsb", [128, 2 * 8 * 512], F32)
    att = nc.alloc_sbuf_tensor("att", [128, 64], BF16)
    rcol = nc.alloc_sbuf_tensor("rcol", [128, 2], F32)
    fb = nc.alloc_sbuf_tensor("fb", [128, 128], F32)
    afm = nc.alloc_sbuf_tensor("afm", [128, 8 * 512], BF16)
    u16 = nc.alloc_sbuf_tensor("u16", [128, 8 * 512], BF16)
    usq = nc.alloc_sbuf_tensor("usq", [128, 8 * 512], BF16)
    x3b = nc.alloc_sbuf_tensor("x3b", [128, 8 * 512], F32)
    z8b = nc.alloc_sbuf_tensor("z8b", [128, 8 * 512], F32)
    strow = nc.alloc_sbuf_tensor("strow", [1, 5 * 512], F32)
    srow16 = nc.alloc_sbuf_tensor("srow16", [1, 2 * 512], BF16)

    ps = nc.alloc_psum_tensor("ps", [128, 3072], F32)
    PS_NUM = [ps[:, 0:65], ps[:, 512:577]]
    PS_MM = ps[:, 1024:1536]
    PS_SUM = ps[0:1, 1536:2048]
    PS_SQ = ps[0:1, 2048:2560]
    PS_R = ps[:, 2560:3072]
    PS_M = ps[:, 1536:2048]
    ps_t = nc.alloc_psum_tensor("ps_t", [128, 256], BF16)
    PT_A = ps_t[0:64, 0:128]

    S = Sched(nc)
    for c in range(8):
        S.dma("gpsimd", w16[:, c * 1024:(c + 1) * 1024],
              owT[c * 128:(c + 1) * 128, :], ch="ldg")
    t_ldg = S.dma("gpsimd", id_sb[:, :], ident, ch="ldg")
    S.dma("sync", vk_a[:, :], vk2[0:64, :], ch="lds")
    S.dma("sync", vk_b[:, :], vk2[64:128, :], ch="lds")
    S.dma("sync", ones1_sb[:, :], ones1, ch="lds")
    t_lds = S.dma("sync", onesc_sb[:, :], onesc, ch="lds")
    t_s1 = S.emit("vector", lambda e: e.tensor_tensor(vk_a[:, :], vk_a[:, :],
                                                      vk_b[:, :], ALU.add),
                  waits=[t_lds, t_ldg])
    t_s2 = S.emit("scalar", lambda e: e.copy(vk16[0:64, :], vk_a[:, :]),
                  waits=[t_s1])
    t_s3 = S.emit("scalar", lambda e: e.copy(vk16[64:128, :], vk_a[:, :]))
    t_fb = S.emit("vector", lambda e: e.memset(fb[:, :], 0.0))
    pre = [t_s2, t_s3, t_fb]

    def ln_rows(S, t_stats):
        RW = 512
        mu = strow[0:1, 0:RW]; ex2 = strow[0:1, RW:2 * RW]
        var = strow[0:1, 2 * RW:3 * RW]; rstd = strow[0:1, 3 * RW:4 * RW]
        mur = strow[0:1, 4 * RW:5 * RW]
        t1 = S.emit("vector", lambda e: e.tensor_scalar_mul(mu, PS_SUM, 1.0 / D),
                    waits=[t_stats])
        S.emit("vector", lambda e: e.tensor_scalar_mul(ex2, PS_SQ, 1.0 / D))
        S.emit("vector", lambda e: e.tensor_tensor(var, mu, mu, ALU.mult))
        S.emit("vector", lambda e: e.tensor_tensor(var, ex2, var, ALU.subtract))
        t4 = S.emit("vector", lambda e: e.tensor_scalar_add(var, var, 1e-5))
        t5 = S.emit("scalar", lambda e: e.activation(rstd, var, AF.Sqrt), waits=[t4])
        t6 = S.emit("vector", lambda e: e.reciprocal(rstd, rstd), waits=[t5])
        t7 = S.emit("vector", lambda e: e.tensor_tensor(mur, mu, rstd, ALU.mult))
        t8 = S.emit("scalar", lambda e: e.copy(srow16[0:1, 0:RW], rstd), waits=[t6])
        t9 = S.emit("scalar", lambda e: e.copy(srow16[0:1, RW:2 * RW], mur),
                    waits=[t7])
        return t1, t8, t9

    prev_cons = []
    num_free = [pre[-1], pre[-1]]
    att_free = None
    pta_free = None
    afm_lastread = None
    mm_evac = None
    prev_R = None
    st_toks = []
    for tt in range(NT):
        t0 = tt * 512
        db = tt % 2
        qb = qsb[:, db * 4096:(db + 1) * 4096]
        xb = xsb[:, db * 4096:(db + 1) * 4096]
        ldw = (pre + prev_cons) if tt < 2 else prev_cons
        t_q = None
        for c in range(8):
            t_q = S.dma("sync", qb[:, c * 512:(c + 1) * 512],
                        qpi[c * 128:(c + 1) * 128, t0:t0 + 512],
                        waits=ldw if c == 0 else [], ch=f"ldq{db}")
        t_x = None
        for c in range(8):
            t_x = S.dma("sync", xb[:, c * 512:(c + 1) * 512],
                        x2i[c * 128:(c + 1) * 128, t0:t0 + 512], ch=f"ldq{db}")
        at_toks = []
        for h in range(16):
            hp, hr = h // 2, (h % 2) * 64
            for st4 in range(4):
                nb = (h * 4 + st4) % 2
                mmn = S.emit("tensor", lambda e, hp=hp, hr=hr, st4=st4, nb=nb,
                             qb=qb, h=h: e.matmul(
                    PS_NUM[nb][0:128, 0:65],
                    qb[hr:hr + 64, hp * 512 + st4 * 128: hp * 512 + (st4 + 1) * 128],
                    vk16[hr:hr + 64, h * 65:(h + 1) * 65],
                    start=True, stop=True),
                    waits=[t_q, num_free[nb]])
                tden = S.emit("vector", lambda e, nb=nb: e.tensor_scalar_add(
                    rcol[:, 0:1], PS_NUM[nb][0:128, 64:65], 1e-6), waits=[mmn])
                S.emit("vector", lambda e: e.tensor_copy(fb[:, 64:128],
                                                         fb[:, 0:64]))
                trc = S.emit("vector", lambda e: e.reciprocal(
                    rcol[:, 1:2], rcol[:, 0:1]))
                S.emit("vector", lambda e: e.tensor_copy(fb[:, 64:128],
                                                         fb[:, 0:64]))
                tat = S.emit("vector", lambda e, nb=nb: e.tensor_scalar_mul(
                    att[:, :], PS_NUM[nb][0:128, 0:64], rcol[:, 1:2]),
                    waits=[att_free])
                num_free[nb] = tat
                tot = S.emit("tensor", lambda e: e.transpose(
                    PT_A, att[:, :], id_sb[:, :]), waits=[tat, pta_free])
                toe = S.emit("scalar", lambda e, hp=hp, hr=hr, st4=st4: e.activation(
                    afm[hr:hr + 64, hp * 512 + st4 * 128: hp * 512 + (st4 + 1) * 128],
                    PT_A, AF.Copy), waits=[tot, afm_lastread])
                att_free = tot
                pta_free = toe
                at_toks.append(toe)
        cons = []
        ev_prev = mm_evac
        for fc in range(8):
            mm = None
            for kc in range(8):
                w = (at_toks + [ev_prev] + ([prev_R] if fc == 0 else [])) \
                    if kc == 0 else []
                mm = S.emit("tensor", lambda e, fc=fc, kc=kc: e.matmul(
                    PS_MM, w16[:, kc * 1024 + fc * 128: kc * 1024 + (fc + 1) * 128],
                    afm[:, kc * 512:(kc + 1) * 512],
                    start=(kc == 0), stop=(kc == 7)), waits=w)
            tu = S.emit("scalar", lambda e, fc=fc: e.activation(
                u16[:, fc * 512:(fc + 1) * 512], PS_MM, AF.Copy), waits=[mm])
            ts_ = S.emit("scalar", lambda e, fc=fc: e.activation(
                usq[:, fc * 512:(fc + 1) * 512], u16[:, fc * 512:(fc + 1) * 512],
                AF.Square))
            ev_prev = tu
            cons += [tu, ts_]
        mm_evac = ev_prev
        afm_lastread = mm
        tm = None
        for c in range(8):
            tm = S.emit("tensor", lambda e, c=c: e.matmul(
                PS_SUM, onesc_sb[:, :], u16[:, c * 512:(c + 1) * 512],
                start=(c == 0), stop=(c == 7)), waits=cons if c == 0 else [])
        for c in range(8):
            tm = S.emit("tensor", lambda e, c=c: e.matmul(
                PS_SQ, onesc_sb[:, :], usq[:, c * 512:(c + 1) * 512],
                start=(c == 0), stop=(c == 7)))
        t1, t8, t9 = ln_rows(S, tm)
        tR = S.emit("tensor", lambda e: e.matmul(PS_R, ones1_sb[:, :],
                                                 srow16[0:1, 0:512],
                                                 start=True, stop=True), waits=[t8])
        tM = S.emit("tensor", lambda e: e.matmul(PS_M, ones1_sb[:, :],
                                                 srow16[0:1, 512:1024],
                                                 start=True, stop=True),
                    waits=[t9, t1])
        x3toks = []
        for c in range(8):
            S.emit("vector", lambda e, c=c: e.tensor_tensor(
                x3b[:, c * 512:(c + 1) * 512], u16[:, c * 512:(c + 1) * 512],
                PS_R, ALU.mult), waits=[tR, t_x])
            S.emit("vector", lambda e, c=c: e.tensor_tensor(
                x3b[:, c * 512:(c + 1) * 512], x3b[:, c * 512:(c + 1) * 512],
                PS_M, ALU.subtract), waits=[tM])
            ta = S.emit("vector", lambda e, c=c, xb=xb: e.tensor_tensor(
                x3b[:, c * 512:(c + 1) * 512], x3b[:, c * 512:(c + 1) * 512],
                xb[:, c * 512:(c + 1) * 512], ALU.add))
            tb = S.emit("scalar", lambda e, c=c: e.activation(
                u16[:, c * 512:(c + 1) * 512], x3b[:, c * 512:(c + 1) * 512],
                AF.Copy), waits=[ta])
            tc_ = S.emit("scalar", lambda e, c=c: e.activation(
                usq[:, c * 512:(c + 1) * 512], u16[:, c * 512:(c + 1) * 512],
                AF.Square))
            x3toks += [ta, tb, tc_]
        tsx = None
        for c in range(8):
            tsx = S.dma("sync", x3o[c * 128:(c + 1) * 128, t0:t0 + 512],
                        x3b[:, c * 512:(c + 1) * 512],
                        waits=x3toks if c == 0 else [], ch=f"sx{db}")
        tm2 = None
        for c in range(8):
            tm2 = S.emit("tensor", lambda e, c=c: e.matmul(
                PS_SUM, onesc_sb[:, :], u16[:, c * 512:(c + 1) * 512],
                start=(c == 0), stop=(c == 7)), waits=x3toks if c == 0 else [])
        for c in range(8):
            tm2 = S.emit("tensor", lambda e, c=c: e.matmul(
                PS_SQ, onesc_sb[:, :], usq[:, c * 512:(c + 1) * 512],
                start=(c == 0), stop=(c == 7)))
        t1b, t8b, t9b = ln_rows(S, tm2)
        tR2 = S.emit("tensor", lambda e: e.matmul(PS_R, ones1_sb[:, :],
                                                  srow16[0:1, 0:512],
                                                  start=True, stop=True),
                     waits=[t8b])
        tM2 = S.emit("tensor", lambda e: e.matmul(PS_M, ones1_sb[:, :],
                                                  srow16[0:1, 512:1024],
                                                  start=True, stop=True),
                     waits=[t9b, t1b])
        z8toks = []
        for c in range(8):
            S.emit("vector", lambda e, c=c: e.tensor_tensor(
                z8b[:, c * 512:(c + 1) * 512], u16[:, c * 512:(c + 1) * 512],
                PS_R, ALU.mult), waits=[tR2])
            tzb = S.emit("vector", lambda e, c=c: e.tensor_tensor(
                z8b[:, c * 512:(c + 1) * 512], z8b[:, c * 512:(c + 1) * 512],
                PS_M, ALU.subtract), waits=[tM2])
            z8toks.append(tzb)
        tsz = None
        for c in range(8):
            tsz = S.dma("sync", z8o[c * 128:(c + 1) * 128, t0:t0 + 512],
                        z8b[:, c * 512:(c + 1) * 512],
                        waits=z8toks if c == 0 else [], ch=f"sz{db}")
        st_toks = [tsx, tsz]
        prev_R = z8toks[-1]
        prev_cons = cons + x3toks + z8toks
    S.emit("gpsimd", lambda e: e.engine_nop(), waits=st_toks, inc=0)
    S.finalize()
    return nc


def build_l1f():
    nc = bass.Bass("TRN2", target_bir_lowering=False,
                   detect_race_conditions=False)
    TTOK = 256
    NT = TC // TTOK
    z8i = nc.dram_tensor("z8i", [D, TC], F32, kind="ExternalInput").ap()
    x3i = nc.dram_tensor("x3i", [D, TC], F32, kind="ExternalInput").ap()
    gwT = nc.dram_tensor("gwT", [D, 8], F32, kind="ExternalInput").ap()
    ewT = nc.dram_tensor("ewT", [8 * D, D], F32, kind="ExternalInput").ap()
    ones1 = nc.dram_tensor("ones1", [1, 128], BF16, kind="ExternalInput").ap()
    onesc = nc.dram_tensor("onesc", [128, 1], BF16, kind="ExternalInput").ap()
    ident = nc.dram_tensor("ident", [128, 128], BF16, kind="ExternalInput").ap()
    outo = nc.dram_tensor("outo", [D, TC], F32, kind="ExternalOutput").ap()

    ew16 = nc.alloc_sbuf_tensor("ew16", [128, 8 * 8 * 1024], BF16)
    gw_sb = nc.alloc_sbuf_tensor("gw_sb", [128, 8 * 8], F32)
    ones1_sb = nc.alloc_sbuf_tensor("ones1_sb", [1, 128], BF16)
    onesc_sb = nc.alloc_sbuf_tensor("onesc_sb", [128, 1], BF16)
    id_sb = nc.alloc_sbuf_tensor("id_sb", [128, 128], BF16)
    zsb = nc.alloc_sbuf_tensor("zsb", [128, 8 * TTOK], F32)
    z16 = nc.alloc_sbuf_tensor("z16", [128, 8 * TTOK], BF16)
    xsb = nc.alloc_sbuf_tensor("xsb", [128, 8 * TTOK], F32)
    mrow = nc.alloc_sbuf_tensor("mrow", [1, 8 * TTOK], BF16)
    mbc16 = nc.alloc_sbuf_tensor("mbc16", [128, 8 * TTOK], BF16)
    mt = nc.alloc_sbuf_tensor("mt", [128, 8], F32)
    mxc = nc.alloc_sbuf_tensor("mxc", [128, 1], F32)
    id32 = nc.alloc_sbuf_tensor("id32", [128, 128], F32)
    fbg = nc.alloc_sbuf_tensor("fbg", [128, 128], F32)
    xe16 = nc.alloc_sbuf_tensor("xe16", [128, 8 * TTOK], BF16)
    u16 = nc.alloc_sbuf_tensor("u16", [128, 8 * TTOK], BF16)
    usq = nc.alloc_sbuf_tensor("usq", [128, 8 * TTOK], BF16)
    ob = nc.alloc_sbuf_tensor("ob", [128, 8 * TTOK], F32)
    strow = nc.alloc_sbuf_tensor("strow", [1, 5 * TTOK], F32)
    srow16 = nc.alloc_sbuf_tensor("srow16", [1, 2 * TTOK], BF16)

    ps = nc.alloc_psum_tensor("ps", [128, 4096], F32)
    PS_FC = [ps[:, fc * 512: fc * 512 + 256] for fc in range(8)]
    PS_G = ps[:, 256:264]
    PT_M = ps[0:1, 768:896]
    PS_ME = ps[:, 1280:1536]
    PS_SUM = ps[0:1, 1792:2048]
    PS_SQ = ps[0:1, 2304:2560]
    PS_R = ps[:, 2816:3072]
    PS_M = ps[:, 3328:3584]

    S = Sched(nc)
    for e_ in range(8):
        for c in range(8):
            S.dma("gpsimd", ew16[:, (e_ * 8 + c) * 1024:(e_ * 8 + c + 1) * 1024],
                  ewT[e_ * D + c * 128: e_ * D + (c + 1) * 128, :], ch="ldg")
    t_ldg = S.dma("gpsimd", id_sb[:, :], ident, ch="ldg")
    for c in range(8):
        S.dma("sync", gw_sb[:, c * 8:(c + 1) * 8],
              gwT[c * 128:(c + 1) * 128, :], ch="lds")
    S.dma("sync", ones1_sb[:, :], ones1, ch="lds")
    t_lds = S.dma("sync", onesc_sb[:, :], onesc, ch="lds")
    t_fb = S.emit("vector", lambda e: e.memset(fbg[:, :], 0.0))
    t_id = S.emit("scalar", lambda e: e.activation(id32[:, :], id_sb[:, :],
                                                   AF.Copy), waits=[t_ldg])
    pre = [t_ldg, t_lds, t_fb, t_id]

    def ln_rows(S, t_stats):
        RW = TTOK
        mu = strow[0:1, 0:RW]; ex2 = strow[0:1, RW:2 * RW]
        var = strow[0:1, 2 * RW:3 * RW]; rstd = strow[0:1, 3 * RW:4 * RW]
        mur = strow[0:1, 4 * RW:5 * RW]
        t1 = S.emit("vector", lambda e: e.tensor_scalar_mul(mu, PS_SUM, 1.0 / D),
                    waits=[t_stats])
        t1x = S.emit("vector", lambda e: e.tensor_scalar_mul(ex2, PS_SQ, 1.0 / D))
        S.emit("vector", lambda e: e.tensor_tensor(var, mu, mu, ALU.mult))
        S.emit("vector", lambda e: e.tensor_tensor(var, ex2, var, ALU.subtract))
        t4 = S.emit("vector", lambda e: e.tensor_scalar_add(var, var, 1e-5))
        t5 = S.emit("scalar", lambda e: e.activation(rstd, var, AF.Sqrt), waits=[t4])
        t6 = S.emit("vector", lambda e: e.reciprocal(rstd, rstd), waits=[t5])
        t7 = S.emit("vector", lambda e: e.tensor_tensor(mur, mu, rstd, ALU.mult))
        t8 = S.emit("scalar", lambda e: e.copy(srow16[0:1, 0:RW], rstd), waits=[t6])
        t9 = S.emit("scalar", lambda e: e.copy(srow16[0:1, RW:2 * RW], mur),
                    waits=[t7])
        return t1x, t8, t9

    prev_cons = []
    g_free = None
    me_free = None
    mt_free = None
    ptm_free = None
    mrow_lastread = None
    mbc_lastread = None
    xe_lastread = None
    fc_free = []
    st_tok = None
    prev_R = None
    for tt in range(NT):
        t0 = tt * TTOK
        ldw = (pre + prev_cons) if tt == 0 else prev_cons
        t_z = None
        for c in range(8):
            t_z = S.dma("sync", zsb[:, c * TTOK:(c + 1) * TTOK],
                        z8i[c * 128:(c + 1) * 128, t0:t0 + TTOK],
                        waits=ldw if c == 0 else [], ch="ldz")
        t_x = None
        for c in range(8):
            t_x = S.dma("sync", xsb[:, c * TTOK:(c + 1) * TTOK],
                        x3i[c * 128:(c + 1) * 128, t0:t0 + TTOK], ch="ldz")
        zc_toks = []
        for c in range(8):
            tz16 = S.emit("scalar", lambda e, c=c: e.activation(
                z16[:, c * TTOK:(c + 1) * TTOK], zsb[:, c * TTOK:(c + 1) * TTOK],
                AF.Copy), waits=[t_z, xe_lastread])
            zc_toks.append(tz16)
        mrow_toks = []
        for st4 in range(TTOK // 128):
            mmg = None
            for kc in range(8):
                w = ([t_z, g_free]) if kc == 0 else []
                mmg = S.emit("tensor", lambda e, st4=st4, kc=kc: e.matmul(
                    PS_G, zsb[:, kc * TTOK + st4 * 128: kc * TTOK + st4 * 128 + 128],
                    gw_sb[:, kc * 8:(kc + 1) * 8],
                    start=(kc == 0), stop=(kc == 7)), waits=w)
            tmx = S.emit("vector", lambda e: e.reduce_max(
                mxc[:, 0:1], PS_G, axis=mybir.AxisListType.X), waits=[mmg, mt_free])
            S.emit("vector", lambda e: e.tensor_copy(fbg[:, 64:128],
                                                     fbg[:, 0:64]))
            tge = S.emit("vector", lambda e: e.tensor_scalar(
                mt[:, :], PS_G, mxc[:, 0:1], None, ALU.is_ge))
            g_free = tge
            for e_ in range(8):
                ttr = S.emit("tensor", lambda e, e_=e_: e.transpose(
                    PT_M, mt[:, e_:e_ + 1], id32[:, :]),
                    waits=[tge, ptm_free])
                mt_free = ttr
                tme = S.emit("scalar", lambda e, e_=e_, st4=st4: e.activation(
                    mrow[0:1, e_ * TTOK + st4 * 128: e_ * TTOK + (st4 + 1) * 128],
                    PT_M, AF.Copy), waits=[ttr, mrow_lastread])
                ptm_free = tme
                mrow_toks.append(tme)
        mbc_toks = []
        tmb_prev = None
        for e_ in range(8):
            tmb = S.emit("tensor", lambda e, e_=e_: e.matmul(
                PS_ME, ones1_sb[:, :], mrow[0:1, e_ * TTOK:(e_ + 1) * TTOK],
                start=True, stop=True),
                waits=(mrow_toks + [me_free]) if e_ == 0 else [me_free])
            tmc = S.emit("vector", lambda e, e_=e_: e.tensor_copy(
                mbc16[:, e_ * TTOK:(e_ + 1) * TTOK], PS_ME),
                waits=[tmb, mbc_lastread])
            me_free = tmc
            mbc_toks.append(tmc)
        mrow_lastread = tmb
        u_toks = []
        mm = None
        for e_ in range(8):
            xet = []
            for kc in range(8):
                tx = S.emit("vector", lambda e, e_=e_, kc=kc: e.tensor_tensor(
                    xe16[:, kc * TTOK:(kc + 1) * TTOK],
                    z16[:, kc * TTOK:(kc + 1) * TTOK],
                    mbc16[:, e_ * TTOK:(e_ + 1) * TTOK], ALU.mult),
                    waits=([mbc_toks[e_], xe_lastread] + zc_toks) if kc == 0
                    else [])
                xet.append(tx)
            for kc in range(8):
                for fc in range(8):
                    w = []
                    if kc == 0 and fc == 0:
                        w = xet + (fc_free if e_ == 0 else [])
                    mm = S.emit("tensor", lambda e, fc=fc, kc=kc, e_=e_: e.matmul(
                        PS_FC[fc],
                        ew16[:, (e_ * 8 + kc) * 1024 + fc * 128:
                             (e_ * 8 + kc) * 1024 + (fc + 1) * 128],
                        xe16[:, kc * TTOK:(kc + 1) * TTOK],
                        start=(e_ == 0 and kc == 0), stop=(e_ == 7 and kc == 7)),
                        waits=w)
            xe_lastread = mm
        fc_free = []
        for fc in range(8):
            tu = S.emit("vector", lambda e, fc=fc: e.tensor_tensor(
                u16[:, fc * TTOK:(fc + 1) * TTOK], PS_FC[fc],
                zsb[:, fc * TTOK:(fc + 1) * TTOK], ALU.add), waits=[mm])
            tc_ = S.emit("scalar", lambda e, fc=fc: e.activation(
                usq[:, fc * TTOK:(fc + 1) * TTOK], u16[:, fc * TTOK:(fc + 1) * TTOK],
                AF.Square), waits=[tu])
            u_toks += [tu, tc_]
            fc_free.append(tu)
        tm = None
        for c in range(8):
            tm = S.emit("tensor", lambda e, c=c: e.matmul(
                PS_SUM, onesc_sb[:, :], u16[:, c * TTOK:(c + 1) * TTOK],
                start=(c == 0), stop=(c == 7)),
                waits=(u_toks + [prev_R]) if c == 0 else [])
        for c in range(8):
            tm = S.emit("tensor", lambda e, c=c: e.matmul(
                PS_SQ, onesc_sb[:, :], usq[:, c * TTOK:(c + 1) * TTOK],
                start=(c == 0), stop=(c == 7)))
        t1x, t8, t9 = ln_rows(S, tm)
        tR = S.emit("tensor", lambda e: e.matmul(PS_R, ones1_sb[:, :],
                                                 srow16[0:1, 0:TTOK],
                                                 start=True, stop=True), waits=[t8])
        tM = S.emit("tensor", lambda e: e.matmul(PS_M, ones1_sb[:, :],
                                                 srow16[0:1, TTOK:2 * TTOK],
                                                 start=True, stop=True),
                    waits=[t9, t1x])
        otoks = []
        for c in range(8):
            S.emit("vector", lambda e, c=c: e.tensor_tensor(
                ob[:, c * TTOK:(c + 1) * TTOK], u16[:, c * TTOK:(c + 1) * TTOK],
                PS_R, ALU.mult), waits=[tR, tM, t_x, st_tok])
            S.emit("vector", lambda e, c=c: e.tensor_tensor(
                ob[:, c * TTOK:(c + 1) * TTOK], ob[:, c * TTOK:(c + 1) * TTOK],
                PS_M, ALU.subtract), waits=[tM])
            ta = S.emit("vector", lambda e, c=c: e.tensor_tensor(
                ob[:, c * TTOK:(c + 1) * TTOK], ob[:, c * TTOK:(c + 1) * TTOK],
                xsb[:, c * TTOK:(c + 1) * TTOK], ALU.add))
            otoks.append(ta)
        for c in range(8):
            st_tok = S.dma("sync", outo[c * 128:(c + 1) * 128, t0:t0 + TTOK],
                           ob[:, c * TTOK:(c + 1) * TTOK],
                           waits=otoks if c == 0 else [], ch="sto")
        prev_R = otoks[-1]
        prev_cons = u_toks + otoks + zc_toks + [t1x]
    S.emit("gpsimd", lambda e: e.engine_nop(), waits=[st_tok], inc=0)
    S.finalize()
    return nc




def _l0f_post_host(inputs, x1_cores, z6_cores, consts):
    in_maps = []
    for cid in range(NCORE):
        in_maps.append({
            "x1i": x1_cores[cid], "z6i": z6_cores[cid],
            "ones1": consts["ones1"], "onesc": consts["onesc"],
        })
    return in_maps


def _l1a_qkv_host(inputs, z7_cores, consts):
    Wq = inputs["L1_attn_Wq"]; Wk = inputs["L1_attn_Wk"]; Wv = inputs["L1_attn_Wv"]
    perm = np.concatenate([np.arange(h * 64, (h + 1) * 64).reshape(32, 2).T.reshape(64)
                           for h in range(16)])
    perm = np.concatenate([np.concatenate([np.arange(h * 64, (h + 1) * 64, 2),
                                           np.arange(h * 64 + 1, (h + 1) * 64, 2)])
                           for h in range(16)])
    Wqp = np.ascontiguousarray(Wq[:, perm])
    Wkp = np.ascontiguousarray(Wk[:, perm])
    m = np.arange(128) % 32
    inv_freq = 1.0 / 10000.0 ** (2 * np.arange(32, dtype=np.float64) / 64)
    sign = np.where((np.arange(128) % 64) < 32, -1.0, 1.0)[:, None]
    tabs = []
    for hh in range(2):
        t = np.arange(hh * TC, (hh + 1) * TC, dtype=np.float64)
        ang = np.outer(inv_freq[m], t)
        cs = np.cos(ang)
        sn = np.sin(ang) * sign
        tabs.append(np.ascontiguousarray(
            np.concatenate([cs, sn], axis=1).astype(np.float32)))
    in_maps = []
    for cid in range(NCORE):
        h = cid % 2
        in_maps.append({
            "z7i": z7_cores[cid], "wq": Wqp, "wk": Wkp,
            "wv": np.ascontiguousarray(Wv),
            "cst": tabs[h], "ident": consts["ident"],
        })
    return in_maps


def _l1a_fin_host(inputs, qp_cores, vkt_cores, x2_cores, consts):
    Wo_ = np.ascontiguousarray(inputs["L1_attn_Wo"])
    in_maps = []
    for cid in range(NCORE):
        b = cid // 2
        vk2 = np.concatenate([vkt_cores[2 * b], vkt_cores[2 * b + 1]], axis=0)
        in_maps.append({
            "qpi": qp_cores[cid], "vk2": np.ascontiguousarray(vk2),
            "x2i": x2_cores[cid], "owT": Wo_,
            "ones1": consts["ones1"], "onesc": consts["onesc"],
            "ident": consts["ident"],
        })
    return in_maps


def _l1f_host(inputs, z8_cores, x3_cores, consts):
    gwT = np.ascontiguousarray(inputs["L1_ffn_gate_w"].T)
    ewT = np.ascontiguousarray(
        inputs["L1_ffn_expert_w"].transpose(0, 2, 1).reshape(8 * D, D))
    in_maps = []
    for cid in range(NCORE):
        in_maps.append({
            "z8i": z8_cores[cid], "x3i": x3_cores[cid],
            "gwT": gwT, "ewT": ewT,
            "ones1": consts["ones1"], "onesc": consts["onesc"],
            "ident": consts["ident"],
        })
    return in_maps


_NC_CACHE = {}


def _get_nc(name):
    if name not in _NC_CACHE:
        _NC_CACHE[name] = {
            "l0a_scan": build_l0a_scan, "l0a_out": build_l0a_out,
            "l0f_layer": build_l0f_layer, "l0f_post": build_l0f_post,
            "l1a_qkv": build_l1a_qkv, "l1a_fin": build_l1a_fin,
            "l1f": build_l1f,
        }[name]()
    return _NC_CACHE[name]


def kernel(**inputs):
    consts = _l0f_host_consts()
    nc = _get_nc("l0a_scan")
    res = run_bass_kernel_spmd(nc, _l0a_scan_host(inputs),
                               core_ids=list(range(NCORE)))
    ycore = [np.asarray(res.results[c]["y16"]) for c in range(NCORE)]
    y_even = [ycore[2 * b] for b in range(4)]
    y_odd = [ycore[2 * b + 1] for b in range(4)]
    nc = _get_nc("l0a_out")
    res = run_bass_kernel_spmd(nc, _l0a_out_host(inputs, y_even, y_odd),
                               core_ids=list(range(NCORE)))
    x1_cores = [np.asarray(res.results[c]["x1o"]) for c in range(NCORE)]
    z_cores = [np.asarray(res.results[c]["z0o"]) for c in range(NCORE)]
    nc = _get_nc("l0f_layer")
    for layer in range(6):
        z_fm = [np.concatenate([z_cores[2 * b], z_cores[2 * b + 1]], axis=1)
                for b in range(4)]
        res = run_bass_kernel_spmd(nc, _l0f_host(inputs, layer, z_fm, consts),
                                   core_ids=list(range(NCORE)))
        z_cores = [np.asarray(res.results[c]["zout"]) for c in range(NCORE)]
    nc = _get_nc("l0f_post")
    res = run_bass_kernel_spmd(nc, _l0f_post_host(inputs, x1_cores, z_cores,
                                                  consts),
                               core_ids=list(range(NCORE)))
    x2_cores = [np.asarray(res.results[c]["x2o"]) for c in range(NCORE)]
    z7_cores = [np.asarray(res.results[c]["z7o"]) for c in range(NCORE)]
    nc = _get_nc("l1a_qkv")
    res = run_bass_kernel_spmd(nc, _l1a_qkv_host(inputs, z7_cores, consts),
                               core_ids=list(range(NCORE)))
    qp_cores = [np.asarray(res.results[c]["qpo"]) for c in range(NCORE)]
    vkt_cores = [np.asarray(res.results[c]["vkto"]) for c in range(NCORE)]
    nc = _get_nc("l1a_fin")
    res = run_bass_kernel_spmd(nc, _l1a_fin_host(inputs, qp_cores, vkt_cores,
                                                 x2_cores, consts),
                               core_ids=list(range(NCORE)))
    x3_cores = [np.asarray(res.results[c]["x3o"]) for c in range(NCORE)]
    z8_cores = [np.asarray(res.results[c]["z8o"]) for c in range(NCORE)]
    nc = _get_nc("l1f")
    res = run_bass_kernel_spmd(nc, _l1f_host(inputs, z8_cores, x3_cores,
                                             consts),
                               core_ids=list(range(NCORE)))
    out = np.zeros((BATCH, SEQ, D), np.float32)
    for cid in range(NCORE):
        b, h = cid // 2, cid % 2
        out[b, h * TC:(h + 1) * TC, :] = np.asarray(
            res.results[cid]["outo"]).T
    return out

